# version 29
# baseline (speedup 1.0000x reference)
import math
import numpy as np

import concourse.bass as bass
import concourse.mybir as mybir
from concourse.bass_utils import run_bass_kernel_spmd

F32 = mybir.dt.float32
BF16 = mybir.dt.bfloat16
OP = mybir.AluOpType
AF = mybir.ActivationFunctionType

D = 1024
S = 2048
NF = 256
F2 = 512
DC = 8
NSUP = 4
TT = 512
NSUB = 4
PI = math.pi
TWO_PI = 2.0 * math.pi
EPS = 1e-5


class Res:
    __slots__ = ("w", "r")

    def __init__(self):
        self.w = None
        self.r = []


class Sched:
    ENGINES = ("T", "V", "S", "G", "Y")
    NSLOT = 8

    def __init__(self):
        self.steps = []
        self.cnt = {e: 0 for e in self.ENGINES}
        self.dma_cnt = {}
        self.dma_idx = {}

    def sem_names(self):
        names = list(self.ENGINES)
        for e in ("Y", "G"):
            names += ["%s%d" % (e, k) for k in range(self.NSLOT)]
        return names

    def step(self, eng, reads, writes, fn, dma=False):
        deps = []
        for r in reads:
            if r.w is not None:
                deps.append(r.w)
        for w in writes:
            if w.w is not None:
                deps.append(w.w)
            deps.extend(w.r)
        if dma:
            idx = self.dma_idx.get(eng, 0)
            self.dma_idx[eng] = idx + 1
            slot = idx % self.NSLOT
            sem_name = "%s%d" % (eng, slot)
            prev = self.dma_cnt.get(sem_name, 0)
            if prev:
                deps.append((sem_name, prev))
            self.dma_cnt[sem_name] = prev + 16
            ev = (sem_name, prev + 16)
            inc = 16
        else:
            self.cnt[eng] += 1
            sem_name = eng
            ev = (eng, self.cnt[eng])
            inc = 1
        deps = sorted(set(deps))
        self.steps.append((eng, deps, fn, sem_name, inc))
        for r in reads:
            r.r.append(ev)
        for w in writes:
            w.w = ev
            w.r = []
        return ev

    def emit(self, block, sems):
        all_names = self.sem_names()

        def make_body(eng_name):
            def body(eng):
                hw = {n: 0 for n in all_names}
                for (e, deps, fn, sem_name, inc) in self.steps:
                    if e != eng_name:
                        continue
                    for (dn, dc) in deps:
                        if dc > hw[dn]:
                            eng.wait_ge(sems[dn], dc)
                            hw[dn] = dc
                    ret = fn(eng)
                    last = ret[-1] if isinstance(ret, (list, tuple)) else ret
                    last.then_inc(sems[sem_name], inc)
            return body
        block.tensor(make_body("T"))
        block.vector(make_body("V"))
        block.scalar(make_body("S"))
        block.gpsimd(make_body("G"))
        block.sync(make_body("Y"))


def build_nc():
    nc = bass.Bass("TRN2", target_bir_lowering=False)

    x_h = nc.dram_tensor("x", [S, D], F32, kind="ExternalInput").ap()
    win_h = nc.dram_tensor("w_in", [F2, D], F32, kind="ExternalInput").ap()
    wout_h = nc.dram_tensor("w_out", [D, F2], F32, kind="ExternalInput").ap()
    ld_h = nc.dram_tensor("ld", [NF], F32, kind="ExternalInput").ap()
    fr_h = nc.dram_tensor("fr", [NF], F32, kind="ExternalInput").ap()
    th_h = nc.dram_tensor("theta", [D, D], F32, kind="ExternalInput").ap()
    ph_h = nc.dram_tensor("phi", [D, D], F32, kind="ExternalInput").ap()
    thn_h = nc.dram_tensor("thn", [D, D], F32, kind="ExternalInput").ap()
    phn_h = nc.dram_tensor("phn", [D, D], F32, kind="ExternalInput").ap()
    c_h = nc.dram_tensor("c", [8, D], F32, kind="ExternalInput").ap()
    wmod_h = nc.dram_tensor("wmod", [D, 768], F32, kind="ExternalInput").ap()
    bsel_h = nc.dram_tensor("bsel", [8], F32, kind="ExternalInput").ap()
    ccin_h = nc.dram_tensor("ccin", [8 * 768], F32).ap()
    ccout_h = nc.dram_tensor("ccout", [8 * 8 * 768], F32).ap()
    out_h = nc.dram_tensor("out", [S, D], F32, kind="ExternalOutput").ap()

    sch = Sched()

    import contextlib
    ctx = contextlib.ExitStack()

    def sb(name, shape, dtype):
        return ctx.enter_context(nc.sbuf_tensor(name, shape, dtype))

    def ps(name, shape, dtype):
        return ctx.enter_context(nc.psum_tensor(name, shape, dtype))

    with ctx:
        xt = sb("xt", [128, 2 * NSUB * D], F32)
        h1tm = sb("h1tm", [128, NSUB * D], BF16)
        hf = sb("hf", [128, DC * TT], BF16)
        fy = sb("fy", [128, 4 * TT], BF16)
        ctbl = sb("ctbl", [128, 1024], BF16)
        stbl = sb("stbl", [128, 1024], BF16)

        poolB = sb("poolB", [128, 2048], F32)
        iotac = poolB[:, 0:512]
        wtmp = poolB[:, 512:1024]
        wtmp2 = poolB[:, 1024:1536]
        kbuf = sb("kbuf", [128, 512], mybir.dt.int32)
        rotc = sb("rotc", [128, 12], F32)
        scrF = sb("scrF", [128, 2048], F32)
        zbuf = sb("zbuf", [128, 1024], BF16)
        vbuf = sb("vbuf", [128, 1024], BF16)
        carry = sb("carry", [128, 4], F32)
        ctbl2 = sb("ctbl2", [128, 1024], BF16)
        stbl2 = sb("stbl2", [128, 1024], BF16)
        pbuf = sb("pbuf", [128, 1024], BF16)
        vb16 = sb("vb16", [128, 1024], BF16)
        scr2 = sb("scr2", [128, 2048], F32)
        fy2 = sb("fy2", [128, 4 * TT], BF16)
        carry2 = sb("carry2", [128, 4], F32)
        winT = sb("winT", [128, DC * F2], BF16)
        woT = sb("woT", [128, 4 * D], BF16)
        cwt = sb("cwt", [128, DC * D], BF16)
        swt = sb("swt", [128, DC * D], BF16)
        arenaF = sb("arenaF", [128, 2048], F32)
        arenaB = sb("arenaB", [128, 6144], BF16)
        grow = sb("grow", [1, 1024], F32)
        sgnbc = sb("sgnbc", [128, 1024], BF16)
        gbbc = sb("gbbc", [128, 1024], F32)
        hf2 = sb("hf2", [128, DC * TT], BF16)
        h1tm2 = sb("h1tm2", [128, NSUB * D], BF16)
        wstageB = h1tm2
        s1crowb = sb("s1crowb", [1, 2048], BF16)
        modc = sb("modc", [128, 48], F32)
        dcol = sb("dcol", [128, 48], F32)
        sab = sb("sab", [128, 8], BF16)
        sbb = sb("sbb", [128, 8], BF16)
        s1col = sb("s1col", [128, 4], F32)
        c_col = sb("c_col", [128, 64], F32)
        silub = sb("silub", [128, 64], BF16)
        rows8 = sb("rows8", [8, 768], F32)
        selbuf = sb("selbuf", [8, 2 * 768], BF16)
        bselc = sb("bselc", [8, 1], BF16)
        ldfr = sb("ldfr", [128, 4], F32)
        rho_col = sb("rho_col", [128, 2], F32)
        omg_col = sb("omg_col", [128, 2], F32)
        sums = sb("sums", [128, 16], F32)
        sumsq = sb("sumsq", [128, 16], F32)
        rsig = sb("rsig", [128, 16], F32)
        nmr = sb("nmr", [128, 16], F32)
        rstat = sb("rstat", [128, 48], F32)
        idf = sb("idf", [128, 128], F32)
        idb = sb("idb", [128, 128], BF16)
        onesf = sb("onesf", [1, 128], F32)
        onesb = sb("onesb", [1, 128], BF16)
        one1 = sb("one1", [1, 1], F32)
        pcol = sb("pcol", [128, 1], F32)
        stage8 = sb("stage8", [68, 128], F32)

        PS = ps("PS", [128, 2048], F32)
        PO = ps("PO", [128, 1024], F32)
        PT = ps("PT", [128, 1024], BF16)
        PF = ps("PF", [128, 512], F32)

        R = {name: Res() for name in [
            "xt0", "xt1", "xt2", "h1tm", "hf", "fy", "ctbl", "stbl", "wtmp",
            "wtmp2", "kbuf", "rotc", "gbbc", "hf2", "h1tm2", "pf", "pt2",
            "arenaF2",
            "iotac",
            "scrF", "zbuf", "vbuf", "carry", "winT", "woT", "poolB",
            "ctbl2", "stbl2", "fy2", "carry2", "ccb", "pbuf", "vb16", "scr2",
            "scr2a", "scr2b",
            "cwt", "swt", "arenaF", "arenaB", "grow", "sgnbc",
            "s1crowb", "modc", "dcol", "sab", "sbb", "s1col", "c_col", "silub",
            "ldfr", "rho_col", "omg_col", "sums", "sumsq", "rsig", "nmr",
            "rstat", "idf", "idb", "onesf", "onesb", "one1", "pcol", "rows8",
            "selbuf", "selbuf2", "bselc",
            "stage8",
            "ps0", "ps1", "ps2", "ps3", "po0", "po1", "pt",
        ]}

        def r(*names):
            return [R[n] for n in names]

        st = sch.step

        st("Y", [], r("stage8"), lambda e: e.dma_start(
            stage8[0:64, :], c_h.rearrange("b (q p) -> (b q) p", p=128)),
           dma=True)
        st("Y", [], r("stage8"), lambda e: e.dma_start(
            stage8[64:66, :], ld_h.rearrange("(q p) -> q p", p=128)), dma=True)
        st("Y", [], r("stage8"), lambda e: e.dma_start(
            stage8[66:68, :], fr_h.rearrange("(q p) -> q p", p=128)), dma=True)

        def _consts(e):
            a = e.memset(onesf.ap(), 1.0)
            b = e.memset(onesb.ap(), 1.0)
            c = e.memset(one1.ap(), 1.0)
            d = e.memset(carry.ap(), 0.0)
            return d
        st("V", [], r("onesf", "onesb", "one1", "carry"), _consts)

        st("G", [], r("pcol"), lambda e: e.iota(
            pcol.ap(), [[1, 1]], channel_multiplier=1, allow_small_or_imprecise_dtypes=True))
        st("G", [], r("iotac"), lambda e: e.iota(
            iotac, [[1, 512]], channel_multiplier=0, allow_small_or_imprecise_dtypes=True))
        st("G", [], r("idf"), lambda e: e.iota(
            idf.ap(), [[1, 128]], channel_multiplier=0, allow_small_or_imprecise_dtypes=True))
        st("V", r("idf", "pcol"), r("idf"), lambda e: e.tensor_scalar(
            idf.ap(), idf.ap(), pcol[:, 0:1], None, OP.is_equal))
        st("V", r("idf"), r("idb"), lambda e: e.tensor_copy(idb.ap(), idf.ap()))

        st("T", r("stage8", "idf"), r("po0"), lambda e: e.transpose(
            PO[:, 0:68], stage8[0:68, :], idf[0:68, 0:68]))
        st("V", r("po0"), r("c_col"), lambda e: e.tensor_copy(
            c_col.ap(), PO[:, 0:64]))
        st("V", r("po0"), r("ldfr"), lambda e: e.tensor_copy(
            ldfr.ap(), PO[:, 64:68]))

        st("S", r("ldfr"), r("rho_col"), lambda e: e.activation(
            rho_col.ap(), ldfr[:, 0:2], AF.Sigmoid))
        st("V", r("ldfr"), r("omg_col"), lambda e: e.tensor_scalar(
            omg_col.ap(), ldfr[:, 2:4], 0.1, None, OP.mult))
        st("V", r("omg_col"), r("rstat"), lambda e: e.tensor_scalar(
            rstat[:, 0:2], omg_col.ap(), 1.0 / TWO_PI, None, OP.mult))
        st("V", r("rstat"), r("kbuf"), lambda e: e.tensor_copy(
            kbuf[:, 0:2], rstat[:, 0:2]))
        st("V", r("kbuf", "omg_col"), r("omg_col"), lambda e:
           e.scalar_tensor_tensor(omg_col.ap(), kbuf[:, 0:2], -TWO_PI,
                                  omg_col.ap(), OP.mult, OP.add))

        st("V", r("omg_col"), r("rstat"), lambda e: e.tensor_scalar(
            rstat[:, 0:2], omg_col.ap(), -PI, PI, OP.max, OP.min))
        st("S", r("rstat"), r("rotc"), lambda e: e.activation(
            rotc[:, 2:4], rstat[:, 0:2], AF.Sin))
        st("V", r("omg_col"), r("rstat"), lambda e: e.tensor_scalar(
            rstat[:, 0:2], omg_col.ap(), PI / 2.0, None, OP.add))
        st("V", r("rstat"), r("rstat"), lambda e: e.tensor_scalar(
            rstat[:, 2:4], rstat[:, 0:2], PI, -TWO_PI, OP.is_gt, OP.mult))
        st("V", r("rstat"), r("rstat"), lambda e: e.tensor_tensor(
            rstat[:, 0:2], rstat[:, 0:2], rstat[:, 2:4], OP.add))
        st("V", r("rstat"), r("rstat"), lambda e: e.tensor_scalar(
            rstat[:, 0:2], rstat[:, 0:2], -PI, PI, OP.max, OP.min))
        st("S", r("rstat"), r("rotc"), lambda e: e.activation(
            rotc[:, 0:2], rstat[:, 0:2], AF.Sin))
        st("V", r("omg_col"), r("rstat"), lambda e: e.tensor_scalar(
            rstat[:, 0:2], omg_col.ap(), 512.0, None, OP.mult))
        st("V", r("rstat"), r("rstat"), lambda e: e.tensor_scalar(
            rstat[:, 2:4], rstat[:, 0:2], 1.0 / TWO_PI, None, OP.mult))
        st("V", r("rstat"), r("kbuf"), lambda e: e.tensor_copy(
            kbuf[:, 0:2], rstat[:, 2:4]))
        st("V", r("kbuf", "rstat"), r("rstat"), lambda e:
           e.scalar_tensor_tensor(rstat[:, 0:2], kbuf[:, 0:2], -TWO_PI,
                                  rstat[:, 0:2], OP.mult, OP.add))
        st("V", r("rstat"), r("rstat"), lambda e: e.tensor_scalar(
            rstat[:, 2:4], rstat[:, 0:2], PI, -TWO_PI, OP.is_gt, OP.mult))
        st("V", r("rstat"), r("rstat"), lambda e: e.tensor_tensor(
            rstat[:, 0:2], rstat[:, 0:2], rstat[:, 2:4], OP.add))
        st("V", r("rstat"), r("rstat"), lambda e: e.tensor_scalar(
            rstat[:, 2:4], rstat[:, 0:2], -PI, TWO_PI, OP.is_lt, OP.mult))
        st("V", r("rstat"), r("rstat"), lambda e: e.tensor_tensor(
            rstat[:, 0:2], rstat[:, 0:2], rstat[:, 2:4], OP.add))
        st("V", r("rstat"), r("rstat"), lambda e: e.tensor_scalar(
            rstat[:, 0:2], rstat[:, 0:2], -PI, PI, OP.max, OP.min))
        st("S", r("rstat"), r("rotc"), lambda e: e.activation(
            rotc[:, 8:10], rstat[:, 0:2], AF.Sin))
        st("V", r("rstat"), r("rstat"), lambda e: e.tensor_scalar(
            rstat[:, 4:6], rstat[:, 0:2], PI / 2.0, None, OP.add))
        st("V", r("rstat"), r("rstat"), lambda e: e.tensor_scalar(
            rstat[:, 2:4], rstat[:, 4:6], PI, -TWO_PI, OP.is_gt, OP.mult))
        st("V", r("rstat"), r("rstat"), lambda e: e.tensor_tensor(
            rstat[:, 4:6], rstat[:, 4:6], rstat[:, 2:4], OP.add))
        st("V", r("rstat"), r("rstat"), lambda e: e.tensor_scalar(
            rstat[:, 4:6], rstat[:, 4:6], -PI, PI, OP.max, OP.min))
        st("S", r("rstat"), r("rotc"), lambda e: e.activation(
            rotc[:, 6:8], rstat[:, 4:6], AF.Sin))
        st("V", r("rotc"), r("rotc"), lambda e: e.tensor_scalar(
            rotc[:, 4:6], rotc[:, 2:4], -1.0, None, OP.mult))
        st("V", r("rotc"), r("rotc"), lambda e: e.tensor_scalar(
            rotc[:, 10:12], rotc[:, 8:10], -1.0, None, OP.mult))


        st("S", r("c_col"), r("silub"), lambda e: e.activation(
            silub.ap(), c_col.ap(), AF.Silu))

        p3_tail = []
        p3_emitters = []

        def emit_modhead():
            st("G", [], r("arenaB"), lambda e: e.dma_start(
                arenaB[:, 0:6144].rearrange("p (c q) -> p c q", c=8),
                wmod_h.rearrange("(c p) q -> c p q", p=128).transpose([1, 0, 2])),
               dma=True)
            st("G", [], r("h1tm2"), lambda e: e.dma_start(
                wstageB.ap().rearrange("p (c d) -> p c d", c=4),
                win_h.rearrange("(c p) d -> c p d", p=128).transpose([1, 0, 2])),
               dma=True)

            def mm_matvec(e):
                ins = []
                silv = silub.ap().rearrange("p (b k) -> p k b", k=8)
                for k in range(8):
                    ins.append(e.matmul(
                        PS[0:8, 0:512], silv[:, k, :],
                        arenaB[:, 768 * k:768 * k + 512],
                        start=(k == 0), stop=(k == 7)))
                for k in range(8):
                    ins.append(e.matmul(
                        PS[0:8, 512:768], silv[:, k, :],
                        arenaB[:, 768 * k + 512:768 * (k + 1)],
                        start=(k == 0), stop=(k == 7)))
                return ins
            st("T", r("silub", "arenaB"), r("ps0", "ps1"), mm_matvec)
            st("S", r("ps0", "ps1"), r("rows8"), lambda e: e.copy(
                rows8.ap(), PS[0:8, 0:768]))
            st("G", r("rows8"), r("ccb"), lambda e: e.dma_start(
                ccin_h.rearrange("(b q) -> b q", q=768), rows8.ap()), dma=True)
            st("G", [], r("bselc"), lambda e: e.dma_start(
                bselc.ap(), bsel_h.rearrange("(b q) -> b q", q=1)),
               dma=True)
            for c4 in range(4):
                def mm_t(e, c4=c4):
                    ins = []
                    for u in range(8):
                        ins.append(e.matmul(
                            PT[:, 128 * u:128 * (u + 1)],
                            wstageB[:, 1024 * c4 + 128 * u:
                                    1024 * c4 + 128 * (u + 1)],
                            idb.ap(), start=(u == 0), stop=(u == 7),
                            is_transpose=True))
                    return ins
                st("T", r("h1tm2", "idb"), r("pt"), mm_t)
                st("V", r("pt"), r("winT"), lambda e, c4=c4: e.tensor_copy(
                    winT.ap().rearrange("p (u f) -> p u f", u=8)
                    [:, :, 128 * c4:128 * (c4 + 1)],
                    PT.ap().rearrange("p (u f) -> p u f", u=8)))

            st("G", [], r("arenaB"), lambda e: e.dma_start(
                arenaB[:, 0:4096].rearrange("p (c f) -> p c f", c=8),
                wout_h.rearrange("(c p) f -> c p f", p=128).transpose([1, 0, 2])),
               dma=True)
            for g in range(4):
                def mm_t(e, g=g):
                    ins = []
                    for c8 in range(8):
                        ins.append(e.matmul(
                            PT[:, 128 * c8:128 * (c8 + 1)],
                            arenaB[:, 512 * c8 + 128 * g:512 * c8 + 128 * (g + 1)],
                            idb.ap(), start=(c8 == 0), stop=(c8 == 7),
                            is_transpose=True))
                    return ins
                st("T", r("arenaB", "idb"), r("pt"), mm_t)
                st("V", r("pt"), r("woT"), lambda e, g=g: e.tensor_copy(
                    woT[:, 1024 * g:1024 * (g + 1)], PT.ap()))
            st("G", r("ccb"), r("ccb"), lambda e: e.collective_compute(
                "AllGather", OP.bypass, replica_groups=[list(range(8))],
                ins=[ccin_h.opt()], outs=[ccout_h.opt()]))

        def emit_p123():
            for k_ in range(8):
                sbase = 768 * (k_ % 2)
                sR = "selbuf" if k_ % 2 == 0 else "selbuf2"
                st("G", r("ccb"), r(sR), lambda e, k_=k_, sbase=sbase:
                   e.dma_start(
                       selbuf[:, sbase:sbase + 768],
                       ccout_h[6144 * k_:6144 * (k_ + 1)]
                       .rearrange("(b q) -> b q", b=8)), dma=True)

                def mm_sel(e, k_=k_, sbase=sbase):
                    ins = []
                    for m in range(6):
                        q = 6 * k_ + m
                        ins.append(e.matmul(
                            PO[:, q:q + 1],
                            selbuf[:, sbase + 128 * m:sbase + 128 * (m + 1)],
                            bselc.ap(),
                            start=(q == 0), stop=(q == 47)))
                    return ins
                st("T", r(sR, "bselc"), r("po0"), mm_sel)
            st("V", r("po0"), r("modc"), lambda e: e.tensor_copy(
                modc.ap(), PO[:, 0:48]))
            st("V", r("modc"), r("dcol"), lambda e: e.tensor_scalar(
                dcol[:, 0:8], modc[:, 8:16], 1.0, None, OP.add))
            st("V", r("modc"), r("dcol"), lambda e: e.tensor_scalar(
                dcol[:, 8:16], modc[:, 32:40], 1.0, None, OP.add))
            st("V", r("modc"), r("sab"), lambda e: e.tensor_copy(
                sab.ap(), modc[:, 0:8]))
            st("V", r("modc"), r("sbb"), lambda e: e.tensor_copy(
                sbb.ap(), modc[:, 24:32]))

            def clamp_cols(src0, dst0, sgn_dst):
                st("V", r("modc"), r("rstat"), lambda e: e.tensor_scalar(
                    rstat[:, 0:8], modc[:, src0:src0 + 8], 0.0, 2.0,
                    OP.is_ge, OP.mult))
                st("V", r("rstat"), r("rstat"), lambda e: e.tensor_scalar(
                    rstat[:, 0:8], rstat[:, 0:8], -1.0, None, OP.add))
                st("S", r("modc"), r("rstat"), lambda e: e.activation(
                    rstat[:, 8:16], modc[:, src0:src0 + 8], AF.Abs))
                st("V", r("rstat"), r("rstat"), lambda e: e.tensor_scalar(
                    rstat[:, 8:16], rstat[:, 8:16], 0.25, None, OP.max))
                st("V", r("rstat"), r("dcol"), lambda e: e.tensor_tensor(
                    dcol[:, dst0:dst0 + 8], rstat[:, 0:8], rstat[:, 8:16],
                    OP.mult))
                if sgn_dst is not None:
                    st("V", r("rstat"), r("dcol"), lambda e: e.tensor_copy(
                        dcol[:, sgn_dst:sgn_dst + 8], rstat[:, 0:8]))
            clamp_cols(16, 16, None)
            clamp_cols(40, 24, 32)

            def bcast(cols_off, dst_name, dst_ap_fn):
                def mm_t(e):
                    ins = []
                    for cch in range(8):
                        ins.append(e.transpose(
                            PO[0:1, 128 * cch:128 * (cch + 1)],
                            dcol[:, cols_off + cch:cols_off + cch + 1],
                            idf.ap()))
                    return ins
                st("T", r("dcol", "idf"), r("po0", "po1"), mm_t)
                st("V", r("po0", "po1"), r("grow"), lambda e: e.tensor_copy(
                    grow.ap(), PO[0:1, 0:1024]))

                def mm_b(e):
                    ins = []
                    for hh in range(2):
                        ins.append(e.matmul(
                            PS[:, 512 * hh:512 * (hh + 1)], onesf.ap(),
                            grow[0:1, 512 * hh:512 * (hh + 1)],
                            start=True, stop=True))
                    return ins
                st("T", r("grow", "onesf"), r("ps0", "ps1"), mm_b)
                st("V", r("ps0", "ps1"), r(dst_name), lambda e: e.tensor_copy(
                    dst_ap_fn(), PS[:, 0:1024]))
            bcast(16, "scrF", lambda: scrF[:, 0:1024])
            bcast(24, "gbbc", lambda: gbbc.ap())
            bcast(32, "sgnbc", lambda: sgnbc.ap())
            def mm_s1(e):
                ins = []
                for u in range(8):
                    for g in range(4):
                        ins.append(e.matmul(
                            s1psum[:, g:g + 1],
                            winT[:, 512 * u + 128 * g:512 * u + 128 * (g + 1)],
                            sab[:, u:u + 1],
                            start=(u == 0 and g == 0), stop=(u == 7 and g == 3)))
                return ins
            s1psum = PO
            st("T", r("winT", "sab"), r("po0"), mm_s1)
            st("V", r("po0"), r("s1col"), lambda e: e.tensor_copy(
                s1col.ap(), PO[:, 0:4]))

            def sc_win(e):
                ins = []
                for u in range(8):
                    ins.append(e.tensor_scalar(
                        winT[:, 512 * u:512 * (u + 1)],
                        winT[:, 512 * u:512 * (u + 1)],
                        dcol[:, u:u + 1], None, OP.mult))
                return ins
            st("V", r("dcol"), r("winT"), sc_win)

            def sc_wo(e):
                ins = []
                for g in range(4):
                    ins.append(e.tensor_tensor(
                        woT[:, 1024 * g:1024 * (g + 1)],
                        woT[:, 1024 * g:1024 * (g + 1)],
                        scrF[:, 0:1024], OP.mult))
                return ins
            st("V", r("scrF"), r("woT"), sc_wo)


        def emit_p3head(lo=0, hi=16):
            pairA = (arenaF[:, 0:1024], ("arenaF",),
                     arenaF[:, 1024:2048], ("arenaF2",))
            pairC = (scr2[:, 0:1024], ("scr2a",),
                     scr2[:, 1024:2048], ("scr2b",))
            for (mi, (mat_h, noise_h, dst, dstR, shift)) in enumerate((
                    (th_h, thn_h, cwt, "cwt", PI / 2.0),
                    (ph_h, phn_h, swt, "swt", 0.0))):
                for q in range(8):
                    qq = 8 * mi + q
                    if not (lo <= qq < hi):
                        continue
                    slot, aR, nslot, nR = pairA if qq < 6 and qq % 2 == 0 \
                        else pairC
                    st("Y", [], r(*aR), lambda e, mat_h=mat_h, q=q, slot=slot:
                       e.dma_start(slot, mat_h[128 * q:128 * (q + 1), :]),
                       dma=True)
                    st("Y", [], r(*nR), lambda e, noise_h=noise_h, q=q,
                       nslot=nslot:
                       e.dma_start(nslot, noise_h[128 * q:128 * (q + 1), :]),
                       dma=True)
                    st("V", r(*aR, *nR), r(*aR),
                       lambda e, shift=shift, slot=slot, nslot=nslot:
                       e.scalar_tensor_tensor(slot, nslot, shift, slot,
                                              OP.add, OP.add))
                    st("V", r(*aR), r(*aR), lambda e, slot=slot:
                       e.tensor_scalar(slot, slot, -PI, PI,
                                       OP.max, OP.min))
                    st("S", r(*aR), r(dstR), lambda e, dst=dst, q=q, slot=slot:
                       e.activation(dst[:, 1024 * q:1024 * (q + 1)],
                                    slot, AF.Sin))
                def sc_gb(e, dst=dst):
                    ins = []
                    for u in range(8):
                        ins.append(e.tensor_tensor(
                            dst[:, 1024 * u:1024 * (u + 1)],
                            dst[:, 1024 * u:1024 * (u + 1)],
                            gbbc.ap(), OP.mult))
                    return ins

                im = 0 if dst is cwt else 1

                def p3_tail_fn(dst=dst, dstR=dstR, im=im, sc_gb=sc_gb):
                    st("V", r("gbbc"), r(dstR), sc_gb)

                    def mm_s1c(e, dst=dst):
                        ins = []
                        for u in range(8):
                            for hh in range(2):
                                ins.append(e.matmul(
                                    PO[0:1, 512 * hh:512 * (hh + 1)],
                                    sbb[:, u:u + 1],
                                    dst[:, 1024 * u + 512 * hh:
                                        1024 * u + 512 * (hh + 1)],
                                    start=(u == 0), stop=(u == 7)))
                        return ins
                    st("T", r(dstR, "sbb"), r("po0", "po1"), mm_s1c)
                    st("V", r("po0", "po1"), r("s1crowb"),
                       lambda e, im=im: e.tensor_copy(
                           s1crowb[0:1, 1024 * im:1024 * (im + 1)],
                           PO[0:1, 0:1024]))

                    def sc_cb(e, dst=dst):
                        ins = []
                        for u in range(8):
                            ins.append(e.tensor_scalar(
                                dst[:, 1024 * u:1024 * (u + 1)],
                                dst[:, 1024 * u:1024 * (u + 1)],
                                dcol[:, 8 + u:9 + u], None, OP.mult))
                        return ins
                    st("V", r("dcol"), r(dstR), sc_cb)
                if hi == 16:
                    p3_tail.append(p3_tail_fn)

        xslot = ["xt0", "xt1", "xt2"]
        XPOOL = ["arenaF", "arenaF2", "poolB", "iotac", "wtmp", "wtmp2"]

        def xap(i, lo, hi):
            sl = i % 3
            if sl < 2:
                base = 4096 * sl
                return xt[:, base + lo:base + hi]
            if hi <= 2048:
                return arenaF[:, lo:hi]
            assert lo >= 2048
            return poolB[:, lo - 2048:hi - 2048]

        def xres(i):
            sl = i % 3
            if sl < 2:
                return [R[xslot[sl]]]
            return [R["xt2"]] + [R[n] for n in XPOOL]

        def hfof(i):
            return (hf, "hf") if i % 2 == 0 else (hf2, "hf2")

        def h1of(i):
            return (h1tm, "h1tm") if i % 2 == 0 else (h1tm2, "h1tm2")

        def ptof(k):
            return (PT, "pt")

        ptctr = [0]

        def stage_load(i):
            sl = i % 3
            if sl < 2:
                st("Y", [], xres(i), lambda e, i=i, sl=sl: e.dma_start(
                    xt[:, 4096 * sl:4096 * sl + 4096]
                    .rearrange("p (s d) -> p s d", s=4),
                    x_h[512 * i:512 * (i + 1), :]
                    .rearrange("(s p) d -> s p d", p=128).transpose([1, 0, 2])),
                   dma=True)
            else:
                st("Y", [], xres(i), lambda e, i=i: e.dma_start(
                    arenaF.ap().rearrange("p (s d) -> p s d", s=2),
                    x_h[512 * i:512 * i + 256, :]
                    .rearrange("(s p) d -> s p d", p=128).transpose([1, 0, 2])),
                   dma=True)
                st("Y", [], xres(i), lambda e, i=i: e.dma_start(
                    poolB.ap().rearrange("p (s d) -> p s d", s=2),
                    x_h[512 * i + 256:512 * (i + 1), :]
                    .rearrange("(s p) d -> s p d", p=128).transpose([1, 0, 2])),
                   dma=True)

        def stage_ln(i, which):
            htm, htmR = h1of(i)
            hfb, hfR = hfof(i)
            c0 = 4 * i

            def stats_g(e, i=i, c0=c0, htm=htm):
                return [e.tensor_scalar(
                    htm[:, 1024 * s_:1024 * (s_ + 1)],
                    xap(i, 1024 * s_, 1024 * (s_ + 1)),
                    1.0, None, OP.mult, OP.add,
                    accum_out=sums[:, c0 + s_:c0 + s_ + 1])
                    for s_ in range(NSUB)]
            st("V", xres(i), r("sums", htmR), stats_g)

            def stats_s(e, i=i, c0=c0):
                return [e.activation(
                    arenaB[:, 1024 * s_:1024 * (s_ + 1)],
                    xap(i, 1024 * s_, 1024 * (s_ + 1)),
                    AF.Square, accum_out=sumsq[:, c0 + s_:c0 + s_ + 1])
                    for s_ in range(NSUB)]
            st("S", xres(i), r("sumsq", "arenaB"), stats_s)

            c1 = c0 + 4
            st("V", r("sumsq"), r("rstat"), lambda e, c0=c0, c1=c1:
               e.tensor_scalar(rstat[:, c0:c1], sumsq[:, c0:c1],
                               1.0 / D, EPS, OP.mult, OP.add))
            st("V", r("sums"), r("nmr"), lambda e, c0=c0, c1=c1:
               e.tensor_scalar(nmr[:, c0:c1], sums[:, c0:c1],
                               -1.0 / D, None, OP.mult))
            st("V", r("nmr"), r("rstat"), lambda e, c0=c0, c1=c1:
               e.tensor_tensor(rstat[:, 16 + c0:16 + c1], nmr[:, c0:c1],
                               nmr[:, c0:c1], OP.mult))
            st("V", r("rstat"), r("rstat"), lambda e, c0=c0, c1=c1:
               e.tensor_tensor(rstat[:, c0:c1], rstat[:, c0:c1],
                               rstat[:, 16 + c0:16 + c1], OP.subtract))
            st("S", r("rstat"), r("rstat"), lambda e, c0=c0: e.activation(
                rstat[:, 32 + c0:32 + c0 + 4], rstat[:, c0:c0 + 4], AF.Sqrt))
            st("V", r("rstat"), r("rsig"), lambda e, c0=c0: e.reciprocal(
                rsig[:, c0:c0 + 4], rstat[:, 32 + c0:32 + c0 + 4]))
            st("V", r("rsig", "nmr"), r("nmr"), lambda e, c0=c0:
               e.tensor_tensor(nmr[:, c0:c0 + 4], nmr[:, c0:c0 + 4],
                               rsig[:, c0:c0 + 4], OP.mult))

            def h_apply(e, i=i, c0=c0, htm=htm):
                return [e.activation(
                    htm[:, 1024 * s_:1024 * (s_ + 1)],
                    xap(i, 1024 * s_, 1024 * (s_ + 1)),
                    AF.Identity, bias=nmr[:, c0 + s_:c0 + s_ + 1],
                    scale=rsig[:, c0 + s_:c0 + s_ + 1]) for s_ in range(NSUB)]
            st("S", xres(i) + r("rsig", "nmr"), r(htmR), h_apply)

            for s_ in range(NSUB):
                def mm_ht(e, s_=s_, htm=htm):
                    ins = []
                    for cch in range(8):
                        ins.append(e.matmul(
                            PT[:, 128 * cch:128 * (cch + 1)],
                            htm[:, 1024 * s_ + 128 * cch:
                                1024 * s_ + 128 * (cch + 1)],
                            idb.ap(), start=(cch == 0), stop=(cch == 7),
                            is_transpose=True))
                    return ins
                st("T", r(htmR, "idb"), r("pt"), mm_ht)
                st("S", r("pt"), r(hfR), lambda e, s_=s_, hfb=hfb:
                   e.copy(
                       hfb.ap().rearrange("p (u t) -> p u t", u=8)
                       [:, :, 128 * s_:128 * (s_ + 1)],
                       PT.ap().rearrange("p (u t) -> p u t", u=8)))

        def stage_spectral(i, g):
            hfb, hfR = hfof(i)

            def mm_sp(e, g=g, hfb=hfb):
                ins = []
                for (bk, fch) in ((2 * g, g), (2 * g + 1, 2 + g)):
                    for u in range(8):
                        ins.append(e.matmul(
                            PS[:, 512 * bk:512 * (bk + 1)],
                            winT[:, 512 * u + 128 * fch:
                                 512 * u + 128 * (fch + 1)],
                            hfb[:, 512 * u:512 * (u + 1)],
                            start=(u == 0), stop=(u == 7)))
                return ins
            st("T", r("winT", hfR), r("ps%d" % (2 * g), "ps%d" % (2 * g + 1)),
               mm_sp)

        def rot_to_tbl2(g):
            gb_ = 512 * g
            st("V", r("ctbl", "rotc"), r("ctbl2"), lambda e:
               e.tensor_scalar(ctbl2[:, gb_:gb_ + 512], ctbl[:, gb_:gb_ + 512],
                               rotc[:, 0 + g:1 + g], None, OP.mult))
            st("V", r("stbl", "rotc", "ctbl2"), r("ctbl2"),
               lambda e: e.scalar_tensor_tensor(
                   ctbl2[:, gb_:gb_ + 512], stbl[:, gb_:gb_ + 512],
                   rotc[:, 4 + g:5 + g], ctbl2[:, gb_:gb_ + 512],
                   OP.mult, OP.add))
            st("V", r("ctbl", "rotc"), r("stbl2"), lambda e:
               e.tensor_scalar(stbl2[:, gb_:gb_ + 512], ctbl[:, gb_:gb_ + 512],
                               rotc[:, 2 + g:3 + g], None, OP.mult))
            st("V", r("stbl", "rotc", "stbl2"), r("stbl2"),
               lambda e: e.scalar_tensor_tensor(
                   stbl2[:, gb_:gb_ + 512], stbl[:, gb_:gb_ + 512],
                   rotc[:, 0 + g:1 + g], stbl2[:, gb_:gb_ + 512],
                   OP.mult, OP.add))

        def rot_carry(g):
            st("V", r("carry", "rotc"), r("carry2"), lambda e, g=g:
               e.tensor_scalar(carry2[:, g:g + 1], carry[:, g:g + 1],
                               rotc[:, 6 + g:7 + g], None, OP.mult))
            st("V", r("carry", "rotc"), r("carry2"), lambda e, g=g:
               e.tensor_scalar(carry2[:, 2 + g:3 + g], carry[:, g:g + 1],
                               rotc[:, 8 + g:9 + g], None, OP.mult))
            st("V", r("carry", "rotc", "carry2"), r("carry"), lambda e, g=g:
               e.scalar_tensor_tensor(
                   carry[:, g:g + 1], carry[:, 2 + g:3 + g],
                   rotc[:, 10 + g:11 + g], carry2[:, g:g + 1],
                   OP.mult, OP.add))
            st("V", r("carry", "rotc", "carry2"), r("carry"), lambda e, g=g:
               e.scalar_tensor_tensor(
                   carry[:, 2 + g:3 + g], carry[:, 2 + g:3 + g],
                   rotc[:, 6 + g:7 + g], carry2[:, 2 + g:3 + g],
                   OP.mult, OP.add))

        def stage_phase(i, g):
            gb_ = 512 * g
            pr, pi_ = "ps%d" % (2 * g), "ps%d" % (2 * g + 1)
            PRa = PS[:, 512 * 2 * g:512 * (2 * g + 1)]
            PIa = PS[:, 512 * (2 * g + 1):512 * (2 * g + 2)]

            if i == 0:
                st("V", r("iotac", "omg_col"), r("scrF"),
                   lambda e, g=g, gb_=gb_: e.tensor_scalar(
                       scrF[:, 1536:2048], iotac,
                       omg_col[:, g:g + 1], None, OP.mult))
                st("V", r("scrF"), r("wtmp2"), lambda e, gb_=gb_:
                   e.tensor_scalar(wtmp2, scrF[:, 1536:2048],
                                   1.0 / TWO_PI, None, OP.mult))
                st("V", r("wtmp2"), r("kbuf"), lambda e: e.tensor_copy(
                    kbuf[:, 0:512], wtmp2))
                st("V", r("kbuf", "scrF"), r("wtmp"), lambda e, gb_=gb_:
                   e.scalar_tensor_tensor(
                       wtmp, kbuf[:, 0:512], -TWO_PI,
                       scrF[:, 1536:2048], OP.mult, OP.add))
                st("V", r("wtmp"), r("wtmp2"), lambda e: e.tensor_scalar(
                    wtmp2, wtmp, PI, -TWO_PI, OP.is_gt, OP.mult))
                st("V", r("wtmp", "wtmp2"), r("wtmp"), lambda e:
                   e.tensor_tensor(wtmp, wtmp, wtmp2, OP.add))
                st("V", r("wtmp"), r("wtmp2"), lambda e: e.tensor_scalar(
                    wtmp2, wtmp, -PI, TWO_PI, OP.is_lt, OP.mult))
                st("V", r("wtmp", "wtmp2"), r("scrF"), lambda e, gb_=gb_:
                   e.tensor_tensor(scrF[:, 1536:2048], wtmp,
                                   wtmp2, OP.add))
                st("V", r("scrF"), r("scrF"), lambda e, gb_=gb_:
                   e.tensor_scalar(scrF[:, 1536:2048],
                                   scrF[:, 1536:2048],
                                   -PI, PI, OP.max, OP.min))
                st("S", r("scrF"), r("stbl"), lambda e, gb_=gb_:
                   e.activation(stbl[:, gb_:gb_ + 512],
                                scrF[:, 1536:2048], AF.Sin))
                st("V", r("scrF"), r("wtmp2"), lambda e, gb_=gb_:
                   e.tensor_scalar(wtmp2, scrF[:, 1536:2048],
                                   PI / 2.0, None, OP.add))
                st("V", r("wtmp2"), r("wtmp"), lambda e: e.tensor_scalar(
                    wtmp, wtmp2, PI, -TWO_PI, OP.is_gt, OP.mult))
                st("V", r("wtmp2", "wtmp"), r("wtmp"), lambda e:
                   e.tensor_tensor(wtmp, wtmp2, wtmp, OP.add))
                st("V", r("wtmp"), r("wtmp"), lambda e: e.tensor_scalar(
                    wtmp, wtmp, -PI, PI, OP.max, OP.min))
                st("S", r("wtmp"), r("ctbl"), lambda e, gb_=gb_:
                   e.activation(ctbl[:, gb_:gb_ + 512], wtmp, AF.Sin))
                rot_to_tbl2(g)

            def ev_p(e, PRa=PRa, PIa=PIa, g=g):
                a = e.activation(pbuf[:, 0:512], PRa, AF.Identity,
                                 bias=s1col[:, g:g + 1])
                b = e.activation(pbuf[:, 512:1024], PIa, AF.Identity,
                                 bias=s1col[:, 2 + g:3 + g])
                return [a, b]
            st("S", r(pr, pi_, "s1col"), r("pbuf"), ev_p)
            st("V", r("pbuf", "ctbl"), r("vb16"), lambda e, gb_=gb_:
               e.tensor_tensor(vb16[:, 0:512], ctbl[:, gb_:gb_ + 512],
                               pbuf[:, 0:512], OP.mult))
            st("V", r("pbuf", "stbl"), r("vb16"), lambda e, gb_=gb_:
               e.tensor_tensor(vb16[:, 512:1024], stbl[:, gb_:gb_ + 512],
                               pbuf[:, 512:1024], OP.mult))
            st("V", r("vb16"), r("vbuf"), lambda e: e.tensor_tensor(
                vbuf[:, 0:512], vb16[:, 0:512], vb16[:, 512:1024], OP.add))
            st("V", r("pbuf", "ctbl"), r("vb16"), lambda e, gb_=gb_:
               e.tensor_tensor(vb16[:, 0:512], ctbl[:, gb_:gb_ + 512],
                               pbuf[:, 512:1024], OP.mult))
            st("V", r("pbuf", "stbl"), r("vb16"), lambda e, gb_=gb_:
               e.tensor_tensor(vb16[:, 512:1024], stbl[:, gb_:gb_ + 512],
                               pbuf[:, 0:512], OP.mult))
            st("V", r("vb16"), r("vbuf"), lambda e: e.tensor_tensor(
                vbuf[:, 512:1024], vb16[:, 0:512], vb16[:, 512:1024],
                OP.subtract))

            init_r = 0.0 if i == 0 else carry[:, g:g + 1]
            init_i = 0.0 if i == 0 else carry[:, 2 + g:3 + g]
            st("V", r("vbuf", "rho_col", "carry"), r("zbuf"),
               lambda e, g=g, init_r=init_r: e.tensor_tensor_scan(
                   zbuf[:, 0:512], rho_col[:, g:g + 1].broadcast_to([128, 512]),
                   vbuf[:, 0:512], init_r, OP.mult, OP.add))
            st("V", r("vbuf", "rho_col", "carry"), r("zbuf"),
               lambda e, g=g, init_i=init_i: e.tensor_tensor_scan(
                   zbuf[:, 512:1024],
                   rho_col[:, g:g + 1].broadcast_to([128, 512]),
                   vbuf[:, 512:1024], init_i, OP.mult, OP.add))

            def carry_copy(e, g=g):
                a = e.tensor_copy(carry[:, g:g + 1], zbuf[:, 511:512])
                b = e.tensor_copy(carry[:, 2 + g:3 + g], zbuf[:, 1023:1024])
                return [a, b]
            st("V", r("zbuf"), r("carry"), carry_copy)
            if i + 1 < NSUP:
                rot_carry(g)

            fyb, fyR = (fy, "fy") if i % 2 == 0 else (fy2, "fy2")
            st("V", r("ctbl2", "zbuf"), r("vb16"), lambda e, gb_=gb_:
               e.tensor_tensor(vb16[:, 0:512], ctbl2[:, gb_:gb_ + 512],
                               zbuf[:, 0:512], OP.mult))
            st("V", r("stbl2", "zbuf"), r("vb16"), lambda e, gb_=gb_:
               e.tensor_tensor(vb16[:, 512:1024], stbl2[:, gb_:gb_ + 512],
                               zbuf[:, 512:1024], OP.mult))
            st("V", r("vb16"), r(fyR), lambda e, g=g, fyb=fyb: e.tensor_tensor(
                fyb[:, 512 * g:512 * (g + 1)], vb16[:, 0:512],
                vb16[:, 512:1024], OP.subtract))
            st("V", r("stbl2", "zbuf"), r("vb16"), lambda e, gb_=gb_:
               e.tensor_tensor(vb16[:, 0:512], stbl2[:, gb_:gb_ + 512],
                               zbuf[:, 0:512], OP.mult))
            st("V", r("ctbl2", "zbuf"), r("vb16"), lambda e, gb_=gb_:
               e.tensor_tensor(vb16[:, 512:1024], ctbl2[:, gb_:gb_ + 512],
                               zbuf[:, 512:1024], OP.mult))
            st("V", r("vb16"), r(fyR), lambda e, g=g, fyb=fyb: e.tensor_tensor(
                fyb[:, 512 * (2 + g):512 * (3 + g)], vb16[:, 0:512],
                vb16[:, 512:1024], OP.add))

        def stage_outproj(i):
            fyb, fyR = (fy, "fy") if i % 2 == 0 else (fy2, "fy2")
            banks = [(PO[:, 0:512], "po0"), (PO[:, 512:1024], "po1"),
                     (PF.ap(), "pf")]
            for k, (s_, hh) in enumerate([(s_, hh) for s_ in range(NSUB)
                                          for hh in range(2)]):
                bk, bkR = banks[k % 3]

                def mm_op(e, s_=s_, hh=hh, fyb=fyb, bk=bk):
                    ins = []
                    for g in range(4):
                        ins.append(e.matmul(
                            bk,
                            fyb[:, 512 * g + 128 * s_:
                                512 * g + 128 * (s_ + 1)],
                            woT[:, 1024 * g + 512 * hh:
                                1024 * g + 512 * (hh + 1)],
                            start=(g == 0), stop=(g == 3)))
                    return ins
                st("T", r(fyR, "woT"), r(bkR), mm_op)
                st("V", r(bkR) + xres(i), xres(i),
                   lambda e, i=i, s_=s_, hh=hh, bk=bk: e.tensor_tensor(
                       xap(i, 1024 * s_ + 512 * hh,
                           1024 * s_ + 512 * (hh + 1)),
                       xap(i, 1024 * s_ + 512 * hh,
                           1024 * s_ + 512 * (hh + 1)),
                       bk, OP.add))

        def stage_ffn(i):
            hfb, hfR = hfof(i)
            banks = [(PO[:, 0:512], "po0"), (PO[:, 512:1024], "po1"),
                     (PF.ap(), "pf")]
            for k, (s_, hh) in enumerate([(s_, hh) for s_ in range(NSUB)
                                          for hh in range(2)]):
                bA, bAR = banks[(2 * k) % 3]
                bB, bBR = banks[(2 * k + 1) % 3]
                if k % 2 == 0:
                    e1, e2, e3, e4 = (scrF[:, 0:512], scrF[:, 512:1024],
                                      scrF[:, 1024:1536], scrF[:, 1536:2048])
                    eR = "scrF"
                else:
                    e1, e2, e3, e4 = (scr2[:, 0:512], scr2[:, 512:1024],
                                      scr2[:, 1024:1536], scr2[:, 1536:2048])
                    eR = "scr2"
                    eR2 = "scr2"

                def mm_f(e, s_=s_, hh=hh, hfb=hfb, bA=bA, bB=bB):
                    ins = []
                    for u in range(8):
                        for (bk, wmat) in ((bA, cwt), (bB, swt)):
                            ins.append(e.matmul(
                                bk,
                                hfb[:, 512 * u + 128 * s_:
                                    512 * u + 128 * (s_ + 1)],
                                wmat[:, 1024 * u + 512 * hh:
                                     1024 * u + 512 * (hh + 1)],
                                start=(u == 0), stop=False))
                    for (bk, im) in ((bA, 0), (bB, 1)):
                        ins.append(e.matmul(
                            bk, onesb.ap(),
                            s1crowb[0:1, 1024 * im + 512 * hh:
                                    1024 * im + 512 * (hh + 1)],
                            start=False, stop=True))
                    return ins
                st("T", r(hfR, "cwt", "swt", "onesb", "s1crowb"),
                   r(bAR, bBR), mm_f)

                if k % 2 == 0:
                    def sq_f(e, bA=bA, bB=bB, e1=e1, e2=e2):
                        a = e.activation(e1, bA, AF.Square)
                        b = e.activation(e2, bB, AF.Square)
                        return [a, b]
                    st("S", r(bAR, bBR), r(eR), sq_f)
                    st("G", r(eR), r(eR), lambda e, e1=e1, e2=e2, e3=e3:
                       e.tensor_tensor(e3, e1, e2, OP.add))
                    st("S", r(eR), r(eR), lambda e, e3=e3, e4=e4:
                       e.activation(e4, e3, AF.Sqrt))
                    st("V", r(eR, "sgnbc"), r(eR),
                       lambda e, e3=e3, e4=e4, hh=hh: e.tensor_tensor(
                           e3, e4, sgnbc[:, 512 * hh:512 * (hh + 1)],
                           OP.mult))
                    st("V", r(eR) + xres(i), xres(i),
                       lambda e, i=i, s_=s_, hh=hh, e3=e3: e.tensor_tensor(
                           xap(i, 1024 * s_ + 512 * hh,
                               1024 * s_ + 512 * (hh + 1)),
                           xap(i, 1024 * s_ + 512 * hh,
                               1024 * s_ + 512 * (hh + 1)),
                           e3, OP.add))
                else:
                    def sq_f(e, bA=bA, bB=bB, e1=e1, e2=e2):
                        a = e.activation(e1, bA, AF.Square)
                        b = e.activation(e2, bB, AF.Square)
                        return [a, b]
                    st("S", r(bAR, bBR), r(eR, "scr2a", "scr2b"), sq_f)
                    st("G", r(eR, eR2), r(eR2), lambda e, e1=e1, e2=e2, e3=e3:
                       e.tensor_tensor(e3, e1, e2, OP.add))
                    st("S", r(eR2), r(eR2), lambda e, e3=e3, e4=e4:
                       e.activation(e4, e3, AF.Sqrt))
                    st("V", r(eR2, "sgnbc"), r(eR2),
                       lambda e, e3=e3, e4=e4, hh=hh: e.tensor_tensor(
                           e3, e4, sgnbc[:, 512 * hh:512 * (hh + 1)],
                           OP.mult))
                    st("V", r(eR2) + xres(i), xres(i),
                       lambda e, i=i, s_=s_, hh=hh, e3=e3: e.tensor_tensor(
                           xap(i, 1024 * s_ + 512 * hh,
                               1024 * s_ + 512 * (hh + 1)),
                           xap(i, 1024 * s_ + 512 * hh,
                               1024 * s_ + 512 * (hh + 1)),
                           e3, OP.add))

        def stage_store(i):
            sl = i % 3
            if sl < 2:
                st("Y", xres(i), xres(i),
                   lambda e, i=i, sl=sl: e.dma_start(
                       out_h[512 * i:512 * (i + 1), :]
                       .rearrange("(s p) d -> s p d", p=128)
                       .transpose([1, 0, 2]),
                       xt[:, 4096 * sl:4096 * sl + 4096]
                       .rearrange("p (s d) -> p s d", s=4)),
                   dma=True)
            else:
                st("Y", xres(i), xres(i),
                   lambda e, i=i: e.dma_start(
                       out_h[512 * i:512 * i + 256, :]
                       .rearrange("(s p) d -> s p d", p=128)
                       .transpose([1, 0, 2]),
                       arenaF.ap().rearrange("p (s d) -> p s d", s=2)),
                   dma=True)
                st("Y", xres(i), xres(i),
                   lambda e, i=i: e.dma_start(
                       out_h[512 * i + 256:512 * (i + 1), :]
                       .rearrange("(s p) d -> s p d", p=128)
                       .transpose([1, 0, 2]),
                       poolB.ap().rearrange("p (s d) -> p s d", s=2)),
                   dma=True)

        stage_load(0)
        stage_load(1)
        emit_modhead()
        stage_ln(0, 1)
        emit_p123()
        stage_spectral(0, 0)
        stage_spectral(0, 1)
        stage_phase(0, 0)
        stage_phase(0, 1)
        emit_p3head(0, 6)
        stage_load(2)
        for i in range(NSUP):
            if i + 1 < NSUP:
                stage_ln(i + 1, 1)
                stage_spectral(i + 1, 0)
                stage_spectral(i + 1, 1)
            if i == 0:
                emit_p3head(6, 10)
            stage_outproj(i)
            if i == 0:
                emit_p3head(10, 13)
            stage_ln(i, 2)
            if i + 1 < NSUP:
                stage_phase(i + 1, 0)
                stage_phase(i + 1, 1)
            if i == 0:
                emit_p3head(13, 16)
                for fn in p3_tail:
                    fn()
            stage_ffn(i)
            stage_store(i)
            if i + 3 < NSUP:
                stage_load(i + 3)

        st("Y", r(xslot[0], xslot[1]), [], lambda e: e.nop())

        semctx = contextlib.ExitStack()
        with semctx:
            sems = {}
            for n in sch.sem_names():
                sems[n] = semctx.enter_context(nc.semaphore("sem" + n))
            with nc.Block() as block:
                sch.emit(block, sems)

    return nc


_NC_CACHE = None


def kernel(**inputs):
    global _NC_CACHE
    x = np.ascontiguousarray(inputs["L0_attn_x"], dtype=np.float32)
    w_in = np.ascontiguousarray(inputs["L0_attn_W_in"], dtype=np.float32)
    w_out = np.ascontiguousarray(inputs["L0_attn_W_out"], dtype=np.float32)
    ld = np.ascontiguousarray(inputs["L0_attn_log_decay"], dtype=np.float32)
    fr = np.ascontiguousarray(inputs["L0_attn_frequencies"], dtype=np.float32)
    th = np.ascontiguousarray(inputs["L0_ffn_theta"], dtype=np.float32)
    ph = np.ascontiguousarray(inputs["L0_ffn_phi"], dtype=np.float32)
    thn = np.ascontiguousarray(inputs["L0_ffn_theta_noise"], dtype=np.float32)
    phn = np.ascontiguousarray(inputs["L0_ffn_phi_noise"], dtype=np.float32)
    c = np.ascontiguousarray(inputs["L0_c"], dtype=np.float32)
    wmod = np.ascontiguousarray(inputs["L0_wmod"], dtype=np.float32)

    if _NC_CACHE is None:
        _NC_CACHE = build_nc()
    nc = _NC_CACHE

    in_maps = []
    for b in range(8):
        in_maps.append({
            "x": x[b], "w_in": w_in, "w_out": w_out, "ld": ld, "fr": fr,
            "theta": th, "phi": ph, "thn": thn, "phn": phn,
            "c": c,
            "bsel": np.eye(8, dtype=np.float32)[b],
            "wmod": np.ascontiguousarray(wmod[:, 768 * b:768 * (b + 1)]),
        })
    res = run_bass_kernel_spmd(nc, in_maps, core_ids=list(range(8)))
    out = np.stack([r["out"] for r in res.results], axis=0)
    return out


if __name__ == "__main__":
    nc = build_nc()
    print("instructions:", nc.n_instructions()
          if hasattr(nc, "n_instructions") else "built")


# version 30
# speedup vs baseline: 1.0587x; 1.0587x over previous
import math
import numpy as np

import concourse.bass as bass
import concourse.mybir as mybir
from concourse.bass_utils import run_bass_kernel_spmd

F32 = mybir.dt.float32
BF16 = mybir.dt.bfloat16
OP = mybir.AluOpType
AF = mybir.ActivationFunctionType

D = 1024
S = 2048
NF = 256
F2 = 512
DC = 8
NSUP = 4
TT = 512
NSUB = 4
PI = math.pi
TWO_PI = 2.0 * math.pi
EPS = 1e-5


class Res:
    __slots__ = ("w", "r")

    def __init__(self):
        self.w = None
        self.r = []


class Sched:
    ENGINES = ("T", "V", "S", "G", "Y")
    NSLOT = 8

    def __init__(self):
        self.steps = []
        self.cnt = {e: 0 for e in self.ENGINES}
        self.dma_cnt = {}
        self.dma_idx = {}

    def sem_names(self):
        names = list(self.ENGINES)
        for e in ("Y", "G"):
            names += ["%s%d" % (e, k) for k in range(self.NSLOT)]
        return names

    def step(self, eng, reads, writes, fn, dma=False):
        deps = []
        for r in reads:
            if r.w is not None:
                deps.append(r.w)
        for w in writes:
            if w.w is not None:
                deps.append(w.w)
            deps.extend(w.r)
        if dma:
            idx = self.dma_idx.get(eng, 0)
            self.dma_idx[eng] = idx + 1
            slot = idx % self.NSLOT
            sem_name = "%s%d" % (eng, slot)
            prev = self.dma_cnt.get(sem_name, 0)
            if prev:
                deps.append((sem_name, prev))
            self.dma_cnt[sem_name] = prev + 16
            ev = (sem_name, prev + 16)
            inc = 16
        else:
            self.cnt[eng] += 1
            sem_name = eng
            ev = (eng, self.cnt[eng])
            inc = 1
        deps = sorted(set(deps))
        self.steps.append((eng, deps, fn, sem_name, inc))
        for r in reads:
            r.r.append(ev)
        for w in writes:
            w.w = ev
            w.r = []
        return ev

    def emit(self, block, sems):
        all_names = self.sem_names()

        def make_body(eng_name):
            def body(eng):
                hw = {n: 0 for n in all_names}
                for (e, deps, fn, sem_name, inc) in self.steps:
                    if e != eng_name:
                        continue
                    for (dn, dc) in deps:
                        if dc > hw[dn]:
                            eng.wait_ge(sems[dn], dc)
                            hw[dn] = dc
                    ret = fn(eng)
                    last = ret[-1] if isinstance(ret, (list, tuple)) else ret
                    last.then_inc(sems[sem_name], inc)
            return body
        block.tensor(make_body("T"))
        block.vector(make_body("V"))
        block.scalar(make_body("S"))
        block.gpsimd(make_body("G"))
        block.sync(make_body("Y"))


def build_nc():
    nc = bass.Bass("TRN2", target_bir_lowering=False)

    x_h = nc.dram_tensor("x", [S, D], F32, kind="ExternalInput").ap()
    win_h = nc.dram_tensor("w_in", [F2, D], F32, kind="ExternalInput").ap()
    wout_h = nc.dram_tensor("w_out", [D, F2], F32, kind="ExternalInput").ap()
    ld_h = nc.dram_tensor("ld", [NF], F32, kind="ExternalInput").ap()
    fr_h = nc.dram_tensor("fr", [NF], F32, kind="ExternalInput").ap()
    th_h = nc.dram_tensor("theta", [D, D], F32, kind="ExternalInput").ap()
    ph_h = nc.dram_tensor("phi", [D, D], F32, kind="ExternalInput").ap()
    thn_h = nc.dram_tensor("thn", [D, D], F32, kind="ExternalInput").ap()
    phn_h = nc.dram_tensor("phn", [D, D], F32, kind="ExternalInput").ap()
    c_h = nc.dram_tensor("c", [8, D], F32, kind="ExternalInput").ap()
    wmod_h = nc.dram_tensor("wmod", [D, 768], F32, kind="ExternalInput").ap()
    bsel_h = nc.dram_tensor("bsel", [8], F32, kind="ExternalInput").ap()
    ccin_h = nc.dram_tensor("ccin", [8 * 768], F32).ap()
    ccout_h = nc.dram_tensor("ccout", [8 * 8 * 768], F32).ap()
    out_h = nc.dram_tensor("out", [S, D], F32, kind="ExternalOutput").ap()

    sch = Sched()

    import contextlib
    ctx = contextlib.ExitStack()

    def sb(name, shape, dtype):
        return ctx.enter_context(nc.sbuf_tensor(name, shape, dtype))

    def ps(name, shape, dtype):
        return ctx.enter_context(nc.psum_tensor(name, shape, dtype))

    with ctx:
        xt = sb("xt", [128, 2 * NSUB * D], F32)
        h1tm = sb("h1tm", [128, NSUB * D], BF16)
        hf = sb("hf", [128, DC * TT], BF16)
        fy = sb("fy", [128, 4 * TT], BF16)
        ctbl = sb("ctbl", [128, 1024], BF16)
        stbl = sb("stbl", [128, 1024], BF16)

        poolB = sb("poolB", [128, 2048], F32)
        iotac = poolB[:, 0:512]
        wtmp = poolB[:, 512:1024]
        wtmp2 = poolB[:, 1024:1536]
        kbuf = sb("kbuf", [128, 512], mybir.dt.int32)
        rotc = sb("rotc", [128, 12], F32)
        scrF = sb("scrF", [128, 2048], F32)
        zbuf = sb("zbuf", [128, 1024], BF16)
        vbuf = sb("vbuf", [128, 1024], BF16)
        carry = sb("carry", [128, 4], F32)
        ctbl2 = sb("ctbl2", [128, 1024], BF16)
        stbl2 = sb("stbl2", [128, 1024], BF16)
        pbuf = sb("pbuf", [128, 1024], BF16)
        vb16 = sb("vb16", [128, 1024], BF16)
        scr2 = sb("scr2", [128, 2048], F32)
        fy2 = sb("fy2", [128, 4 * TT], BF16)
        carry2 = sb("carry2", [128, 4], F32)
        winT = sb("winT", [128, DC * F2], BF16)
        woT = sb("woT", [128, 4 * D], BF16)
        cwt = sb("cwt", [128, DC * D], BF16)
        swt = sb("swt", [128, DC * D], BF16)
        arenaF = sb("arenaF", [128, 2048], F32)
        arenaB = sb("arenaB", [128, 6144], BF16)
        grow = sb("grow", [1, 1024], F32)
        sgnbc = sb("sgnbc", [128, 1024], BF16)
        gbbc = sb("gbbc", [128, 1024], F32)
        hf2 = sb("hf2", [128, DC * TT], BF16)
        h1tm2 = sb("h1tm2", [128, NSUB * D], BF16)
        wstageB = h1tm2
        s1crowb = sb("s1crowb", [1, 2048], BF16)
        modc = sb("modc", [128, 48], F32)
        dcol = sb("dcol", [128, 48], F32)
        sab = sb("sab", [128, 8], BF16)
        sbb = sb("sbb", [128, 8], BF16)
        s1col = sb("s1col", [128, 4], F32)
        c_col = sb("c_col", [128, 64], F32)
        silub = sb("silub", [128, 64], BF16)
        rows8 = sb("rows8", [8, 768], F32)
        selbuf = sb("selbuf", [8, 2 * 768], BF16)
        bselc = sb("bselc", [8, 1], BF16)
        ldfr = sb("ldfr", [128, 4], F32)
        rho_col = sb("rho_col", [128, 2], F32)
        omg_col = sb("omg_col", [128, 2], F32)
        sums = sb("sums", [128, 16], F32)
        sumsq = sb("sumsq", [128, 16], F32)
        rsig = sb("rsig", [128, 16], F32)
        nmr = sb("nmr", [128, 16], F32)
        rstat = sb("rstat", [128, 48], F32)
        idf = sb("idf", [128, 128], F32)
        idb = sb("idb", [128, 128], BF16)
        onesf = sb("onesf", [1, 128], F32)
        onesb = sb("onesb", [1, 128], BF16)
        one1 = sb("one1", [1, 1], F32)
        pcol = sb("pcol", [128, 1], F32)
        stage8 = sb("stage8", [68, 128], F32)

        PS = ps("PS", [128, 2048], F32)
        PO = ps("PO", [128, 1024], F32)
        PT = ps("PT", [128, 1024], BF16)
        PF = ps("PF", [128, 512], F32)

        R = {name: Res() for name in [
            "xt0", "xt1", "xt2", "h1tm", "hf", "fy", "ctbl", "stbl", "wtmp",
            "wtmp2", "kbuf", "rotc", "gbbc", "hf2", "h1tm2", "pf", "pt2",
            "arenaF2",
            "iotac",
            "scrF", "zbuf", "vbuf", "carry", "winT", "woT", "poolB",
            "ctbl2", "stbl2", "fy2", "carry2", "ccb", "pbuf", "vb16", "scr2",
            "scr2a", "scr2b",
            "cwt", "swt", "arenaF", "arenaB", "grow", "sgnbc",
            "s1crowb", "modc", "dcol", "sab", "sbb", "s1col", "c_col", "silub",
            "ldfr", "rho_col", "omg_col", "sums", "sumsq", "rsig", "nmr",
            "rstat", "idf", "idb", "onesf", "onesb", "one1", "pcol", "rows8",
            "selbuf", "selbuf2", "bselc",
            "stage8",
            "ps0", "ps1", "ps2", "ps3", "po0", "po1", "pt",
        ]}

        def r(*names):
            return [R[n] for n in names]

        st = sch.step

        st("Y", [], r("stage8"), lambda e: e.dma_start(
            stage8[0:64, :], c_h.rearrange("b (q p) -> (b q) p", p=128)),
           dma=True)
        st("Y", [], r("stage8"), lambda e: e.dma_start(
            stage8[64:66, :], ld_h.rearrange("(q p) -> q p", p=128)), dma=True)
        st("Y", [], r("stage8"), lambda e: e.dma_start(
            stage8[66:68, :], fr_h.rearrange("(q p) -> q p", p=128)), dma=True)

        def _consts(e):
            a = e.memset(onesf.ap(), 1.0)
            b = e.memset(onesb.ap(), 1.0)
            c = e.memset(one1.ap(), 1.0)
            d = e.memset(carry.ap(), 0.0)
            return d
        st("V", [], r("onesf", "onesb", "one1", "carry"), _consts)

        st("G", [], r("pcol"), lambda e: e.iota(
            pcol.ap(), [[1, 1]], channel_multiplier=1, allow_small_or_imprecise_dtypes=True))
        st("G", [], r("iotac"), lambda e: e.iota(
            iotac, [[1, 512]], channel_multiplier=0, allow_small_or_imprecise_dtypes=True))
        st("G", [], r("idf"), lambda e: e.iota(
            idf.ap(), [[1, 128]], channel_multiplier=0, allow_small_or_imprecise_dtypes=True))
        st("V", r("idf", "pcol"), r("idf"), lambda e: e.tensor_scalar(
            idf.ap(), idf.ap(), pcol[:, 0:1], None, OP.is_equal))
        st("V", r("idf"), r("idb"), lambda e: e.tensor_copy(idb.ap(), idf.ap()))

        st("T", r("stage8", "idf"), r("po0"), lambda e: e.transpose(
            PO[:, 0:68], stage8[0:68, :], idf[0:68, 0:68]))
        st("V", r("po0"), r("c_col"), lambda e: e.tensor_copy(
            c_col.ap(), PO[:, 0:64]))
        st("V", r("po0"), r("ldfr"), lambda e: e.tensor_copy(
            ldfr.ap(), PO[:, 64:68]))

        st("S", r("ldfr"), r("rho_col"), lambda e: e.activation(
            rho_col.ap(), ldfr[:, 0:2], AF.Sigmoid))
        st("V", r("ldfr"), r("omg_col"), lambda e: e.tensor_scalar(
            omg_col.ap(), ldfr[:, 2:4], 0.1, None, OP.mult))
        st("V", r("omg_col"), r("rstat"), lambda e: e.tensor_scalar(
            rstat[:, 0:2], omg_col.ap(), 1.0 / TWO_PI, None, OP.mult))
        st("V", r("rstat"), r("kbuf"), lambda e: e.tensor_copy(
            kbuf[:, 0:2], rstat[:, 0:2]))
        st("V", r("kbuf", "omg_col"), r("omg_col"), lambda e:
           e.scalar_tensor_tensor(omg_col.ap(), kbuf[:, 0:2], -TWO_PI,
                                  omg_col.ap(), OP.mult, OP.add))

        st("V", r("omg_col"), r("rstat"), lambda e: e.tensor_scalar(
            rstat[:, 0:2], omg_col.ap(), -PI, PI, OP.max, OP.min))
        st("S", r("rstat"), r("rotc"), lambda e: e.activation(
            rotc[:, 2:4], rstat[:, 0:2], AF.Sin))
        st("V", r("omg_col"), r("rstat"), lambda e: e.tensor_scalar(
            rstat[:, 0:2], omg_col.ap(), PI / 2.0, None, OP.add))
        st("V", r("rstat"), r("rstat"), lambda e: e.tensor_scalar(
            rstat[:, 2:4], rstat[:, 0:2], PI, -TWO_PI, OP.is_gt, OP.mult))
        st("V", r("rstat"), r("rstat"), lambda e: e.tensor_tensor(
            rstat[:, 0:2], rstat[:, 0:2], rstat[:, 2:4], OP.add))
        st("V", r("rstat"), r("rstat"), lambda e: e.tensor_scalar(
            rstat[:, 0:2], rstat[:, 0:2], -PI, PI, OP.max, OP.min))
        st("S", r("rstat"), r("rotc"), lambda e: e.activation(
            rotc[:, 0:2], rstat[:, 0:2], AF.Sin))
        st("V", r("omg_col"), r("rstat"), lambda e: e.tensor_scalar(
            rstat[:, 0:2], omg_col.ap(), 512.0, None, OP.mult))
        st("V", r("rstat"), r("rstat"), lambda e: e.tensor_scalar(
            rstat[:, 2:4], rstat[:, 0:2], 1.0 / TWO_PI, None, OP.mult))
        st("V", r("rstat"), r("kbuf"), lambda e: e.tensor_copy(
            kbuf[:, 0:2], rstat[:, 2:4]))
        st("V", r("kbuf", "rstat"), r("rstat"), lambda e:
           e.scalar_tensor_tensor(rstat[:, 0:2], kbuf[:, 0:2], -TWO_PI,
                                  rstat[:, 0:2], OP.mult, OP.add))
        st("V", r("rstat"), r("rstat"), lambda e: e.tensor_scalar(
            rstat[:, 2:4], rstat[:, 0:2], PI, -TWO_PI, OP.is_gt, OP.mult))
        st("V", r("rstat"), r("rstat"), lambda e: e.tensor_tensor(
            rstat[:, 0:2], rstat[:, 0:2], rstat[:, 2:4], OP.add))
        st("V", r("rstat"), r("rstat"), lambda e: e.tensor_scalar(
            rstat[:, 2:4], rstat[:, 0:2], -PI, TWO_PI, OP.is_lt, OP.mult))
        st("V", r("rstat"), r("rstat"), lambda e: e.tensor_tensor(
            rstat[:, 0:2], rstat[:, 0:2], rstat[:, 2:4], OP.add))
        st("V", r("rstat"), r("rstat"), lambda e: e.tensor_scalar(
            rstat[:, 0:2], rstat[:, 0:2], -PI, PI, OP.max, OP.min))
        st("S", r("rstat"), r("rotc"), lambda e: e.activation(
            rotc[:, 8:10], rstat[:, 0:2], AF.Sin))
        st("V", r("rstat"), r("rstat"), lambda e: e.tensor_scalar(
            rstat[:, 4:6], rstat[:, 0:2], PI / 2.0, None, OP.add))
        st("V", r("rstat"), r("rstat"), lambda e: e.tensor_scalar(
            rstat[:, 2:4], rstat[:, 4:6], PI, -TWO_PI, OP.is_gt, OP.mult))
        st("V", r("rstat"), r("rstat"), lambda e: e.tensor_tensor(
            rstat[:, 4:6], rstat[:, 4:6], rstat[:, 2:4], OP.add))
        st("V", r("rstat"), r("rstat"), lambda e: e.tensor_scalar(
            rstat[:, 4:6], rstat[:, 4:6], -PI, PI, OP.max, OP.min))
        st("S", r("rstat"), r("rotc"), lambda e: e.activation(
            rotc[:, 6:8], rstat[:, 4:6], AF.Sin))
        st("V", r("rotc"), r("rotc"), lambda e: e.tensor_scalar(
            rotc[:, 4:6], rotc[:, 2:4], -1.0, None, OP.mult))
        st("V", r("rotc"), r("rotc"), lambda e: e.tensor_scalar(
            rotc[:, 10:12], rotc[:, 8:10], -1.0, None, OP.mult))


        st("S", r("c_col"), r("silub"), lambda e: e.activation(
            silub.ap(), c_col.ap(), AF.Silu))

        p3_tail = []
        p3_emitters = []

        def emit_modhead():
            st("G", [], r("arenaB"), lambda e: e.dma_start(
                arenaB[:, 0:6144].rearrange("p (c q) -> p c q", c=8),
                wmod_h.rearrange("(c p) q -> c p q", p=128).transpose([1, 0, 2])),
               dma=True)
            st("G", [], r("h1tm2"), lambda e: e.dma_start(
                wstageB.ap().rearrange("p (c d) -> p c d", c=4),
                win_h.rearrange("(c p) d -> c p d", p=128).transpose([1, 0, 2])),
               dma=True)

            def mm_matvec(e):
                ins = []
                silv = silub.ap().rearrange("p (b k) -> p k b", k=8)
                for k in range(8):
                    ins.append(e.matmul(
                        PS[0:8, 0:512], silv[:, k, :],
                        arenaB[:, 768 * k:768 * k + 512],
                        start=(k == 0), stop=(k == 7)))
                for k in range(8):
                    ins.append(e.matmul(
                        PS[0:8, 512:768], silv[:, k, :],
                        arenaB[:, 768 * k + 512:768 * (k + 1)],
                        start=(k == 0), stop=(k == 7)))
                return ins
            st("T", r("silub", "arenaB"), r("ps0", "ps1"), mm_matvec)
            st("S", r("ps0", "ps1"), r("rows8"), lambda e: e.copy(
                rows8.ap(), PS[0:8, 0:768]))
            st("G", r("rows8"), r("ccb"), lambda e: e.dma_start(
                ccin_h.rearrange("(b q) -> b q", q=768), rows8.ap()), dma=True)
            st("G", [], r("bselc"), lambda e: e.dma_start(
                bselc.ap(), bsel_h.rearrange("(b q) -> b q", q=1)),
               dma=True)
            for c4 in range(4):
                def mm_t(e, c4=c4):
                    ins = []
                    for u in range(8):
                        ins.append(e.matmul(
                            PT[:, 128 * u:128 * (u + 1)],
                            wstageB[:, 1024 * c4 + 128 * u:
                                    1024 * c4 + 128 * (u + 1)],
                            idb.ap(), start=(u == 0), stop=(u == 7),
                            is_transpose=True))
                    return ins
                st("T", r("h1tm2", "idb"), r("pt"), mm_t)
                st("V", r("pt"), r("winT"), lambda e, c4=c4: e.tensor_copy(
                    winT.ap().rearrange("p (u f) -> p u f", u=8)
                    [:, :, 128 * c4:128 * (c4 + 1)],
                    PT.ap().rearrange("p (u f) -> p u f", u=8)))

            st("G", r("ccb"), r("ccb"), lambda e: e.collective_compute(
                "AllGather", OP.bypass, replica_groups=[list(range(8))],
                ins=[ccin_h.opt()], outs=[ccout_h.opt()]))

        def emit_p123():
            for k_ in range(8):
                sbase = 768 * (k_ % 2)
                sR = "selbuf" if k_ % 2 == 0 else "selbuf2"
                st("G", r("ccb"), r(sR), lambda e, k_=k_, sbase=sbase:
                   e.dma_start(
                       selbuf[:, sbase:sbase + 768],
                       ccout_h[6144 * k_:6144 * (k_ + 1)]
                       .rearrange("(b q) -> b q", b=8)), dma=True)

                def mm_sel(e, k_=k_, sbase=sbase):
                    ins = []
                    for m in range(6):
                        q = 6 * k_ + m
                        ins.append(e.matmul(
                            PO[:, q:q + 1],
                            selbuf[:, sbase + 128 * m:sbase + 128 * (m + 1)],
                            bselc.ap(),
                            start=(q == 0), stop=(q == 47)))
                    return ins
                st("T", r(sR, "bselc"), r("po0"), mm_sel)
            st("V", r("po0"), r("modc"), lambda e: e.tensor_copy(
                modc.ap(), PO[:, 0:48]))
            st("V", r("modc"), r("dcol"), lambda e: e.tensor_scalar(
                dcol[:, 0:8], modc[:, 8:16], 1.0, None, OP.add))
            st("V", r("modc"), r("dcol"), lambda e: e.tensor_scalar(
                dcol[:, 8:16], modc[:, 32:40], 1.0, None, OP.add))
            st("V", r("modc"), r("sab"), lambda e: e.tensor_copy(
                sab.ap(), modc[:, 0:8]))
            st("V", r("modc"), r("sbb"), lambda e: e.tensor_copy(
                sbb.ap(), modc[:, 24:32]))

            def clamp_cols(src0, dst0, sgn_dst):
                st("V", r("modc"), r("rstat"), lambda e: e.tensor_scalar(
                    rstat[:, 0:8], modc[:, src0:src0 + 8], 0.0, 2.0,
                    OP.is_ge, OP.mult))
                st("V", r("rstat"), r("rstat"), lambda e: e.tensor_scalar(
                    rstat[:, 0:8], rstat[:, 0:8], -1.0, None, OP.add))
                st("S", r("modc"), r("rstat"), lambda e: e.activation(
                    rstat[:, 8:16], modc[:, src0:src0 + 8], AF.Abs))
                st("V", r("rstat"), r("rstat"), lambda e: e.tensor_scalar(
                    rstat[:, 8:16], rstat[:, 8:16], 0.25, None, OP.max))
                st("V", r("rstat"), r("dcol"), lambda e: e.tensor_tensor(
                    dcol[:, dst0:dst0 + 8], rstat[:, 0:8], rstat[:, 8:16],
                    OP.mult))
                if sgn_dst is not None:
                    st("V", r("rstat"), r("dcol"), lambda e: e.tensor_copy(
                        dcol[:, sgn_dst:sgn_dst + 8], rstat[:, 0:8]))
            clamp_cols(16, 16, None)
            clamp_cols(40, 24, 32)

            def bcast(cols_off, dst_name, dst_ap_fn):
                def mm_t(e):
                    ins = []
                    for cch in range(8):
                        ins.append(e.transpose(
                            PO[0:1, 128 * cch:128 * (cch + 1)],
                            dcol[:, cols_off + cch:cols_off + cch + 1],
                            idf.ap()))
                    return ins
                st("T", r("dcol", "idf"), r("po0", "po1"), mm_t)
                st("V", r("po0", "po1"), r("grow"), lambda e: e.tensor_copy(
                    grow.ap(), PO[0:1, 0:1024]))

                def mm_b(e):
                    ins = []
                    for hh in range(2):
                        ins.append(e.matmul(
                            PS[:, 512 * hh:512 * (hh + 1)], onesf.ap(),
                            grow[0:1, 512 * hh:512 * (hh + 1)],
                            start=True, stop=True))
                    return ins
                st("T", r("grow", "onesf"), r("ps0", "ps1"), mm_b)
                st("V", r("ps0", "ps1"), r(dst_name), lambda e: e.tensor_copy(
                    dst_ap_fn(), PS[:, 0:1024]))
            bcast(16, "scrF", lambda: scrF[:, 0:1024])
            bcast(24, "gbbc", lambda: gbbc.ap())
            bcast(32, "sgnbc", lambda: sgnbc.ap())
            def mm_s1(e):
                ins = []
                for u in range(8):
                    for g in range(4):
                        ins.append(e.matmul(
                            s1psum[:, g:g + 1],
                            winT[:, 512 * u + 128 * g:512 * u + 128 * (g + 1)],
                            sab[:, u:u + 1],
                            start=(u == 0 and g == 0), stop=(u == 7 and g == 3)))
                return ins
            s1psum = PO
            st("T", r("winT", "sab"), r("po0"), mm_s1)
            st("V", r("po0"), r("s1col"), lambda e: e.tensor_copy(
                s1col.ap(), PO[:, 0:4]))

            def sc_win(e):
                ins = []
                for u in range(8):
                    ins.append(e.tensor_scalar(
                        winT[:, 512 * u:512 * (u + 1)],
                        winT[:, 512 * u:512 * (u + 1)],
                        dcol[:, u:u + 1], None, OP.mult))
                return ins
            st("V", r("dcol"), r("winT"), sc_win)

            st("G", [], r("arenaB"), lambda e: e.dma_start(
                arenaB[:, 0:4096].rearrange("p (c f) -> p c f", c=8),
                wout_h.rearrange("(c p) f -> c p f", p=128).transpose([1, 0, 2])),
               dma=True)
            for g in range(4):
                def mm_t(e, g=g):
                    ins = []
                    for c8 in range(8):
                        ins.append(e.matmul(
                            PT[:, 128 * c8:128 * (c8 + 1)],
                            arenaB[:, 512 * c8 + 128 * g:512 * c8 + 128 * (g + 1)],
                            idb.ap(), start=(c8 == 0), stop=(c8 == 7),
                            is_transpose=True))
                    return ins
                st("T", r("arenaB", "idb"), r("pt"), mm_t)
                st("V", r("pt"), r("woT"), lambda e, g=g: e.tensor_copy(
                    woT[:, 1024 * g:1024 * (g + 1)], PT.ap()))

            def sc_wo(e):
                ins = []
                for g in range(4):
                    ins.append(e.tensor_tensor(
                        woT[:, 1024 * g:1024 * (g + 1)],
                        woT[:, 1024 * g:1024 * (g + 1)],
                        scrF[:, 0:1024], OP.mult))
                return ins
            st("V", r("scrF"), r("woT"), sc_wo)


        def emit_p3head(lo=0, hi=16):
            pairA = (arenaF[:, 0:1024], ("arenaF",),
                     arenaF[:, 1024:2048], ("arenaF2",))
            pairC = (scr2[:, 0:1024], ("scr2a",),
                     scr2[:, 1024:2048], ("scr2b",))
            for (mi, (mat_h, noise_h, dst, dstR, shift)) in enumerate((
                    (th_h, thn_h, cwt, "cwt", PI / 2.0),
                    (ph_h, phn_h, swt, "swt", 0.0))):
                for q in range(8):
                    qq = 8 * mi + q
                    if not (lo <= qq < hi):
                        continue
                    slot, aR, nslot, nR = pairA if qq < 6 and qq % 2 == 0 \
                        else pairC
                    st("Y", [], r(*aR), lambda e, mat_h=mat_h, q=q, slot=slot:
                       e.dma_start(slot, mat_h[128 * q:128 * (q + 1), :]),
                       dma=True)
                    st("Y", [], r(*nR), lambda e, noise_h=noise_h, q=q,
                       nslot=nslot:
                       e.dma_start(nslot, noise_h[128 * q:128 * (q + 1), :]),
                       dma=True)
                    st("V", r(*aR, *nR), r(*aR),
                       lambda e, shift=shift, slot=slot, nslot=nslot:
                       e.scalar_tensor_tensor(slot, nslot, shift, slot,
                                              OP.add, OP.add))
                    st("V", r(*aR), r(*aR), lambda e, slot=slot:
                       e.tensor_scalar(slot, slot, -PI, PI,
                                       OP.max, OP.min))
                    st("S", r(*aR), r(dstR), lambda e, dst=dst, q=q, slot=slot:
                       e.activation(dst[:, 1024 * q:1024 * (q + 1)],
                                    slot, AF.Sin))
                def sc_gb(e, dst=dst):
                    ins = []
                    for u in range(8):
                        ins.append(e.tensor_tensor(
                            dst[:, 1024 * u:1024 * (u + 1)],
                            dst[:, 1024 * u:1024 * (u + 1)],
                            gbbc.ap(), OP.mult))
                    return ins

                im = 0 if dst is cwt else 1

                def p3_tail_fn(dst=dst, dstR=dstR, im=im, sc_gb=sc_gb):
                    st("V", r("gbbc"), r(dstR), sc_gb)

                    def mm_s1c(e, dst=dst):
                        ins = []
                        for u in range(8):
                            for hh in range(2):
                                ins.append(e.matmul(
                                    PO[0:1, 512 * hh:512 * (hh + 1)],
                                    sbb[:, u:u + 1],
                                    dst[:, 1024 * u + 512 * hh:
                                        1024 * u + 512 * (hh + 1)],
                                    start=(u == 0), stop=(u == 7)))
                        return ins
                    st("T", r(dstR, "sbb"), r("po0", "po1"), mm_s1c)
                    st("V", r("po0", "po1"), r("s1crowb"),
                       lambda e, im=im: e.tensor_copy(
                           s1crowb[0:1, 1024 * im:1024 * (im + 1)],
                           PO[0:1, 0:1024]))

                    def sc_cb(e, dst=dst):
                        ins = []
                        for u in range(8):
                            ins.append(e.tensor_scalar(
                                dst[:, 1024 * u:1024 * (u + 1)],
                                dst[:, 1024 * u:1024 * (u + 1)],
                                dcol[:, 8 + u:9 + u], None, OP.mult))
                        return ins
                    st("V", r("dcol"), r(dstR), sc_cb)
                if hi == 16:
                    p3_tail.append(p3_tail_fn)

        xslot = ["xt0", "xt1", "xt2"]
        XPOOL = ["arenaF", "arenaF2", "poolB", "iotac", "wtmp", "wtmp2"]

        def xap(i, lo, hi):
            sl = i % 3
            if sl < 2:
                base = 4096 * sl
                return xt[:, base + lo:base + hi]
            if hi <= 2048:
                return arenaF[:, lo:hi]
            assert lo >= 2048
            return poolB[:, lo - 2048:hi - 2048]

        def xres(i):
            sl = i % 3
            if sl < 2:
                return [R[xslot[sl]]]
            return [R["xt2"]] + [R[n] for n in XPOOL]

        def hfof(i):
            return (hf, "hf") if i % 2 == 0 else (hf2, "hf2")

        def h1of(i):
            return (h1tm, "h1tm") if i % 2 == 0 else (h1tm2, "h1tm2")

        def ptof(k):
            return (PT, "pt")

        ptctr = [0]

        def stage_load(i):
            sl = i % 3
            if sl < 2:
                st("Y", [], xres(i), lambda e, i=i, sl=sl: e.dma_start(
                    xt[:, 4096 * sl:4096 * sl + 4096]
                    .rearrange("p (s d) -> p s d", s=4),
                    x_h[512 * i:512 * (i + 1), :]
                    .rearrange("(s p) d -> s p d", p=128).transpose([1, 0, 2])),
                   dma=True)
            else:
                st("Y", [], xres(i), lambda e, i=i: e.dma_start(
                    arenaF.ap().rearrange("p (s d) -> p s d", s=2),
                    x_h[512 * i:512 * i + 256, :]
                    .rearrange("(s p) d -> s p d", p=128).transpose([1, 0, 2])),
                   dma=True)
                st("Y", [], xres(i), lambda e, i=i: e.dma_start(
                    poolB.ap().rearrange("p (s d) -> p s d", s=2),
                    x_h[512 * i + 256:512 * (i + 1), :]
                    .rearrange("(s p) d -> s p d", p=128).transpose([1, 0, 2])),
                   dma=True)

        def stage_ln(i, which, part="all"):
            htm, htmR = h1of(i)
            hfb, hfR = hfof(i)
            c0 = 4 * i
            if part == "apply":
                emit_ln_apply(i, htm, htmR, hfb, hfR, c0)
                return

            def stats_g(e, i=i, c0=c0, htm=htm):
                return [e.tensor_scalar(
                    htm[:, 1024 * s_:1024 * (s_ + 1)],
                    xap(i, 1024 * s_, 1024 * (s_ + 1)),
                    1.0, None, OP.mult, OP.add,
                    accum_out=sums[:, c0 + s_:c0 + s_ + 1])
                    for s_ in range(NSUB)]
            st("V", xres(i), r("sums", htmR), stats_g)

            def stats_s(e, i=i, c0=c0):
                return [e.activation(
                    arenaB[:, 1024 * s_:1024 * (s_ + 1)],
                    xap(i, 1024 * s_, 1024 * (s_ + 1)),
                    AF.Square, accum_out=sumsq[:, c0 + s_:c0 + s_ + 1])
                    for s_ in range(NSUB)]
            st("S", xres(i), r("sumsq", "arenaB"), stats_s)

            c1 = c0 + 4
            st("V", r("sumsq"), r("rstat"), lambda e, c0=c0, c1=c1:
               e.tensor_scalar(rstat[:, c0:c1], sumsq[:, c0:c1],
                               1.0 / D, EPS, OP.mult, OP.add))
            st("V", r("sums"), r("nmr"), lambda e, c0=c0, c1=c1:
               e.tensor_scalar(nmr[:, c0:c1], sums[:, c0:c1],
                               -1.0 / D, None, OP.mult))
            st("V", r("nmr"), r("rstat"), lambda e, c0=c0, c1=c1:
               e.tensor_tensor(rstat[:, 16 + c0:16 + c1], nmr[:, c0:c1],
                               nmr[:, c0:c1], OP.mult))
            st("V", r("rstat"), r("rstat"), lambda e, c0=c0, c1=c1:
               e.tensor_tensor(rstat[:, c0:c1], rstat[:, c0:c1],
                               rstat[:, 16 + c0:16 + c1], OP.subtract))
            st("S", r("rstat"), r("rstat"), lambda e, c0=c0: e.activation(
                rstat[:, 32 + c0:32 + c0 + 4], rstat[:, c0:c0 + 4], AF.Sqrt))
            st("V", r("rstat"), r("rsig"), lambda e, c0=c0: e.reciprocal(
                rsig[:, c0:c0 + 4], rstat[:, 32 + c0:32 + c0 + 4]))
            st("V", r("rsig", "nmr"), r("nmr"), lambda e, c0=c0:
               e.tensor_tensor(nmr[:, c0:c0 + 4], nmr[:, c0:c0 + 4],
                               rsig[:, c0:c0 + 4], OP.mult))
            if part == "stats":
                return
            emit_ln_apply(i, htm, htmR, hfb, hfR, c0)

        def emit_ln_apply(i, htm, htmR, hfb, hfR, c0):
            def h_apply(e, i=i, c0=c0, htm=htm):
                return [e.activation(
                    htm[:, 1024 * s_:1024 * (s_ + 1)],
                    xap(i, 1024 * s_, 1024 * (s_ + 1)),
                    AF.Identity, bias=nmr[:, c0 + s_:c0 + s_ + 1],
                    scale=rsig[:, c0 + s_:c0 + s_ + 1]) for s_ in range(NSUB)]
            st("S", xres(i) + r("rsig", "nmr"), r(htmR), h_apply)

            for s_ in range(NSUB):
                def mm_ht(e, s_=s_, htm=htm):
                    ins = []
                    for cch in range(8):
                        ins.append(e.matmul(
                            PT[:, 128 * cch:128 * (cch + 1)],
                            htm[:, 1024 * s_ + 128 * cch:
                                1024 * s_ + 128 * (cch + 1)],
                            idb.ap(), start=(cch == 0), stop=(cch == 7),
                            is_transpose=True))
                    return ins
                st("T", r(htmR, "idb"), r("pt"), mm_ht)
                st("S", r("pt"), r(hfR), lambda e, s_=s_, hfb=hfb:
                   e.copy(
                       hfb.ap().rearrange("p (u t) -> p u t", u=8)
                       [:, :, 128 * s_:128 * (s_ + 1)],
                       PT.ap().rearrange("p (u t) -> p u t", u=8)))

        def stage_spectral(i, g):
            hfb, hfR = hfof(i)

            def mm_sp(e, g=g, hfb=hfb):
                ins = []
                for (bk, fch) in ((2 * g, g), (2 * g + 1, 2 + g)):
                    for u in range(8):
                        ins.append(e.matmul(
                            PS[:, 512 * bk:512 * (bk + 1)],
                            winT[:, 512 * u + 128 * fch:
                                 512 * u + 128 * (fch + 1)],
                            hfb[:, 512 * u:512 * (u + 1)],
                            start=(u == 0), stop=(u == 7)))
                return ins
            st("T", r("winT", hfR), r("ps%d" % (2 * g), "ps%d" % (2 * g + 1)),
               mm_sp)

        def rot_to_tbl2(g):
            gb_ = 512 * g
            st("V", r("ctbl", "rotc"), r("ctbl2"), lambda e:
               e.tensor_scalar(ctbl2[:, gb_:gb_ + 512], ctbl[:, gb_:gb_ + 512],
                               rotc[:, 0 + g:1 + g], None, OP.mult))
            st("V", r("stbl", "rotc", "ctbl2"), r("ctbl2"),
               lambda e: e.scalar_tensor_tensor(
                   ctbl2[:, gb_:gb_ + 512], stbl[:, gb_:gb_ + 512],
                   rotc[:, 4 + g:5 + g], ctbl2[:, gb_:gb_ + 512],
                   OP.mult, OP.add))
            st("V", r("ctbl", "rotc"), r("stbl2"), lambda e:
               e.tensor_scalar(stbl2[:, gb_:gb_ + 512], ctbl[:, gb_:gb_ + 512],
                               rotc[:, 2 + g:3 + g], None, OP.mult))
            st("V", r("stbl", "rotc", "stbl2"), r("stbl2"),
               lambda e: e.scalar_tensor_tensor(
                   stbl2[:, gb_:gb_ + 512], stbl[:, gb_:gb_ + 512],
                   rotc[:, 0 + g:1 + g], stbl2[:, gb_:gb_ + 512],
                   OP.mult, OP.add))

        def rot_carry(g):
            st("V", r("carry", "rotc"), r("carry2"), lambda e, g=g:
               e.tensor_scalar(carry2[:, g:g + 1], carry[:, g:g + 1],
                               rotc[:, 6 + g:7 + g], None, OP.mult))
            st("V", r("carry", "rotc"), r("carry2"), lambda e, g=g:
               e.tensor_scalar(carry2[:, 2 + g:3 + g], carry[:, g:g + 1],
                               rotc[:, 8 + g:9 + g], None, OP.mult))
            st("V", r("carry", "rotc", "carry2"), r("carry"), lambda e, g=g:
               e.scalar_tensor_tensor(
                   carry[:, g:g + 1], carry[:, 2 + g:3 + g],
                   rotc[:, 10 + g:11 + g], carry2[:, g:g + 1],
                   OP.mult, OP.add))
            st("V", r("carry", "rotc", "carry2"), r("carry"), lambda e, g=g:
               e.scalar_tensor_tensor(
                   carry[:, 2 + g:3 + g], carry[:, 2 + g:3 + g],
                   rotc[:, 6 + g:7 + g], carry2[:, 2 + g:3 + g],
                   OP.mult, OP.add))

        def emit_tblinit(g):
            gb_ = 512 * g
            if True:
                st("V", r("iotac", "omg_col"), r("scrF"),
                   lambda e, g=g, gb_=gb_: e.tensor_scalar(
                       scrF[:, 1536:2048], iotac,
                       omg_col[:, g:g + 1], None, OP.mult))
                st("V", r("scrF"), r("wtmp2"), lambda e, gb_=gb_:
                   e.tensor_scalar(wtmp2, scrF[:, 1536:2048],
                                   1.0 / TWO_PI, None, OP.mult))
                st("V", r("wtmp2"), r("kbuf"), lambda e: e.tensor_copy(
                    kbuf[:, 0:512], wtmp2))
                st("V", r("kbuf", "scrF"), r("wtmp"), lambda e, gb_=gb_:
                   e.scalar_tensor_tensor(
                       wtmp, kbuf[:, 0:512], -TWO_PI,
                       scrF[:, 1536:2048], OP.mult, OP.add))
                st("V", r("wtmp"), r("wtmp2"), lambda e: e.tensor_scalar(
                    wtmp2, wtmp, PI, -TWO_PI, OP.is_gt, OP.mult))
                st("V", r("wtmp", "wtmp2"), r("wtmp"), lambda e:
                   e.tensor_tensor(wtmp, wtmp, wtmp2, OP.add))
                st("V", r("wtmp"), r("wtmp2"), lambda e: e.tensor_scalar(
                    wtmp2, wtmp, -PI, TWO_PI, OP.is_lt, OP.mult))
                st("V", r("wtmp", "wtmp2"), r("scrF"), lambda e, gb_=gb_:
                   e.tensor_tensor(scrF[:, 1536:2048], wtmp,
                                   wtmp2, OP.add))
                st("V", r("scrF"), r("scrF"), lambda e, gb_=gb_:
                   e.tensor_scalar(scrF[:, 1536:2048],
                                   scrF[:, 1536:2048],
                                   -PI, PI, OP.max, OP.min))
                st("S", r("scrF"), r("stbl"), lambda e, gb_=gb_:
                   e.activation(stbl[:, gb_:gb_ + 512],
                                scrF[:, 1536:2048], AF.Sin))
                st("V", r("scrF"), r("wtmp2"), lambda e, gb_=gb_:
                   e.tensor_scalar(wtmp2, scrF[:, 1536:2048],
                                   PI / 2.0, None, OP.add))
                st("V", r("wtmp2"), r("wtmp"), lambda e: e.tensor_scalar(
                    wtmp, wtmp2, PI, -TWO_PI, OP.is_gt, OP.mult))
                st("V", r("wtmp2", "wtmp"), r("wtmp"), lambda e:
                   e.tensor_tensor(wtmp, wtmp2, wtmp, OP.add))
                st("V", r("wtmp"), r("wtmp"), lambda e: e.tensor_scalar(
                    wtmp, wtmp, -PI, PI, OP.max, OP.min))
                st("S", r("wtmp"), r("ctbl"), lambda e, gb_=gb_:
                   e.activation(ctbl[:, gb_:gb_ + 512], wtmp, AF.Sin))
                rot_to_tbl2(g)

        def stage_phase(i, g):
            gb_ = 512 * g
            pr, pi_ = "ps%d" % (2 * g), "ps%d" % (2 * g + 1)
            PRa = PS[:, 512 * 2 * g:512 * (2 * g + 1)]
            PIa = PS[:, 512 * (2 * g + 1):512 * (2 * g + 2)]

            def ev_p(e, PRa=PRa, PIa=PIa, g=g):
                a = e.activation(pbuf[:, 0:512], PRa, AF.Identity,
                                 bias=s1col[:, g:g + 1])
                b = e.activation(pbuf[:, 512:1024], PIa, AF.Identity,
                                 bias=s1col[:, 2 + g:3 + g])
                return [a, b]
            st("S", r(pr, pi_, "s1col"), r("pbuf"), ev_p)
            st("V", r("pbuf", "ctbl"), r("vb16"), lambda e, gb_=gb_:
               e.tensor_tensor(vb16[:, 0:512], ctbl[:, gb_:gb_ + 512],
                               pbuf[:, 0:512], OP.mult))
            st("V", r("pbuf", "stbl"), r("vb16"), lambda e, gb_=gb_:
               e.tensor_tensor(vb16[:, 512:1024], stbl[:, gb_:gb_ + 512],
                               pbuf[:, 512:1024], OP.mult))
            st("V", r("vb16"), r("vbuf"), lambda e: e.tensor_tensor(
                vbuf[:, 0:512], vb16[:, 0:512], vb16[:, 512:1024], OP.add))
            st("V", r("pbuf", "ctbl"), r("vb16"), lambda e, gb_=gb_:
               e.tensor_tensor(vb16[:, 0:512], ctbl[:, gb_:gb_ + 512],
                               pbuf[:, 512:1024], OP.mult))
            st("V", r("pbuf", "stbl"), r("vb16"), lambda e, gb_=gb_:
               e.tensor_tensor(vb16[:, 512:1024], stbl[:, gb_:gb_ + 512],
                               pbuf[:, 0:512], OP.mult))
            st("V", r("vb16"), r("vbuf"), lambda e: e.tensor_tensor(
                vbuf[:, 512:1024], vb16[:, 0:512], vb16[:, 512:1024],
                OP.subtract))

            init_r = 0.0 if i == 0 else carry[:, g:g + 1]
            init_i = 0.0 if i == 0 else carry[:, 2 + g:3 + g]
            st("V", r("vbuf", "rho_col", "carry"), r("zbuf"),
               lambda e, g=g, init_r=init_r: e.tensor_tensor_scan(
                   zbuf[:, 0:512], rho_col[:, g:g + 1].broadcast_to([128, 512]),
                   vbuf[:, 0:512], init_r, OP.mult, OP.add))
            st("V", r("vbuf", "rho_col", "carry"), r("zbuf"),
               lambda e, g=g, init_i=init_i: e.tensor_tensor_scan(
                   zbuf[:, 512:1024],
                   rho_col[:, g:g + 1].broadcast_to([128, 512]),
                   vbuf[:, 512:1024], init_i, OP.mult, OP.add))

            def carry_copy(e, g=g):
                a = e.tensor_copy(carry[:, g:g + 1], zbuf[:, 511:512])
                b = e.tensor_copy(carry[:, 2 + g:3 + g], zbuf[:, 1023:1024])
                return [a, b]
            st("V", r("zbuf"), r("carry"), carry_copy)
            if i + 1 < NSUP:
                rot_carry(g)

            fyb, fyR = (fy, "fy") if i % 2 == 0 else (fy2, "fy2")
            st("V", r("ctbl2", "zbuf"), r("vb16"), lambda e, gb_=gb_:
               e.tensor_tensor(vb16[:, 0:512], ctbl2[:, gb_:gb_ + 512],
                               zbuf[:, 0:512], OP.mult))
            st("V", r("stbl2", "zbuf"), r("vb16"), lambda e, gb_=gb_:
               e.tensor_tensor(vb16[:, 512:1024], stbl2[:, gb_:gb_ + 512],
                               zbuf[:, 512:1024], OP.mult))
            st("V", r("vb16"), r(fyR), lambda e, g=g, fyb=fyb: e.tensor_tensor(
                fyb[:, 512 * g:512 * (g + 1)], vb16[:, 0:512],
                vb16[:, 512:1024], OP.subtract))
            st("V", r("stbl2", "zbuf"), r("vb16"), lambda e, gb_=gb_:
               e.tensor_tensor(vb16[:, 0:512], stbl2[:, gb_:gb_ + 512],
                               zbuf[:, 0:512], OP.mult))
            st("V", r("ctbl2", "zbuf"), r("vb16"), lambda e, gb_=gb_:
               e.tensor_tensor(vb16[:, 512:1024], ctbl2[:, gb_:gb_ + 512],
                               zbuf[:, 512:1024], OP.mult))
            st("V", r("vb16"), r(fyR), lambda e, g=g, fyb=fyb: e.tensor_tensor(
                fyb[:, 512 * (2 + g):512 * (3 + g)], vb16[:, 0:512],
                vb16[:, 512:1024], OP.add))

        def stage_outproj(i):
            fyb, fyR = (fy, "fy") if i % 2 == 0 else (fy2, "fy2")
            banks = [(PO[:, 0:512], "po0"), (PO[:, 512:1024], "po1"),
                     (PF.ap(), "pf")]
            for k, (s_, hh) in enumerate([(s_, hh) for s_ in range(NSUB)
                                          for hh in range(2)]):
                bk, bkR = banks[k % 3]

                def mm_op(e, s_=s_, hh=hh, fyb=fyb, bk=bk):
                    ins = []
                    for g in range(4):
                        ins.append(e.matmul(
                            bk,
                            fyb[:, 512 * g + 128 * s_:
                                512 * g + 128 * (s_ + 1)],
                            woT[:, 1024 * g + 512 * hh:
                                1024 * g + 512 * (hh + 1)],
                            start=(g == 0), stop=(g == 3)))
                    return ins
                st("T", r(fyR, "woT"), r(bkR), mm_op)
                st("V", r(bkR) + xres(i), xres(i),
                   lambda e, i=i, s_=s_, hh=hh, bk=bk: e.tensor_tensor(
                       xap(i, 1024 * s_ + 512 * hh,
                           1024 * s_ + 512 * (hh + 1)),
                       xap(i, 1024 * s_ + 512 * hh,
                           1024 * s_ + 512 * (hh + 1)),
                       bk, OP.add))

        def stage_ffn(i):
            hfb, hfR = hfof(i)
            banks = [(PO[:, 0:512], "po0"), (PO[:, 512:1024], "po1"),
                     (PF.ap(), "pf")]
            for k, (s_, hh) in enumerate([(s_, hh) for s_ in range(NSUB)
                                          for hh in range(2)]):
                bA, bAR = banks[(2 * k) % 3]
                bB, bBR = banks[(2 * k + 1) % 3]
                if k % 2 == 0:
                    e1, e2, e3, e4 = (scrF[:, 0:512], scrF[:, 512:1024],
                                      scrF[:, 1024:1536], scrF[:, 1536:2048])
                    eR = "scrF"
                else:
                    e1, e2, e3, e4 = (scr2[:, 0:512], scr2[:, 512:1024],
                                      scr2[:, 1024:1536], scr2[:, 1536:2048])
                    eR = "scr2"
                    eR2 = "scr2"

                def mm_f(e, s_=s_, hh=hh, hfb=hfb, bA=bA, bB=bB):
                    ins = []
                    for u in range(8):
                        for (bk, wmat) in ((bA, cwt), (bB, swt)):
                            ins.append(e.matmul(
                                bk,
                                hfb[:, 512 * u + 128 * s_:
                                    512 * u + 128 * (s_ + 1)],
                                wmat[:, 1024 * u + 512 * hh:
                                     1024 * u + 512 * (hh + 1)],
                                start=(u == 0), stop=False))
                    for (bk, im) in ((bA, 0), (bB, 1)):
                        ins.append(e.matmul(
                            bk, onesb.ap(),
                            s1crowb[0:1, 1024 * im + 512 * hh:
                                    1024 * im + 512 * (hh + 1)],
                            start=False, stop=True))
                    return ins
                st("T", r(hfR, "cwt", "swt", "onesb", "s1crowb"),
                   r(bAR, bBR), mm_f)

                if k % 2 == 0:
                    def sq_f(e, bA=bA, bB=bB, e1=e1, e2=e2):
                        a = e.activation(e1, bA, AF.Square)
                        b = e.activation(e2, bB, AF.Square)
                        return [a, b]
                    st("S", r(bAR, bBR), r(eR), sq_f)
                    st("G", r(eR), r(eR), lambda e, e1=e1, e2=e2, e3=e3:
                       e.tensor_tensor(e3, e1, e2, OP.add))
                    st("S", r(eR), r(eR), lambda e, e3=e3, e4=e4:
                       e.activation(e4, e3, AF.Sqrt))
                    st("V", r(eR, "sgnbc"), r(eR),
                       lambda e, e3=e3, e4=e4, hh=hh: e.tensor_tensor(
                           e3, e4, sgnbc[:, 512 * hh:512 * (hh + 1)],
                           OP.mult))
                    st("V", r(eR) + xres(i), xres(i),
                       lambda e, i=i, s_=s_, hh=hh, e3=e3: e.tensor_tensor(
                           xap(i, 1024 * s_ + 512 * hh,
                               1024 * s_ + 512 * (hh + 1)),
                           xap(i, 1024 * s_ + 512 * hh,
                               1024 * s_ + 512 * (hh + 1)),
                           e3, OP.add))
                else:
                    def sq_f(e, bA=bA, bB=bB, e1=e1, e2=e2):
                        a = e.activation(e1, bA, AF.Square)
                        b = e.activation(e2, bB, AF.Square)
                        return [a, b]
                    st("S", r(bAR, bBR), r(eR, "scr2a", "scr2b"), sq_f)
                    st("G", r(eR, eR2), r(eR2), lambda e, e1=e1, e2=e2, e3=e3:
                       e.tensor_tensor(e3, e1, e2, OP.add))
                    st("S", r(eR2), r(eR2), lambda e, e3=e3, e4=e4:
                       e.activation(e4, e3, AF.Sqrt))
                    st("V", r(eR2, "sgnbc"), r(eR2),
                       lambda e, e3=e3, e4=e4, hh=hh: e.tensor_tensor(
                           e3, e4, sgnbc[:, 512 * hh:512 * (hh + 1)],
                           OP.mult))
                    st("V", r(eR2) + xres(i), xres(i),
                       lambda e, i=i, s_=s_, hh=hh, e3=e3: e.tensor_tensor(
                           xap(i, 1024 * s_ + 512 * hh,
                               1024 * s_ + 512 * (hh + 1)),
                           xap(i, 1024 * s_ + 512 * hh,
                               1024 * s_ + 512 * (hh + 1)),
                           e3, OP.add))

        def stage_store(i):
            sl = i % 3
            if sl < 2:
                st("Y", xres(i), xres(i),
                   lambda e, i=i, sl=sl: e.dma_start(
                       out_h[512 * i:512 * (i + 1), :]
                       .rearrange("(s p) d -> s p d", p=128)
                       .transpose([1, 0, 2]),
                       xt[:, 4096 * sl:4096 * sl + 4096]
                       .rearrange("p (s d) -> p s d", s=4)),
                   dma=True)
            else:
                st("Y", xres(i), xres(i),
                   lambda e, i=i: e.dma_start(
                       out_h[512 * i:512 * i + 256, :]
                       .rearrange("(s p) d -> s p d", p=128)
                       .transpose([1, 0, 2]),
                       arenaF.ap().rearrange("p (s d) -> p s d", s=2)),
                   dma=True)
                st("Y", xres(i), xres(i),
                   lambda e, i=i: e.dma_start(
                       out_h[512 * i + 256:512 * (i + 1), :]
                       .rearrange("(s p) d -> s p d", p=128)
                       .transpose([1, 0, 2]),
                       poolB.ap().rearrange("p (s d) -> p s d", s=2)),
                   dma=True)

        stage_load(0)
        stage_load(1)
        emit_modhead()
        stage_ln(0, 1)
        emit_tblinit(0)
        emit_tblinit(1)
        stage_ln(1, 1)
        emit_p3head(0, 6)
        stage_load(2)
        emit_p123()
        stage_ln(2, 1, "stats")
        stage_spectral(0, 0)
        stage_spectral(0, 1)
        stage_phase(0, 0)
        stage_phase(0, 1)
        emit_p3head(6, 10)
        for i in range(NSUP):
            stage_outproj(i)
            stage_ln(i, 2, "stats")
            if i + 1 < NSUP:
                stage_spectral(i + 1, 0)
                stage_spectral(i + 1, 1)
            if i == 0:
                emit_p3head(10, 13)
            stage_ln(i, 2, "apply")
            if i + 1 < NSUP:
                stage_phase(i + 1, 0)
                stage_phase(i + 1, 1)
            if i == 0:
                emit_p3head(13, 16)
                for fn in p3_tail:
                    fn()
            stage_ffn(i)
            if i + 2 < NSUP:
                stage_ln(i + 2, 1, "apply")
            stage_store(i)
            if i + 3 < NSUP:
                stage_load(i + 3)
                stage_ln(i + 3, 1, "stats")

        st("Y", r(xslot[0], xslot[1], *XPOOL), [], lambda e: e.nop())

        semctx = contextlib.ExitStack()
        with semctx:
            sems = {}
            for n in sch.sem_names():
                sems[n] = semctx.enter_context(nc.semaphore("sem" + n))
            with nc.Block() as block:
                sch.emit(block, sems)

    return nc


_NC_CACHE = None


def kernel(**inputs):
    global _NC_CACHE
    x = np.ascontiguousarray(inputs["L0_attn_x"], dtype=np.float32)
    w_in = np.ascontiguousarray(inputs["L0_attn_W_in"], dtype=np.float32)
    w_out = np.ascontiguousarray(inputs["L0_attn_W_out"], dtype=np.float32)
    ld = np.ascontiguousarray(inputs["L0_attn_log_decay"], dtype=np.float32)
    fr = np.ascontiguousarray(inputs["L0_attn_frequencies"], dtype=np.float32)
    th = np.ascontiguousarray(inputs["L0_ffn_theta"], dtype=np.float32)
    ph = np.ascontiguousarray(inputs["L0_ffn_phi"], dtype=np.float32)
    thn = np.ascontiguousarray(inputs["L0_ffn_theta_noise"], dtype=np.float32)
    phn = np.ascontiguousarray(inputs["L0_ffn_phi_noise"], dtype=np.float32)
    c = np.ascontiguousarray(inputs["L0_c"], dtype=np.float32)
    wmod = np.ascontiguousarray(inputs["L0_wmod"], dtype=np.float32)

    if _NC_CACHE is None:
        _NC_CACHE = build_nc()
    nc = _NC_CACHE

    in_maps = []
    for b in range(8):
        in_maps.append({
            "x": x[b], "w_in": w_in, "w_out": w_out, "ld": ld, "fr": fr,
            "theta": th, "phi": ph, "thn": thn, "phn": phn,
            "c": c,
            "bsel": np.eye(8, dtype=np.float32)[b],
            "wmod": np.ascontiguousarray(wmod[:, 768 * b:768 * (b + 1)]),
        })
    res = run_bass_kernel_spmd(nc, in_maps, core_ids=list(range(8)))
    out = np.stack([r["out"] for r in res.results], axis=0)
    return out


if __name__ == "__main__":
    nc = build_nc()
    print("instructions:", nc.n_instructions()
          if hasattr(nc, "n_instructions") else "built")


# version 31
# speedup vs baseline: 1.1178x; 1.0558x over previous
import math
import numpy as np

import concourse.bass as bass
import concourse.mybir as mybir
from concourse.bass_utils import run_bass_kernel_spmd

F32 = mybir.dt.float32
BF16 = mybir.dt.bfloat16
OP = mybir.AluOpType
AF = mybir.ActivationFunctionType

D = 1024
S = 2048
NF = 256
F2 = 512
DC = 8
NSUP = 4
TT = 512
NSUB = 4
PI = math.pi
TWO_PI = 2.0 * math.pi
EPS = 1e-5


class Res:
    __slots__ = ("w", "r")

    def __init__(self):
        self.w = None
        self.r = []


class Sched:
    ENGINES = ("T", "V", "S", "G", "Y")
    NSLOT = 8

    def __init__(self):
        self.steps = []
        self.cnt = {e: 0 for e in self.ENGINES}
        self.dma_cnt = {}
        self.dma_idx = {}

    def sem_names(self):
        names = list(self.ENGINES)
        for e in ("Y", "G"):
            names += ["%s%d" % (e, k) for k in range(self.NSLOT)]
        return names

    def step(self, eng, reads, writes, fn, dma=False):
        deps = []
        for r in reads:
            if r.w is not None:
                deps.append(r.w)
        for w in writes:
            if w.w is not None:
                deps.append(w.w)
            deps.extend(w.r)
        if dma:
            idx = self.dma_idx.get(eng, 0)
            self.dma_idx[eng] = idx + 1
            slot = idx % self.NSLOT
            sem_name = "%s%d" % (eng, slot)
            prev = self.dma_cnt.get(sem_name, 0)
            if prev:
                deps.append((sem_name, prev))
            self.dma_cnt[sem_name] = prev + 16
            ev = (sem_name, prev + 16)
            inc = 16
        else:
            self.cnt[eng] += 1
            sem_name = eng
            ev = (eng, self.cnt[eng])
            inc = 1
        deps = sorted(set(deps))
        self.steps.append((eng, deps, fn, sem_name, inc))
        for r in reads:
            r.r.append(ev)
        for w in writes:
            w.w = ev
            w.r = []
        return ev

    def emit(self, block, sems):
        all_names = self.sem_names()

        def make_body(eng_name):
            def body(eng):
                hw = {n: 0 for n in all_names}
                for (e, deps, fn, sem_name, inc) in self.steps:
                    if e != eng_name:
                        continue
                    for (dn, dc) in deps:
                        if dc > hw[dn]:
                            eng.wait_ge(sems[dn], dc)
                            hw[dn] = dc
                    ret = fn(eng)
                    last = ret[-1] if isinstance(ret, (list, tuple)) else ret
                    last.then_inc(sems[sem_name], inc)
            return body
        block.tensor(make_body("T"))
        block.vector(make_body("V"))
        block.scalar(make_body("S"))
        block.gpsimd(make_body("G"))
        block.sync(make_body("Y"))


def build_nc():
    nc = bass.Bass("TRN2", target_bir_lowering=False)

    x_h = nc.dram_tensor("x", [S, D], F32, kind="ExternalInput").ap()
    win_h = nc.dram_tensor("w_in", [F2, D], F32, kind="ExternalInput").ap()
    wout_h = nc.dram_tensor("w_out", [D, F2], F32, kind="ExternalInput").ap()
    ld_h = nc.dram_tensor("ld", [NF], F32, kind="ExternalInput").ap()
    fr_h = nc.dram_tensor("fr", [NF], F32, kind="ExternalInput").ap()
    th_h = nc.dram_tensor("theta", [D, D], F32, kind="ExternalInput").ap()
    ph_h = nc.dram_tensor("phi", [D, D], F32, kind="ExternalInput").ap()
    thn_h = nc.dram_tensor("thn", [D, D], F32, kind="ExternalInput").ap()
    phn_h = nc.dram_tensor("phn", [D, D], F32, kind="ExternalInput").ap()
    c_h = nc.dram_tensor("c", [8, D], F32, kind="ExternalInput").ap()
    wmod_h = nc.dram_tensor("wmod", [D, 768], F32, kind="ExternalInput").ap()
    bsel_h = nc.dram_tensor("bsel", [8], F32, kind="ExternalInput").ap()
    ccin_h = nc.dram_tensor("ccin", [8 * 768], F32).ap()
    ccout_h = nc.dram_tensor("ccout", [8 * 8 * 768], F32).ap()
    out_h = nc.dram_tensor("out", [S, D], F32, kind="ExternalOutput").ap()

    sch = Sched()

    import contextlib
    ctx = contextlib.ExitStack()

    def sb(name, shape, dtype):
        return ctx.enter_context(nc.sbuf_tensor(name, shape, dtype))

    def ps(name, shape, dtype):
        return ctx.enter_context(nc.psum_tensor(name, shape, dtype))

    with ctx:
        xt = sb("xt", [128, 2 * NSUB * D], F32)
        h1tm = sb("h1tm", [128, NSUB * D], BF16)
        hf = sb("hf", [128, DC * TT], BF16)
        fy = sb("fy", [128, 4 * TT], BF16)
        ctbl = sb("ctbl", [128, 1024], BF16)
        stbl = sb("stbl", [128, 1024], BF16)

        poolB = sb("poolB", [128, 2048], F32)
        iotac = poolB[:, 0:512]
        wtmp = poolB[:, 512:1024]
        wtmp2 = poolB[:, 1024:1536]
        kbuf = sb("kbuf", [128, 512], mybir.dt.int32)
        rotc = sb("rotc", [128, 12], F32)
        scrF = sb("scrF", [128, 2048], F32)
        zbuf = sb("zbuf", [128, 1024], BF16)
        vbuf = sb("vbuf", [128, 1024], BF16)
        carry = sb("carry", [128, 4], F32)
        ctbl2 = sb("ctbl2", [128, 1024], BF16)
        stbl2 = sb("stbl2", [128, 1024], BF16)
        pbuf = sb("pbuf", [128, 1024], BF16)
        vb16 = sb("vb16", [128, 1024], BF16)
        scr2 = sb("scr2", [128, 2048], F32)
        fy2 = sb("fy2", [128, 4 * TT], BF16)
        carry2 = sb("carry2", [128, 4], F32)
        winT = sb("winT", [128, DC * F2], BF16)
        woT = sb("woT", [128, 4 * D], BF16)
        cwt = sb("cwt", [128, DC * D], BF16)
        swt = sb("swt", [128, DC * D], BF16)
        arenaF = sb("arenaF", [128, 2048], F32)
        arenaB = sb("arenaB", [128, 6144], BF16)
        grow = sb("grow", [1, 1024], F32)
        sgnbc = sb("sgnbc", [128, 1024], BF16)
        gbbc = sb("gbbc", [128, 1024], F32)
        hf2 = sb("hf2", [128, DC * TT], BF16)
        h1tm2 = sb("h1tm2", [128, NSUB * D], BF16)
        wstageB = h1tm2
        s1crowb = sb("s1crowb", [1, 2048], BF16)
        modc = sb("modc", [128, 48], F32)
        dcol = sb("dcol", [128, 48], F32)
        sab = sb("sab", [128, 8], BF16)
        sbb = sb("sbb", [128, 8], BF16)
        s1col = sb("s1col", [128, 4], F32)
        c_col = sb("c_col", [128, 64], F32)
        silub = sb("silub", [128, 64], BF16)
        rows8 = sb("rows8", [8, 768], F32)
        selbuf = sb("selbuf", [8, 2 * 768], BF16)
        bselc = sb("bselc", [8, 1], BF16)
        ldfr = sb("ldfr", [128, 4], F32)
        rho_col = sb("rho_col", [128, 2], F32)
        omg_col = sb("omg_col", [128, 2], F32)
        sums = sb("sums", [128, 16], F32)
        sumsq = sb("sumsq", [128, 16], F32)
        rsig = sb("rsig", [128, 16], F32)
        nmr = sb("nmr", [128, 16], F32)
        rstat = sb("rstat", [128, 48], F32)
        idf = sb("idf", [128, 128], F32)
        idb = sb("idb", [128, 128], BF16)
        onesf = sb("onesf", [1, 128], F32)
        onesb = sb("onesb", [1, 128], BF16)
        one1 = sb("one1", [1, 1], F32)
        pcol = sb("pcol", [128, 1], F32)
        stage8 = sb("stage8", [68, 128], F32)

        PS = ps("PS", [128, 2048], F32)
        PO = ps("PO", [128, 1024], F32)
        PT = ps("PT", [128, 1024], BF16)
        PF = ps("PF", [128, 512], F32)

        R = {name: Res() for name in [
            "xt0", "xt1", "xt2", "h1tm", "hf", "fy", "ctbl", "stbl", "wtmp",
            "wtmp2", "kbuf", "rotc", "gbbc", "hf2", "h1tm2", "pf", "pt2",
            "arenaF2",
            "iotac",
            "scrF", "zbuf", "vbuf", "carry", "winT", "woT", "poolB",
            "ctbl2", "stbl2", "fy2", "carry2", "ccb", "pbuf", "vb16", "scr2",
            "scr2a", "scr2b", "scrFa", "scrFb",
            "cwt", "swt", "arenaF", "arenaB", "grow", "sgnbc",
            "s1crowb", "modc", "dcol", "sab", "sbb", "s1col", "c_col", "silub",
            "ldfr", "rho_col", "omg_col", "sums", "sumsq", "rsig", "nmr",
            "rstat", "idf", "idb", "onesf", "onesb", "one1", "pcol", "rows8",
            "selbuf", "selbuf2", "bselc",
            "stage8",
            "ps0", "ps1", "ps2", "ps3", "po0", "po1", "pt",
        ]}

        def r(*names):
            return [R[n] for n in names]

        st = sch.step

        st("Y", [], r("stage8"), lambda e: e.dma_start(
            stage8[0:64, :], c_h.rearrange("b (q p) -> (b q) p", p=128)),
           dma=True)
        st("Y", [], r("stage8"), lambda e: e.dma_start(
            stage8[64:66, :], ld_h.rearrange("(q p) -> q p", p=128)), dma=True)
        st("Y", [], r("stage8"), lambda e: e.dma_start(
            stage8[66:68, :], fr_h.rearrange("(q p) -> q p", p=128)), dma=True)

        def _consts(e):
            a = e.memset(onesf.ap(), 1.0)
            b = e.memset(onesb.ap(), 1.0)
            c = e.memset(one1.ap(), 1.0)
            d = e.memset(carry.ap(), 0.0)
            return d
        st("V", [], r("onesf", "onesb", "one1", "carry"), _consts)

        st("G", [], r("pcol"), lambda e: e.iota(
            pcol.ap(), [[1, 1]], channel_multiplier=1, allow_small_or_imprecise_dtypes=True))
        st("G", [], r("iotac"), lambda e: e.iota(
            iotac, [[1, 512]], channel_multiplier=0, allow_small_or_imprecise_dtypes=True))
        st("G", [], r("idf"), lambda e: e.iota(
            idf.ap(), [[1, 128]], channel_multiplier=0, allow_small_or_imprecise_dtypes=True))
        st("V", r("idf", "pcol"), r("idf"), lambda e: e.tensor_scalar(
            idf.ap(), idf.ap(), pcol[:, 0:1], None, OP.is_equal))
        st("V", r("idf"), r("idb"), lambda e: e.tensor_copy(idb.ap(), idf.ap()))

        st("T", r("stage8", "idf"), r("po0"), lambda e: e.transpose(
            PO[:, 0:68], stage8[0:68, :], idf[0:68, 0:68]))
        st("V", r("po0"), r("c_col"), lambda e: e.tensor_copy(
            c_col.ap(), PO[:, 0:64]))
        st("V", r("po0"), r("ldfr"), lambda e: e.tensor_copy(
            ldfr.ap(), PO[:, 64:68]))

        st("S", r("ldfr"), r("rho_col"), lambda e: e.activation(
            rho_col.ap(), ldfr[:, 0:2], AF.Sigmoid))
        st("V", r("ldfr"), r("omg_col"), lambda e: e.tensor_scalar(
            omg_col.ap(), ldfr[:, 2:4], 0.1, None, OP.mult))
        st("V", r("omg_col"), r("rstat"), lambda e: e.tensor_scalar(
            rstat[:, 0:2], omg_col.ap(), 1.0 / TWO_PI, None, OP.mult))
        st("V", r("rstat"), r("kbuf"), lambda e: e.tensor_copy(
            kbuf[:, 0:2], rstat[:, 0:2]))
        st("V", r("kbuf", "omg_col"), r("omg_col"), lambda e:
           e.scalar_tensor_tensor(omg_col.ap(), kbuf[:, 0:2], -TWO_PI,
                                  omg_col.ap(), OP.mult, OP.add))

        st("V", r("omg_col"), r("rstat"), lambda e: e.tensor_scalar(
            rstat[:, 0:2], omg_col.ap(), -PI, PI, OP.max, OP.min))
        st("S", r("rstat"), r("rotc"), lambda e: e.activation(
            rotc[:, 2:4], rstat[:, 0:2], AF.Sin))
        st("V", r("omg_col"), r("rstat"), lambda e: e.tensor_scalar(
            rstat[:, 0:2], omg_col.ap(), PI / 2.0, None, OP.add))
        st("V", r("rstat"), r("rstat"), lambda e: e.tensor_scalar(
            rstat[:, 2:4], rstat[:, 0:2], PI, -TWO_PI, OP.is_gt, OP.mult))
        st("V", r("rstat"), r("rstat"), lambda e: e.tensor_tensor(
            rstat[:, 0:2], rstat[:, 0:2], rstat[:, 2:4], OP.add))
        st("V", r("rstat"), r("rstat"), lambda e: e.tensor_scalar(
            rstat[:, 0:2], rstat[:, 0:2], -PI, PI, OP.max, OP.min))
        st("S", r("rstat"), r("rotc"), lambda e: e.activation(
            rotc[:, 0:2], rstat[:, 0:2], AF.Sin))
        st("V", r("omg_col"), r("rstat"), lambda e: e.tensor_scalar(
            rstat[:, 0:2], omg_col.ap(), 512.0, None, OP.mult))
        st("V", r("rstat"), r("rstat"), lambda e: e.tensor_scalar(
            rstat[:, 2:4], rstat[:, 0:2], 1.0 / TWO_PI, None, OP.mult))
        st("V", r("rstat"), r("kbuf"), lambda e: e.tensor_copy(
            kbuf[:, 0:2], rstat[:, 2:4]))
        st("V", r("kbuf", "rstat"), r("rstat"), lambda e:
           e.scalar_tensor_tensor(rstat[:, 0:2], kbuf[:, 0:2], -TWO_PI,
                                  rstat[:, 0:2], OP.mult, OP.add))
        st("V", r("rstat"), r("rstat"), lambda e: e.tensor_scalar(
            rstat[:, 2:4], rstat[:, 0:2], PI, -TWO_PI, OP.is_gt, OP.mult))
        st("V", r("rstat"), r("rstat"), lambda e: e.tensor_tensor(
            rstat[:, 0:2], rstat[:, 0:2], rstat[:, 2:4], OP.add))
        st("V", r("rstat"), r("rstat"), lambda e: e.tensor_scalar(
            rstat[:, 2:4], rstat[:, 0:2], -PI, TWO_PI, OP.is_lt, OP.mult))
        st("V", r("rstat"), r("rstat"), lambda e: e.tensor_tensor(
            rstat[:, 0:2], rstat[:, 0:2], rstat[:, 2:4], OP.add))
        st("V", r("rstat"), r("rstat"), lambda e: e.tensor_scalar(
            rstat[:, 0:2], rstat[:, 0:2], -PI, PI, OP.max, OP.min))
        st("S", r("rstat"), r("rotc"), lambda e: e.activation(
            rotc[:, 8:10], rstat[:, 0:2], AF.Sin))
        st("V", r("rstat"), r("rstat"), lambda e: e.tensor_scalar(
            rstat[:, 4:6], rstat[:, 0:2], PI / 2.0, None, OP.add))
        st("V", r("rstat"), r("rstat"), lambda e: e.tensor_scalar(
            rstat[:, 2:4], rstat[:, 4:6], PI, -TWO_PI, OP.is_gt, OP.mult))
        st("V", r("rstat"), r("rstat"), lambda e: e.tensor_tensor(
            rstat[:, 4:6], rstat[:, 4:6], rstat[:, 2:4], OP.add))
        st("V", r("rstat"), r("rstat"), lambda e: e.tensor_scalar(
            rstat[:, 4:6], rstat[:, 4:6], -PI, PI, OP.max, OP.min))
        st("S", r("rstat"), r("rotc"), lambda e: e.activation(
            rotc[:, 6:8], rstat[:, 4:6], AF.Sin))
        st("V", r("rotc"), r("rotc"), lambda e: e.tensor_scalar(
            rotc[:, 4:6], rotc[:, 2:4], -1.0, None, OP.mult))
        st("V", r("rotc"), r("rotc"), lambda e: e.tensor_scalar(
            rotc[:, 10:12], rotc[:, 8:10], -1.0, None, OP.mult))


        st("S", r("c_col"), r("silub"), lambda e: e.activation(
            silub.ap(), c_col.ap(), AF.Silu))

        p3_tail = []
        p3_emitters = []

        def emit_modhead():
            st("G", [], r("arenaB"), lambda e: e.dma_start(
                arenaB[:, 0:6144].rearrange("p (c q) -> p c q", c=8),
                wmod_h.rearrange("(c p) q -> c p q", p=128).transpose([1, 0, 2])),
               dma=True)
            st("G", [], r("h1tm2"), lambda e: e.dma_start(
                wstageB.ap().rearrange("p (c d) -> p c d", c=4),
                win_h.rearrange("(c p) d -> c p d", p=128).transpose([1, 0, 2])),
               dma=True)

            def mm_matvec(e):
                ins = []
                silv = silub.ap().rearrange("p (b k) -> p k b", k=8)
                for k in range(8):
                    ins.append(e.matmul(
                        PS[0:8, 0:512], silv[:, k, :],
                        arenaB[:, 768 * k:768 * k + 512],
                        start=(k == 0), stop=(k == 7)))
                for k in range(8):
                    ins.append(e.matmul(
                        PS[0:8, 512:768], silv[:, k, :],
                        arenaB[:, 768 * k + 512:768 * (k + 1)],
                        start=(k == 0), stop=(k == 7)))
                return ins
            st("T", r("silub", "arenaB"), r("ps0", "ps1"), mm_matvec)
            st("S", r("ps0", "ps1"), r("rows8"), lambda e: e.copy(
                rows8.ap(), PS[0:8, 0:768]))
            st("G", r("rows8"), r("ccb"), lambda e: e.dma_start(
                ccin_h.rearrange("(b q) -> b q", q=768), rows8.ap()), dma=True)
            st("G", [], r("bselc"), lambda e: e.dma_start(
                bselc.ap(), bsel_h.rearrange("(b q) -> b q", q=1)),
               dma=True)
            for c4 in range(4):
                def mm_t(e, c4=c4):
                    ins = []
                    for u in range(8):
                        ins.append(e.matmul(
                            PT[:, 128 * u:128 * (u + 1)],
                            wstageB[:, 1024 * c4 + 128 * u:
                                    1024 * c4 + 128 * (u + 1)],
                            idb.ap(), start=(u == 0), stop=(u == 7),
                            is_transpose=True))
                    return ins
                st("T", r("h1tm2", "idb"), r("pt"), mm_t)
                st("V", r("pt"), r("winT"), lambda e, c4=c4: e.tensor_copy(
                    winT.ap().rearrange("p (u f) -> p u f", u=8)
                    [:, :, 128 * c4:128 * (c4 + 1)],
                    PT.ap().rearrange("p (u f) -> p u f", u=8)))

            st("G", r("ccb"), r("ccb"), lambda e: e.collective_compute(
                "AllGather", OP.bypass, replica_groups=[list(range(8))],
                ins=[ccin_h.opt()], outs=[ccout_h.opt()]))

        def emit_p123():
            for k_ in range(8):
                sbase = 768 * (k_ % 2)
                sR = "selbuf" if k_ % 2 == 0 else "selbuf2"
                st("G", r("ccb"), r(sR), lambda e, k_=k_, sbase=sbase:
                   e.dma_start(
                       selbuf[:, sbase:sbase + 768],
                       ccout_h[6144 * k_:6144 * (k_ + 1)]
                       .rearrange("(b q) -> b q", b=8)), dma=True)

                def mm_sel(e, k_=k_, sbase=sbase):
                    ins = []
                    for m in range(6):
                        q = 6 * k_ + m
                        ins.append(e.matmul(
                            PO[:, q:q + 1],
                            selbuf[:, sbase + 128 * m:sbase + 128 * (m + 1)],
                            bselc.ap(),
                            start=(q == 0), stop=(q == 47)))
                    return ins
                st("T", r(sR, "bselc"), r("po0"), mm_sel)
            st("V", r("po0"), r("modc"), lambda e: e.tensor_copy(
                modc.ap(), PO[:, 0:48]))
            st("V", r("modc"), r("dcol"), lambda e: e.tensor_scalar(
                dcol[:, 0:8], modc[:, 8:16], 1.0, None, OP.add))
            st("V", r("modc"), r("dcol"), lambda e: e.tensor_scalar(
                dcol[:, 8:16], modc[:, 32:40], 1.0, None, OP.add))
            st("V", r("modc"), r("sab"), lambda e: e.tensor_copy(
                sab.ap(), modc[:, 0:8]))
            st("V", r("modc"), r("sbb"), lambda e: e.tensor_copy(
                sbb.ap(), modc[:, 24:32]))

            def clamp_cols(src0, dst0, sgn_dst):
                st("V", r("modc"), r("rstat"), lambda e: e.tensor_scalar(
                    rstat[:, 0:8], modc[:, src0:src0 + 8], 0.0, 2.0,
                    OP.is_ge, OP.mult))
                st("V", r("rstat"), r("rstat"), lambda e: e.tensor_scalar(
                    rstat[:, 0:8], rstat[:, 0:8], -1.0, None, OP.add))
                st("S", r("modc"), r("rstat"), lambda e: e.activation(
                    rstat[:, 8:16], modc[:, src0:src0 + 8], AF.Abs))
                st("V", r("rstat"), r("rstat"), lambda e: e.tensor_scalar(
                    rstat[:, 8:16], rstat[:, 8:16], 0.25, None, OP.max))
                st("V", r("rstat"), r("dcol"), lambda e: e.tensor_tensor(
                    dcol[:, dst0:dst0 + 8], rstat[:, 0:8], rstat[:, 8:16],
                    OP.mult))
                if sgn_dst is not None:
                    st("V", r("rstat"), r("dcol"), lambda e: e.tensor_copy(
                        dcol[:, sgn_dst:sgn_dst + 8], rstat[:, 0:8]))
            clamp_cols(16, 16, None)
            clamp_cols(40, 24, 32)

            def bcast(cols_off, dst_name, dst_ap_fn):
                def mm_t(e):
                    ins = []
                    for cch in range(8):
                        ins.append(e.transpose(
                            PO[0:1, 128 * cch:128 * (cch + 1)],
                            dcol[:, cols_off + cch:cols_off + cch + 1],
                            idf.ap()))
                    return ins
                st("T", r("dcol", "idf"), r("po0", "po1"), mm_t)
                st("V", r("po0", "po1"), r("grow"), lambda e: e.tensor_copy(
                    grow.ap(), PO[0:1, 0:1024]))

                def mm_b(e):
                    ins = []
                    for hh in range(2):
                        ins.append(e.matmul(
                            PS[:, 512 * hh:512 * (hh + 1)], onesf.ap(),
                            grow[0:1, 512 * hh:512 * (hh + 1)],
                            start=True, stop=True))
                    return ins
                st("T", r("grow", "onesf"), r("ps0", "ps1"), mm_b)
                st("V", r("ps0", "ps1"), r(dst_name), lambda e: e.tensor_copy(
                    dst_ap_fn(), PS[:, 0:1024]))
            bcast(16, "scrF", lambda: scrF[:, 0:1024])
            bcast(24, "gbbc", lambda: gbbc.ap())
            bcast(32, "sgnbc", lambda: sgnbc.ap())
            def mm_s1(e):
                ins = []
                for u in range(8):
                    for g in range(4):
                        ins.append(e.matmul(
                            s1psum[:, g:g + 1],
                            winT[:, 512 * u + 128 * g:512 * u + 128 * (g + 1)],
                            sab[:, u:u + 1],
                            start=(u == 0 and g == 0), stop=(u == 7 and g == 3)))
                return ins
            s1psum = PO
            st("T", r("winT", "sab"), r("po0"), mm_s1)
            st("V", r("po0"), r("s1col"), lambda e: e.tensor_copy(
                s1col.ap(), PO[:, 0:4]))

            def sc_win(e):
                ins = []
                for u in range(8):
                    ins.append(e.tensor_scalar(
                        winT[:, 512 * u:512 * (u + 1)],
                        winT[:, 512 * u:512 * (u + 1)],
                        dcol[:, u:u + 1], None, OP.mult))
                return ins
            st("V", r("dcol"), r("winT"), sc_win)

            st("G", [], r("arenaB"), lambda e: e.dma_start(
                arenaB[:, 0:4096].rearrange("p (c f) -> p c f", c=8),
                wout_h.rearrange("(c p) f -> c p f", p=128).transpose([1, 0, 2])),
               dma=True)
            for g in range(4):
                def mm_t(e, g=g):
                    ins = []
                    for c8 in range(8):
                        ins.append(e.matmul(
                            PT[:, 128 * c8:128 * (c8 + 1)],
                            arenaB[:, 512 * c8 + 128 * g:512 * c8 + 128 * (g + 1)],
                            idb.ap(), start=(c8 == 0), stop=(c8 == 7),
                            is_transpose=True))
                    return ins
                st("T", r("arenaB", "idb"), r("pt"), mm_t)
                st("V", r("pt"), r("woT"), lambda e, g=g: e.tensor_copy(
                    woT[:, 1024 * g:1024 * (g + 1)], PT.ap()))

            def sc_wo(e):
                ins = []
                for g in range(4):
                    ins.append(e.tensor_tensor(
                        woT[:, 1024 * g:1024 * (g + 1)],
                        woT[:, 1024 * g:1024 * (g + 1)],
                        scrF[:, 0:1024], OP.mult))
                return ins
            st("V", r("scrF"), r("woT"), sc_wo)
            st("Y", [], r("scrF", "scrFa", "scrFb"), lambda e: e.nop())


        def emit_p3head(lo=0, hi=16):
            pairA = (arenaF[:, 0:1024], ("arenaF",),
                     arenaF[:, 1024:2048], ("arenaF2",))
            pairC = (scr2[:, 0:1024], ("scr2a",),
                     scr2[:, 1024:2048], ("scr2b",))
            pairD = (scrF[:, 0:1024], ("scrFa",),
                     scrF[:, 1024:2048], ("scrFb",))
            for (mi, (mat_h, noise_h, dst, dstR, shift)) in enumerate((
                    (th_h, thn_h, cwt, "cwt", PI / 2.0),
                    (ph_h, phn_h, swt, "swt", 0.0))):
                for q in range(8):
                    qq = 8 * mi + q
                    if not (lo <= qq < hi):
                        continue
                    if qq < 6:
                        slot, aR, nslot, nR = pairA if qq % 2 == 0 else pairC
                    else:
                        slot, aR, nslot, nR = pairC if qq % 2 == 0 else pairD
                    st("Y", [], r(*aR), lambda e, mat_h=mat_h, q=q, slot=slot:
                       e.dma_start(slot, mat_h[128 * q:128 * (q + 1), :]),
                       dma=True)
                    st("Y", [], r(*nR), lambda e, noise_h=noise_h, q=q,
                       nslot=nslot:
                       e.dma_start(nslot, noise_h[128 * q:128 * (q + 1), :]),
                       dma=True)
                    st("V", r(*aR, *nR), r(*aR),
                       lambda e, shift=shift, slot=slot, nslot=nslot:
                       e.scalar_tensor_tensor(slot, nslot, shift, slot,
                                              OP.add, OP.add))
                    st("V", r(*aR), r(*aR), lambda e, slot=slot:
                       e.tensor_scalar(slot, slot, -PI, PI,
                                       OP.max, OP.min))
                    st("S", r(*aR), r(dstR), lambda e, dst=dst, q=q, slot=slot:
                       e.activation(dst[:, 1024 * q:1024 * (q + 1)],
                                    slot, AF.Sin))
                def sc_gb(e, dst=dst):
                    ins = []
                    for u in range(8):
                        ins.append(e.tensor_tensor(
                            dst[:, 1024 * u:1024 * (u + 1)],
                            dst[:, 1024 * u:1024 * (u + 1)],
                            gbbc.ap(), OP.mult))
                    return ins

                im = 0 if dst is cwt else 1

                def p3_tail_fn(dst=dst, dstR=dstR, im=im, sc_gb=sc_gb):
                    st("V", r("gbbc"), r(dstR), sc_gb)

                    def mm_s1c(e, dst=dst):
                        ins = []
                        for u in range(8):
                            for hh in range(2):
                                ins.append(e.matmul(
                                    PO[0:1, 512 * hh:512 * (hh + 1)],
                                    sbb[:, u:u + 1],
                                    dst[:, 1024 * u + 512 * hh:
                                        1024 * u + 512 * (hh + 1)],
                                    start=(u == 0), stop=(u == 7)))
                        return ins
                    st("T", r(dstR, "sbb"), r("po0", "po1"), mm_s1c)
                    st("V", r("po0", "po1"), r("s1crowb"),
                       lambda e, im=im: e.tensor_copy(
                           s1crowb[0:1, 1024 * im:1024 * (im + 1)],
                           PO[0:1, 0:1024]))

                    def sc_cb(e, dst=dst):
                        ins = []
                        for u in range(8):
                            ins.append(e.tensor_scalar(
                                dst[:, 1024 * u:1024 * (u + 1)],
                                dst[:, 1024 * u:1024 * (u + 1)],
                                dcol[:, 8 + u:9 + u], None, OP.mult))
                        return ins
                    st("V", r("dcol"), r(dstR), sc_cb)
                if hi == 16:
                    p3_tail.append(p3_tail_fn)

        xslot = ["xt0", "xt1", "xt2"]
        XPOOL = ["arenaF", "arenaF2", "poolB", "iotac", "wtmp", "wtmp2"]

        def xap(i, lo, hi):
            sl = i % 3
            if sl < 2:
                base = 4096 * sl
                return xt[:, base + lo:base + hi]
            if hi <= 2048:
                return arenaF[:, lo:hi]
            assert lo >= 2048
            return poolB[:, lo - 2048:hi - 2048]

        def xres(i):
            sl = i % 3
            if sl < 2:
                return [R[xslot[sl]]]
            return [R["xt2"]] + [R[n] for n in XPOOL]

        def hfof(i):
            return (hf, "hf") if i % 2 == 0 else (hf2, "hf2")

        def h1of(i):
            return (h1tm, "h1tm") if i % 2 == 0 else (h1tm2, "h1tm2")

        def ptof(k):
            return (PT, "pt")

        ptctr = [0]

        def stage_load(i):
            sl = i % 3
            if sl < 2:
                st("Y", [], xres(i), lambda e, i=i, sl=sl: e.dma_start(
                    xt[:, 4096 * sl:4096 * sl + 4096]
                    .rearrange("p (s d) -> p s d", s=4),
                    x_h[512 * i:512 * (i + 1), :]
                    .rearrange("(s p) d -> s p d", p=128).transpose([1, 0, 2])),
                   dma=True)
            else:
                st("Y", [], xres(i), lambda e, i=i: e.dma_start(
                    arenaF.ap().rearrange("p (s d) -> p s d", s=2),
                    x_h[512 * i:512 * i + 256, :]
                    .rearrange("(s p) d -> s p d", p=128).transpose([1, 0, 2])),
                   dma=True)
                st("Y", [], xres(i), lambda e, i=i: e.dma_start(
                    poolB.ap().rearrange("p (s d) -> p s d", s=2),
                    x_h[512 * i + 256:512 * (i + 1), :]
                    .rearrange("(s p) d -> s p d", p=128).transpose([1, 0, 2])),
                   dma=True)

        def stage_ln(i, which, part="all"):
            htm, htmR = h1of(i)
            hfb, hfR = hfof(i)
            c0 = 4 * i
            if part == "apply":
                emit_ln_apply(i, htm, htmR, hfb, hfR, c0)
                return

            def stats_g(e, i=i, c0=c0, htm=htm):
                return [e.tensor_scalar(
                    htm[:, 1024 * s_:1024 * (s_ + 1)],
                    xap(i, 1024 * s_, 1024 * (s_ + 1)),
                    1.0, None, OP.mult, OP.add,
                    accum_out=sums[:, c0 + s_:c0 + s_ + 1])
                    for s_ in range(NSUB)]
            st("V", xres(i), r("sums", htmR), stats_g)

            def stats_s(e, i=i, c0=c0):
                return [e.activation(
                    arenaB[:, 1024 * s_:1024 * (s_ + 1)],
                    xap(i, 1024 * s_, 1024 * (s_ + 1)),
                    AF.Square, accum_out=sumsq[:, c0 + s_:c0 + s_ + 1])
                    for s_ in range(NSUB)]
            st("S", xres(i), r("sumsq", "arenaB"), stats_s)

            c1 = c0 + 4
            st("V", r("sumsq"), r("rstat"), lambda e, c0=c0, c1=c1:
               e.tensor_scalar(rstat[:, c0:c1], sumsq[:, c0:c1],
                               1.0 / D, EPS, OP.mult, OP.add))
            st("V", r("sums"), r("nmr"), lambda e, c0=c0, c1=c1:
               e.tensor_scalar(nmr[:, c0:c1], sums[:, c0:c1],
                               -1.0 / D, None, OP.mult))
            st("V", r("nmr"), r("rstat"), lambda e, c0=c0, c1=c1:
               e.tensor_tensor(rstat[:, 16 + c0:16 + c1], nmr[:, c0:c1],
                               nmr[:, c0:c1], OP.mult))
            st("V", r("rstat"), r("rstat"), lambda e, c0=c0, c1=c1:
               e.tensor_tensor(rstat[:, c0:c1], rstat[:, c0:c1],
                               rstat[:, 16 + c0:16 + c1], OP.subtract))
            st("S", r("rstat"), r("rstat"), lambda e, c0=c0: e.activation(
                rstat[:, 32 + c0:32 + c0 + 4], rstat[:, c0:c0 + 4], AF.Sqrt))
            st("V", r("rstat"), r("rsig"), lambda e, c0=c0: e.reciprocal(
                rsig[:, c0:c0 + 4], rstat[:, 32 + c0:32 + c0 + 4]))
            st("V", r("rsig", "nmr"), r("nmr"), lambda e, c0=c0:
               e.tensor_tensor(nmr[:, c0:c0 + 4], nmr[:, c0:c0 + 4],
                               rsig[:, c0:c0 + 4], OP.mult))
            if part == "stats":
                return
            emit_ln_apply(i, htm, htmR, hfb, hfR, c0)

        def emit_ln_apply(i, htm, htmR, hfb, hfR, c0):
            def h_apply(e, i=i, c0=c0, htm=htm):
                return [e.activation(
                    htm[:, 1024 * s_:1024 * (s_ + 1)],
                    xap(i, 1024 * s_, 1024 * (s_ + 1)),
                    AF.Identity, bias=nmr[:, c0 + s_:c0 + s_ + 1],
                    scale=rsig[:, c0 + s_:c0 + s_ + 1]) for s_ in range(NSUB)]
            st("S", xres(i) + r("rsig", "nmr"), r(htmR), h_apply)

            for s_ in range(NSUB):
                def mm_ht(e, s_=s_, htm=htm):
                    ins = []
                    for cch in range(8):
                        ins.append(e.matmul(
                            PT[:, 128 * cch:128 * (cch + 1)],
                            htm[:, 1024 * s_ + 128 * cch:
                                1024 * s_ + 128 * (cch + 1)],
                            idb.ap(), start=(cch == 0), stop=(cch == 7),
                            is_transpose=True))
                    return ins
                st("T", r(htmR, "idb"), r("pt"), mm_ht)
                st("S", r("pt"), r(hfR), lambda e, s_=s_, hfb=hfb:
                   e.copy(
                       hfb.ap().rearrange("p (u t) -> p u t", u=8)
                       [:, :, 128 * s_:128 * (s_ + 1)],
                       PT.ap().rearrange("p (u t) -> p u t", u=8)))

        def stage_spectral(i, g):
            hfb, hfR = hfof(i)

            def mm_sp(e, g=g, hfb=hfb):
                ins = []
                for (bk, fch) in ((2 * g, g), (2 * g + 1, 2 + g)):
                    for u in range(8):
                        ins.append(e.matmul(
                            PS[:, 512 * bk:512 * (bk + 1)],
                            winT[:, 512 * u + 128 * fch:
                                 512 * u + 128 * (fch + 1)],
                            hfb[:, 512 * u:512 * (u + 1)],
                            start=(u == 0), stop=(u == 7)))
                return ins
            st("T", r("winT", hfR), r("ps%d" % (2 * g), "ps%d" % (2 * g + 1)),
               mm_sp)

        def rot_to_tbl2(g):
            gb_ = 512 * g
            st("V", r("ctbl", "rotc"), r("ctbl2"), lambda e:
               e.tensor_scalar(ctbl2[:, gb_:gb_ + 512], ctbl[:, gb_:gb_ + 512],
                               rotc[:, 0 + g:1 + g], None, OP.mult))
            st("V", r("stbl", "rotc", "ctbl2"), r("ctbl2"),
               lambda e: e.scalar_tensor_tensor(
                   ctbl2[:, gb_:gb_ + 512], stbl[:, gb_:gb_ + 512],
                   rotc[:, 4 + g:5 + g], ctbl2[:, gb_:gb_ + 512],
                   OP.mult, OP.add))
            st("V", r("ctbl", "rotc"), r("stbl2"), lambda e:
               e.tensor_scalar(stbl2[:, gb_:gb_ + 512], ctbl[:, gb_:gb_ + 512],
                               rotc[:, 2 + g:3 + g], None, OP.mult))
            st("V", r("stbl", "rotc", "stbl2"), r("stbl2"),
               lambda e: e.scalar_tensor_tensor(
                   stbl2[:, gb_:gb_ + 512], stbl[:, gb_:gb_ + 512],
                   rotc[:, 0 + g:1 + g], stbl2[:, gb_:gb_ + 512],
                   OP.mult, OP.add))

        def rot_carry(g):
            st("V", r("carry", "rotc"), r("carry2"), lambda e, g=g:
               e.tensor_scalar(carry2[:, g:g + 1], carry[:, g:g + 1],
                               rotc[:, 6 + g:7 + g], None, OP.mult))
            st("V", r("carry", "rotc"), r("carry2"), lambda e, g=g:
               e.tensor_scalar(carry2[:, 2 + g:3 + g], carry[:, g:g + 1],
                               rotc[:, 8 + g:9 + g], None, OP.mult))
            st("V", r("carry", "rotc", "carry2"), r("carry"), lambda e, g=g:
               e.scalar_tensor_tensor(
                   carry[:, g:g + 1], carry[:, 2 + g:3 + g],
                   rotc[:, 10 + g:11 + g], carry2[:, g:g + 1],
                   OP.mult, OP.add))
            st("V", r("carry", "rotc", "carry2"), r("carry"), lambda e, g=g:
               e.scalar_tensor_tensor(
                   carry[:, 2 + g:3 + g], carry[:, 2 + g:3 + g],
                   rotc[:, 6 + g:7 + g], carry2[:, 2 + g:3 + g],
                   OP.mult, OP.add))

        def emit_tblinit(g):
            gb_ = 512 * g
            if True:
                st("V", r("iotac", "omg_col"), r("scrF"),
                   lambda e, g=g, gb_=gb_: e.tensor_scalar(
                       scrF[:, 1536:2048], iotac,
                       omg_col[:, g:g + 1], None, OP.mult))
                st("V", r("scrF"), r("wtmp2"), lambda e, gb_=gb_:
                   e.tensor_scalar(wtmp2, scrF[:, 1536:2048],
                                   1.0 / TWO_PI, None, OP.mult))
                st("V", r("wtmp2"), r("kbuf"), lambda e: e.tensor_copy(
                    kbuf[:, 0:512], wtmp2))
                st("V", r("kbuf", "scrF"), r("wtmp"), lambda e, gb_=gb_:
                   e.scalar_tensor_tensor(
                       wtmp, kbuf[:, 0:512], -TWO_PI,
                       scrF[:, 1536:2048], OP.mult, OP.add))
                st("V", r("wtmp"), r("wtmp2"), lambda e: e.tensor_scalar(
                    wtmp2, wtmp, PI, -TWO_PI, OP.is_gt, OP.mult))
                st("V", r("wtmp", "wtmp2"), r("wtmp"), lambda e:
                   e.tensor_tensor(wtmp, wtmp, wtmp2, OP.add))
                st("V", r("wtmp"), r("wtmp2"), lambda e: e.tensor_scalar(
                    wtmp2, wtmp, -PI, TWO_PI, OP.is_lt, OP.mult))
                st("V", r("wtmp", "wtmp2"), r("scrF"), lambda e, gb_=gb_:
                   e.tensor_tensor(scrF[:, 1536:2048], wtmp,
                                   wtmp2, OP.add))
                st("V", r("scrF"), r("scrF"), lambda e, gb_=gb_:
                   e.tensor_scalar(scrF[:, 1536:2048],
                                   scrF[:, 1536:2048],
                                   -PI, PI, OP.max, OP.min))
                st("S", r("scrF"), r("stbl"), lambda e, gb_=gb_:
                   e.activation(stbl[:, gb_:gb_ + 512],
                                scrF[:, 1536:2048], AF.Sin))
                st("V", r("scrF"), r("wtmp2"), lambda e, gb_=gb_:
                   e.tensor_scalar(wtmp2, scrF[:, 1536:2048],
                                   PI / 2.0, None, OP.add))
                st("V", r("wtmp2"), r("wtmp"), lambda e: e.tensor_scalar(
                    wtmp, wtmp2, PI, -TWO_PI, OP.is_gt, OP.mult))
                st("V", r("wtmp2", "wtmp"), r("wtmp"), lambda e:
                   e.tensor_tensor(wtmp, wtmp2, wtmp, OP.add))
                st("V", r("wtmp"), r("wtmp"), lambda e: e.tensor_scalar(
                    wtmp, wtmp, -PI, PI, OP.max, OP.min))
                st("S", r("wtmp"), r("ctbl"), lambda e, gb_=gb_:
                   e.activation(ctbl[:, gb_:gb_ + 512], wtmp, AF.Sin))
                rot_to_tbl2(g)

        def stage_phase(i, g):
            gb_ = 512 * g
            pr, pi_ = "ps%d" % (2 * g), "ps%d" % (2 * g + 1)
            PRa = PS[:, 512 * 2 * g:512 * (2 * g + 1)]
            PIa = PS[:, 512 * (2 * g + 1):512 * (2 * g + 2)]

            def ev_p(e, PRa=PRa, PIa=PIa, g=g):
                a = e.activation(pbuf[:, 0:512], PRa, AF.Identity,
                                 bias=s1col[:, g:g + 1])
                b = e.activation(pbuf[:, 512:1024], PIa, AF.Identity,
                                 bias=s1col[:, 2 + g:3 + g])
                return [a, b]
            st("S", r(pr, pi_, "s1col"), r("pbuf"), ev_p)
            st("V", r("pbuf", "ctbl"), r("vb16"), lambda e, gb_=gb_:
               e.tensor_tensor(vb16[:, 0:512], ctbl[:, gb_:gb_ + 512],
                               pbuf[:, 0:512], OP.mult))
            st("V", r("pbuf", "stbl"), r("vb16"), lambda e, gb_=gb_:
               e.tensor_tensor(vb16[:, 512:1024], stbl[:, gb_:gb_ + 512],
                               pbuf[:, 512:1024], OP.mult))
            st("V", r("vb16"), r("vbuf"), lambda e: e.tensor_tensor(
                vbuf[:, 0:512], vb16[:, 0:512], vb16[:, 512:1024], OP.add))
            st("V", r("pbuf", "ctbl"), r("vb16"), lambda e, gb_=gb_:
               e.tensor_tensor(vb16[:, 0:512], ctbl[:, gb_:gb_ + 512],
                               pbuf[:, 512:1024], OP.mult))
            st("V", r("pbuf", "stbl"), r("vb16"), lambda e, gb_=gb_:
               e.tensor_tensor(vb16[:, 512:1024], stbl[:, gb_:gb_ + 512],
                               pbuf[:, 0:512], OP.mult))
            st("V", r("vb16"), r("vbuf"), lambda e: e.tensor_tensor(
                vbuf[:, 512:1024], vb16[:, 0:512], vb16[:, 512:1024],
                OP.subtract))

            init_r = 0.0 if i == 0 else carry[:, g:g + 1]
            init_i = 0.0 if i == 0 else carry[:, 2 + g:3 + g]
            st("V", r("vbuf", "rho_col", "carry"), r("zbuf"),
               lambda e, g=g, init_r=init_r: e.tensor_tensor_scan(
                   zbuf[:, 0:512], rho_col[:, g:g + 1].broadcast_to([128, 512]),
                   vbuf[:, 0:512], init_r, OP.mult, OP.add))
            st("V", r("vbuf", "rho_col", "carry"), r("zbuf"),
               lambda e, g=g, init_i=init_i: e.tensor_tensor_scan(
                   zbuf[:, 512:1024],
                   rho_col[:, g:g + 1].broadcast_to([128, 512]),
                   vbuf[:, 512:1024], init_i, OP.mult, OP.add))

            def carry_copy(e, g=g):
                a = e.tensor_copy(carry[:, g:g + 1], zbuf[:, 511:512])
                b = e.tensor_copy(carry[:, 2 + g:3 + g], zbuf[:, 1023:1024])
                return [a, b]
            st("V", r("zbuf"), r("carry"), carry_copy)
            if i + 1 < NSUP:
                rot_carry(g)

            fyb, fyR = (fy, "fy") if i % 2 == 0 else (fy2, "fy2")
            st("V", r("ctbl2", "zbuf"), r("vb16"), lambda e, gb_=gb_:
               e.tensor_tensor(vb16[:, 0:512], ctbl2[:, gb_:gb_ + 512],
                               zbuf[:, 0:512], OP.mult))
            st("V", r("stbl2", "zbuf"), r("vb16"), lambda e, gb_=gb_:
               e.tensor_tensor(vb16[:, 512:1024], stbl2[:, gb_:gb_ + 512],
                               zbuf[:, 512:1024], OP.mult))
            st("V", r("vb16"), r(fyR), lambda e, g=g, fyb=fyb: e.tensor_tensor(
                fyb[:, 512 * g:512 * (g + 1)], vb16[:, 0:512],
                vb16[:, 512:1024], OP.subtract))
            st("V", r("stbl2", "zbuf"), r("vb16"), lambda e, gb_=gb_:
               e.tensor_tensor(vb16[:, 0:512], stbl2[:, gb_:gb_ + 512],
                               zbuf[:, 0:512], OP.mult))
            st("V", r("ctbl2", "zbuf"), r("vb16"), lambda e, gb_=gb_:
               e.tensor_tensor(vb16[:, 512:1024], ctbl2[:, gb_:gb_ + 512],
                               zbuf[:, 512:1024], OP.mult))
            st("V", r("vb16"), r(fyR), lambda e, g=g, fyb=fyb: e.tensor_tensor(
                fyb[:, 512 * (2 + g):512 * (3 + g)], vb16[:, 0:512],
                vb16[:, 512:1024], OP.add))

        def stage_outproj(i):
            fyb, fyR = (fy, "fy") if i % 2 == 0 else (fy2, "fy2")
            banks = [(PO[:, 0:512], "po0"), (PO[:, 512:1024], "po1"),
                     (PF.ap(), "pf")]
            for k, (s_, hh) in enumerate([(s_, hh) for s_ in range(NSUB)
                                          for hh in range(2)]):
                bk, bkR = banks[k % 3]

                def mm_op(e, s_=s_, hh=hh, fyb=fyb, bk=bk):
                    ins = []
                    for g in range(4):
                        ins.append(e.matmul(
                            bk,
                            fyb[:, 512 * g + 128 * s_:
                                512 * g + 128 * (s_ + 1)],
                            woT[:, 1024 * g + 512 * hh:
                                1024 * g + 512 * (hh + 1)],
                            start=(g == 0), stop=(g == 3)))
                    return ins
                st("T", r(fyR, "woT"), r(bkR), mm_op)
                st("V", r(bkR) + xres(i), xres(i),
                   lambda e, i=i, s_=s_, hh=hh, bk=bk: e.tensor_tensor(
                       xap(i, 1024 * s_ + 512 * hh,
                           1024 * s_ + 512 * (hh + 1)),
                       xap(i, 1024 * s_ + 512 * hh,
                           1024 * s_ + 512 * (hh + 1)),
                       bk, OP.add))

        def stage_ffn(i):
            hfb, hfR = hfof(i)
            banks = [(PO[:, 0:512], "po0"), (PO[:, 512:1024], "po1"),
                     (PF.ap(), "pf")]
            for k, (s_, hh) in enumerate([(s_, hh) for s_ in range(NSUB)
                                          for hh in range(2)]):
                bA, bAR = banks[(2 * k) % 3]
                bB, bBR = banks[(2 * k + 1) % 3]
                if k % 2 == 0:
                    e1, e2, e3, e4 = (scrF[:, 0:512], scrF[:, 512:1024],
                                      scrF[:, 1024:1536], scrF[:, 1536:2048])
                    eR = "scrF"
                else:
                    e1, e2, e3, e4 = (scr2[:, 0:512], scr2[:, 512:1024],
                                      scr2[:, 1024:1536], scr2[:, 1536:2048])
                    eR = "scr2"
                    eR2 = "scr2"

                def mm_f(e, s_=s_, hh=hh, hfb=hfb, bA=bA, bB=bB):
                    ins = []
                    for u in range(8):
                        for (bk, wmat) in ((bA, cwt), (bB, swt)):
                            ins.append(e.matmul(
                                bk,
                                hfb[:, 512 * u + 128 * s_:
                                    512 * u + 128 * (s_ + 1)],
                                wmat[:, 1024 * u + 512 * hh:
                                     1024 * u + 512 * (hh + 1)],
                                start=(u == 0), stop=False))
                    for (bk, im) in ((bA, 0), (bB, 1)):
                        ins.append(e.matmul(
                            bk, onesb.ap(),
                            s1crowb[0:1, 1024 * im + 512 * hh:
                                    1024 * im + 512 * (hh + 1)],
                            start=False, stop=True))
                    return ins
                st("T", r(hfR, "cwt", "swt", "onesb", "s1crowb"),
                   r(bAR, bBR), mm_f)

                if k % 2 == 0:
                    def sq_f(e, bA=bA, bB=bB, e1=e1, e2=e2):
                        a = e.activation(e1, bA, AF.Square)
                        b = e.activation(e2, bB, AF.Square)
                        return [a, b]
                    st("S", r(bAR, bBR), r(eR, "scrFa", "scrFb"), sq_f)
                    st("G", r(eR), r(eR), lambda e, e1=e1, e2=e2, e3=e3:
                       e.tensor_tensor(e3, e1, e2, OP.add))
                    st("S", r(eR), r(eR), lambda e, e3=e3, e4=e4:
                       e.activation(e4, e3, AF.Sqrt))
                    st("V", r(eR, "sgnbc"), r(eR),
                       lambda e, e3=e3, e4=e4, hh=hh: e.tensor_tensor(
                           e3, e4, sgnbc[:, 512 * hh:512 * (hh + 1)],
                           OP.mult))
                    st("V", r(eR) + xres(i), xres(i),
                       lambda e, i=i, s_=s_, hh=hh, e3=e3: e.tensor_tensor(
                           xap(i, 1024 * s_ + 512 * hh,
                               1024 * s_ + 512 * (hh + 1)),
                           xap(i, 1024 * s_ + 512 * hh,
                               1024 * s_ + 512 * (hh + 1)),
                           e3, OP.add))
                else:
                    def sq_f(e, bA=bA, bB=bB, e1=e1, e2=e2):
                        a = e.activation(e1, bA, AF.Square)
                        b = e.activation(e2, bB, AF.Square)
                        return [a, b]
                    st("S", r(bAR, bBR), r(eR, "scr2a", "scr2b"), sq_f)
                    st("G", r(eR, eR2), r(eR2), lambda e, e1=e1, e2=e2, e3=e3:
                       e.tensor_tensor(e3, e1, e2, OP.add))
                    st("S", r(eR2), r(eR2), lambda e, e3=e3, e4=e4:
                       e.activation(e4, e3, AF.Sqrt))
                    st("V", r(eR2, "sgnbc"), r(eR2),
                       lambda e, e3=e3, e4=e4, hh=hh: e.tensor_tensor(
                           e3, e4, sgnbc[:, 512 * hh:512 * (hh + 1)],
                           OP.mult))
                    st("V", r(eR2) + xres(i), xres(i),
                       lambda e, i=i, s_=s_, hh=hh, e3=e3: e.tensor_tensor(
                           xap(i, 1024 * s_ + 512 * hh,
                               1024 * s_ + 512 * (hh + 1)),
                           xap(i, 1024 * s_ + 512 * hh,
                               1024 * s_ + 512 * (hh + 1)),
                           e3, OP.add))

        def stage_store(i):
            sl = i % 3
            if sl < 2:
                st("Y", xres(i), xres(i),
                   lambda e, i=i, sl=sl: e.dma_start(
                       out_h[512 * i:512 * (i + 1), :]
                       .rearrange("(s p) d -> s p d", p=128)
                       .transpose([1, 0, 2]),
                       xt[:, 4096 * sl:4096 * sl + 4096]
                       .rearrange("p (s d) -> p s d", s=4)),
                   dma=True)
            else:
                st("Y", xres(i), xres(i),
                   lambda e, i=i: e.dma_start(
                       out_h[512 * i:512 * i + 256, :]
                       .rearrange("(s p) d -> s p d", p=128)
                       .transpose([1, 0, 2]),
                       arenaF.ap().rearrange("p (s d) -> p s d", s=2)),
                   dma=True)
                st("Y", xres(i), xres(i),
                   lambda e, i=i: e.dma_start(
                       out_h[512 * i + 256:512 * (i + 1), :]
                       .rearrange("(s p) d -> s p d", p=128)
                       .transpose([1, 0, 2]),
                       poolB.ap().rearrange("p (s d) -> p s d", s=2)),
                   dma=True)

        stage_load(0)
        stage_load(1)
        emit_modhead()
        stage_ln(0, 1)
        emit_tblinit(0)
        emit_tblinit(1)
        stage_ln(1, 1)
        emit_p3head(0, 6)
        stage_load(2)
        emit_p123()
        stage_spectral(0, 0)
        stage_spectral(0, 1)
        stage_phase(0, 0)
        stage_phase(0, 1)
        stage_ln(2, 1, "stats")
        emit_p3head(6, 10)
        for i in range(NSUP):
            stage_outproj(i)
            stage_ln(i, 2, "stats")
            if i + 1 < NSUP:
                stage_spectral(i + 1, 0)
                stage_spectral(i + 1, 1)
            if i == 0:
                emit_p3head(10, 13)
            stage_ln(i, 2, "apply")
            if i + 1 < NSUP:
                stage_phase(i + 1, 0)
                stage_phase(i + 1, 1)
            if i == 0:
                emit_p3head(13, 16)
                for fn in p3_tail:
                    fn()
            stage_ffn(i)
            if i + 2 < NSUP:
                stage_ln(i + 2, 1, "apply")
            stage_store(i)
            if i + 3 < NSUP:
                stage_load(i + 3)
                stage_ln(i + 3, 1, "stats")

        st("Y", r(xslot[0], xslot[1], *XPOOL), [], lambda e: e.nop())

        semctx = contextlib.ExitStack()
        with semctx:
            sems = {}
            for n in sch.sem_names():
                sems[n] = semctx.enter_context(nc.semaphore("sem" + n))
            with nc.Block() as block:
                sch.emit(block, sems)

    return nc


_NC_CACHE = None


def kernel(**inputs):
    global _NC_CACHE
    x = np.ascontiguousarray(inputs["L0_attn_x"], dtype=np.float32)
    w_in = np.ascontiguousarray(inputs["L0_attn_W_in"], dtype=np.float32)
    w_out = np.ascontiguousarray(inputs["L0_attn_W_out"], dtype=np.float32)
    ld = np.ascontiguousarray(inputs["L0_attn_log_decay"], dtype=np.float32)
    fr = np.ascontiguousarray(inputs["L0_attn_frequencies"], dtype=np.float32)
    th = np.ascontiguousarray(inputs["L0_ffn_theta"], dtype=np.float32)
    ph = np.ascontiguousarray(inputs["L0_ffn_phi"], dtype=np.float32)
    thn = np.ascontiguousarray(inputs["L0_ffn_theta_noise"], dtype=np.float32)
    phn = np.ascontiguousarray(inputs["L0_ffn_phi_noise"], dtype=np.float32)
    c = np.ascontiguousarray(inputs["L0_c"], dtype=np.float32)
    wmod = np.ascontiguousarray(inputs["L0_wmod"], dtype=np.float32)

    if _NC_CACHE is None:
        _NC_CACHE = build_nc()
    nc = _NC_CACHE

    in_maps = []
    for b in range(8):
        in_maps.append({
            "x": x[b], "w_in": w_in, "w_out": w_out, "ld": ld, "fr": fr,
            "theta": th, "phi": ph, "thn": thn, "phn": phn,
            "c": c,
            "bsel": np.eye(8, dtype=np.float32)[b],
            "wmod": np.ascontiguousarray(wmod[:, 768 * b:768 * (b + 1)]),
        })
    res = run_bass_kernel_spmd(nc, in_maps, core_ids=list(range(8)))
    out = np.stack([r["out"] for r in res.results], axis=0)
    return out


if __name__ == "__main__":
    nc = build_nc()
    print("instructions:", nc.n_instructions()
          if hasattr(nc, "n_instructions") else "built")


# version 32
# speedup vs baseline: 1.1764x; 1.0524x over previous
import math
import numpy as np

import concourse.bass as bass
import concourse.mybir as mybir
from concourse.bass_utils import run_bass_kernel_spmd

F32 = mybir.dt.float32
BF16 = mybir.dt.bfloat16
OP = mybir.AluOpType
AF = mybir.ActivationFunctionType

D = 1024
S = 2048
NF = 256
F2 = 512
DC = 8
NSUP = 4
TT = 512
NSUB = 4
PI = math.pi
TWO_PI = 2.0 * math.pi
EPS = 1e-5


class Res:
    __slots__ = ("w", "r")

    def __init__(self):
        self.w = None
        self.r = []


class Sched:
    ENGINES = ("T", "V", "S", "G", "Y")
    NSLOT = 8

    def __init__(self):
        self.steps = []
        self.cnt = {e: 0 for e in self.ENGINES}
        self.dma_cnt = {}
        self.dma_idx = {}

    def sem_names(self):
        names = list(self.ENGINES)
        for e in ("Y", "G"):
            names += ["%s%d" % (e, k) for k in range(self.NSLOT)]
        return names

    def step(self, eng, reads, writes, fn, dma=False):
        deps = []
        for r in reads:
            if r.w is not None:
                deps.append(r.w)
        for w in writes:
            if w.w is not None:
                deps.append(w.w)
            deps.extend(w.r)
        if dma:
            idx = self.dma_idx.get(eng, 0)
            self.dma_idx[eng] = idx + 1
            slot = idx % self.NSLOT
            sem_name = "%s%d" % (eng, slot)
            prev = self.dma_cnt.get(sem_name, 0)
            if prev:
                deps.append((sem_name, prev))
            self.dma_cnt[sem_name] = prev + 16
            ev = (sem_name, prev + 16)
            inc = 16
        else:
            self.cnt[eng] += 1
            sem_name = eng
            ev = (eng, self.cnt[eng])
            inc = 1
        deps = sorted(set(deps))
        self.steps.append((eng, deps, fn, sem_name, inc))
        for r in reads:
            r.r.append(ev)
        for w in writes:
            w.w = ev
            w.r = []
        return ev

    def emit(self, block, sems):
        all_names = self.sem_names()

        def make_body(eng_name):
            def body(eng):
                hw = {n: 0 for n in all_names}
                for (e, deps, fn, sem_name, inc) in self.steps:
                    if e != eng_name:
                        continue
                    for (dn, dc) in deps:
                        if dc > hw[dn]:
                            eng.wait_ge(sems[dn], dc)
                            hw[dn] = dc
                    ret = fn(eng)
                    last = ret[-1] if isinstance(ret, (list, tuple)) else ret
                    last.then_inc(sems[sem_name], inc)
            return body
        block.tensor(make_body("T"))
        block.vector(make_body("V"))
        block.scalar(make_body("S"))
        block.gpsimd(make_body("G"))
        block.sync(make_body("Y"))


def build_nc():
    nc = bass.Bass("TRN2", target_bir_lowering=False)

    x_h = nc.dram_tensor("x", [S, D], F32, kind="ExternalInput").ap()
    win_h = nc.dram_tensor("w_in", [F2, D], F32, kind="ExternalInput").ap()
    wout_h = nc.dram_tensor("w_out", [D, F2], F32, kind="ExternalInput").ap()
    ld_h = nc.dram_tensor("ld", [NF], F32, kind="ExternalInput").ap()
    fr_h = nc.dram_tensor("fr", [NF], F32, kind="ExternalInput").ap()
    th_h = nc.dram_tensor("theta", [D, D], F32, kind="ExternalInput").ap()
    ph_h = nc.dram_tensor("phi", [D, D], F32, kind="ExternalInput").ap()
    thn_h = nc.dram_tensor("thn", [D, D], F32, kind="ExternalInput").ap()
    phn_h = nc.dram_tensor("phn", [D, D], F32, kind="ExternalInput").ap()
    c_h = nc.dram_tensor("c", [8, D], F32, kind="ExternalInput").ap()
    wmod_h = nc.dram_tensor("wmod", [D, 768], F32, kind="ExternalInput").ap()
    bsel_h = nc.dram_tensor("bsel", [8], F32, kind="ExternalInput").ap()
    ccin_h = nc.dram_tensor("ccin", [8 * 768], F32).ap()
    ccout_h = nc.dram_tensor("ccout", [8 * 8 * 768], F32).ap()
    out_h = nc.dram_tensor("out", [S, D], F32, kind="ExternalOutput").ap()

    sch = Sched()

    import contextlib
    ctx = contextlib.ExitStack()

    def sb(name, shape, dtype):
        return ctx.enter_context(nc.sbuf_tensor(name, shape, dtype))

    def ps(name, shape, dtype):
        return ctx.enter_context(nc.psum_tensor(name, shape, dtype))

    with ctx:
        xt = sb("xt", [128, 2 * NSUB * D], F32)
        h1tm = sb("h1tm", [128, NSUB * D], BF16)
        hf = sb("hf", [128, DC * TT], BF16)
        fy = sb("fy", [128, 4 * TT], BF16)
        ctbl = sb("ctbl", [128, 1024], BF16)
        stbl = sb("stbl", [128, 1024], BF16)

        poolB = sb("poolB", [128, 2048], F32)
        iotac = poolB[:, 0:512]
        wtmp = poolB[:, 512:1024]
        wtmp2 = poolB[:, 1024:1536]
        kbuf = sb("kbuf", [128, 512], mybir.dt.int32)
        rotc = sb("rotc", [128, 12], F32)
        scrF = sb("scrF", [128, 2048], F32)
        zbuf = sb("zbuf", [128, 1024], BF16)
        vbuf = sb("vbuf", [128, 1024], BF16)
        carry = sb("carry", [128, 4], F32)
        ctbl2 = sb("ctbl2", [128, 1024], BF16)
        stbl2 = sb("stbl2", [128, 1024], BF16)
        pbuf = sb("pbuf", [128, 1024], BF16)
        vb16 = sb("vb16", [128, 1024], BF16)
        scr2 = sb("scr2", [128, 2048], F32)
        fy2 = sb("fy2", [128, 4 * TT], BF16)
        carry2 = sb("carry2", [128, 4], F32)
        winT = sb("winT", [128, DC * F2], BF16)
        woT = sb("woT", [128, 4 * D], BF16)
        cwt = sb("cwt", [128, DC * D], BF16)
        swt = sb("swt", [128, DC * D], BF16)
        arenaF = sb("arenaF", [128, 2048], F32)
        arenaB = sb("arenaB", [128, 6144], BF16)
        grow = sb("grow", [1, 1024], F32)
        sgnbc = sb("sgnbc", [128, 1024], BF16)
        gbbc = sb("gbbc", [128, 1024], F32)
        hf2 = sb("hf2", [128, DC * TT], BF16)
        h1tm2 = sb("h1tm2", [128, NSUB * D], BF16)
        wstageB = h1tm2
        s1crowb = sb("s1crowb", [1, 2048], BF16)
        modc = sb("modc", [128, 48], F32)
        dcol = sb("dcol", [128, 48], F32)
        sab = sb("sab", [128, 8], BF16)
        sbb = sb("sbb", [128, 8], BF16)
        s1col = sb("s1col", [128, 4], F32)
        c_col = sb("c_col", [128, 64], F32)
        silub = sb("silub", [128, 64], BF16)
        rows8 = sb("rows8", [8, 768], F32)
        selbuf = sb("selbuf", [8, 2 * 768], BF16)
        bselc = sb("bselc", [8, 1], BF16)
        ldfr = sb("ldfr", [128, 4], F32)
        rho_col = sb("rho_col", [128, 2], F32)
        omg_col = sb("omg_col", [128, 2], F32)
        sums = sb("sums", [128, 16], F32)
        sumsq = sb("sumsq", [128, 16], F32)
        rsig = sb("rsig", [128, 16], F32)
        nmr = sb("nmr", [128, 16], F32)
        rstat = sb("rstat", [128, 48], F32)
        idf = sb("idf", [128, 128], F32)
        idb = sb("idb", [128, 128], BF16)
        onesf = sb("onesf", [1, 128], F32)
        onesb = sb("onesb", [1, 128], BF16)
        one1 = sb("one1", [1, 1], F32)
        pcol = sb("pcol", [128, 1], F32)
        stage8 = sb("stage8", [68, 128], F32)

        PS = ps("PS", [128, 2048], F32)
        PO = ps("PO", [128, 1024], F32)
        PT = ps("PT", [128, 1024], BF16)
        PF = ps("PF", [128, 512], F32)

        R = {name: Res() for name in [
            "xt0", "xt1", "xt2", "h1tm", "hf", "fy", "ctbl", "stbl", "wtmp",
            "wtmp2", "kbuf", "rotc", "gbbc", "hf2", "h1tm2", "pf", "pt2",
            "arenaF2",
            "iotac",
            "scrF", "zbuf", "vbuf", "carry", "winT", "woT", "poolB",
            "ctbl2", "stbl2", "fy2", "carry2", "ccb", "pbuf", "vb16", "scr2",
            "scr2a", "scr2b", "scrFa", "scrFb",
            "cwt", "swt", "arenaF", "arenaB", "grow", "sgnbc",
            "s1crowb", "modc", "dcol", "sab", "sbb", "s1col", "c_col", "silub",
            "ldfr", "rho_col", "omg_col", "sums", "sumsq", "rsig", "nmr",
            "rstat", "idf", "idb", "onesf", "onesb", "one1", "pcol", "rows8",
            "selbuf", "selbuf2", "bselc",
            "stage8",
            "ps0", "ps1", "ps2", "ps3", "po0", "po1", "pt",
        ]}

        def r(*names):
            return [R[n] for n in names]

        st = sch.step

        st("Y", [], r("stage8"), lambda e: e.dma_start(
            stage8[0:64, :], c_h.rearrange("b (q p) -> (b q) p", p=128)),
           dma=True)
        st("Y", [], r("stage8"), lambda e: e.dma_start(
            stage8[64:66, :], ld_h.rearrange("(q p) -> q p", p=128)), dma=True)
        st("Y", [], r("stage8"), lambda e: e.dma_start(
            stage8[66:68, :], fr_h.rearrange("(q p) -> q p", p=128)), dma=True)

        def _consts(e):
            a = e.memset(onesf.ap(), 1.0)
            b = e.memset(onesb.ap(), 1.0)
            c = e.memset(one1.ap(), 1.0)
            d = e.memset(carry.ap(), 0.0)
            return d
        st("V", [], r("onesf", "onesb", "one1", "carry"), _consts)

        st("G", [], r("pcol"), lambda e: e.iota(
            pcol.ap(), [[1, 1]], channel_multiplier=1, allow_small_or_imprecise_dtypes=True))
        st("G", [], r("iotac"), lambda e: e.iota(
            iotac, [[1, 512]], channel_multiplier=0, allow_small_or_imprecise_dtypes=True))
        st("G", [], r("idf"), lambda e: e.iota(
            idf.ap(), [[1, 128]], channel_multiplier=0, allow_small_or_imprecise_dtypes=True))
        st("V", r("idf", "pcol"), r("idf"), lambda e: e.tensor_scalar(
            idf.ap(), idf.ap(), pcol[:, 0:1], None, OP.is_equal))
        st("V", r("idf"), r("idb"), lambda e: e.tensor_copy(idb.ap(), idf.ap()))

        st("T", r("stage8", "idf"), r("po0"), lambda e: e.transpose(
            PO[:, 0:68], stage8[0:68, :], idf[0:68, 0:68]))
        st("V", r("po0"), r("c_col"), lambda e: e.tensor_copy(
            c_col.ap(), PO[:, 0:64]))
        st("V", r("po0"), r("ldfr"), lambda e: e.tensor_copy(
            ldfr.ap(), PO[:, 64:68]))

        st("S", r("ldfr"), r("rho_col"), lambda e: e.activation(
            rho_col.ap(), ldfr[:, 0:2], AF.Sigmoid))
        st("V", r("ldfr"), r("omg_col"), lambda e: e.tensor_scalar(
            omg_col.ap(), ldfr[:, 2:4], 0.1, None, OP.mult))
        st("V", r("omg_col"), r("rstat"), lambda e: e.tensor_scalar(
            rstat[:, 0:2], omg_col.ap(), 1.0 / TWO_PI, None, OP.mult))
        st("V", r("rstat"), r("kbuf"), lambda e: e.tensor_copy(
            kbuf[:, 0:2], rstat[:, 0:2]))
        st("V", r("kbuf", "omg_col"), r("omg_col"), lambda e:
           e.scalar_tensor_tensor(omg_col.ap(), kbuf[:, 0:2], -TWO_PI,
                                  omg_col.ap(), OP.mult, OP.add))

        st("V", r("omg_col"), r("rstat"), lambda e: e.tensor_scalar(
            rstat[:, 0:2], omg_col.ap(), -PI, PI, OP.max, OP.min))
        st("S", r("rstat"), r("rotc"), lambda e: e.activation(
            rotc[:, 2:4], rstat[:, 0:2], AF.Sin))
        st("V", r("omg_col"), r("rstat"), lambda e: e.tensor_scalar(
            rstat[:, 0:2], omg_col.ap(), PI / 2.0, None, OP.add))
        st("V", r("rstat"), r("rstat"), lambda e: e.tensor_scalar(
            rstat[:, 2:4], rstat[:, 0:2], PI, -TWO_PI, OP.is_gt, OP.mult))
        st("V", r("rstat"), r("rstat"), lambda e: e.tensor_tensor(
            rstat[:, 0:2], rstat[:, 0:2], rstat[:, 2:4], OP.add))
        st("V", r("rstat"), r("rstat"), lambda e: e.tensor_scalar(
            rstat[:, 0:2], rstat[:, 0:2], -PI, PI, OP.max, OP.min))
        st("S", r("rstat"), r("rotc"), lambda e: e.activation(
            rotc[:, 0:2], rstat[:, 0:2], AF.Sin))
        st("V", r("omg_col"), r("rstat"), lambda e: e.tensor_scalar(
            rstat[:, 0:2], omg_col.ap(), 512.0, None, OP.mult))
        st("V", r("rstat"), r("rstat"), lambda e: e.tensor_scalar(
            rstat[:, 2:4], rstat[:, 0:2], 1.0 / TWO_PI, None, OP.mult))
        st("V", r("rstat"), r("kbuf"), lambda e: e.tensor_copy(
            kbuf[:, 0:2], rstat[:, 2:4]))
        st("V", r("kbuf", "rstat"), r("rstat"), lambda e:
           e.scalar_tensor_tensor(rstat[:, 0:2], kbuf[:, 0:2], -TWO_PI,
                                  rstat[:, 0:2], OP.mult, OP.add))
        st("V", r("rstat"), r("rstat"), lambda e: e.tensor_scalar(
            rstat[:, 2:4], rstat[:, 0:2], PI, -TWO_PI, OP.is_gt, OP.mult))
        st("V", r("rstat"), r("rstat"), lambda e: e.tensor_tensor(
            rstat[:, 0:2], rstat[:, 0:2], rstat[:, 2:4], OP.add))
        st("V", r("rstat"), r("rstat"), lambda e: e.tensor_scalar(
            rstat[:, 2:4], rstat[:, 0:2], -PI, TWO_PI, OP.is_lt, OP.mult))
        st("V", r("rstat"), r("rstat"), lambda e: e.tensor_tensor(
            rstat[:, 0:2], rstat[:, 0:2], rstat[:, 2:4], OP.add))
        st("V", r("rstat"), r("rstat"), lambda e: e.tensor_scalar(
            rstat[:, 0:2], rstat[:, 0:2], -PI, PI, OP.max, OP.min))
        st("S", r("rstat"), r("rotc"), lambda e: e.activation(
            rotc[:, 8:10], rstat[:, 0:2], AF.Sin))
        st("V", r("rstat"), r("rstat"), lambda e: e.tensor_scalar(
            rstat[:, 4:6], rstat[:, 0:2], PI / 2.0, None, OP.add))
        st("V", r("rstat"), r("rstat"), lambda e: e.tensor_scalar(
            rstat[:, 2:4], rstat[:, 4:6], PI, -TWO_PI, OP.is_gt, OP.mult))
        st("V", r("rstat"), r("rstat"), lambda e: e.tensor_tensor(
            rstat[:, 4:6], rstat[:, 4:6], rstat[:, 2:4], OP.add))
        st("V", r("rstat"), r("rstat"), lambda e: e.tensor_scalar(
            rstat[:, 4:6], rstat[:, 4:6], -PI, PI, OP.max, OP.min))
        st("S", r("rstat"), r("rotc"), lambda e: e.activation(
            rotc[:, 6:8], rstat[:, 4:6], AF.Sin))
        st("V", r("rotc"), r("rotc"), lambda e: e.tensor_scalar(
            rotc[:, 4:6], rotc[:, 2:4], -1.0, None, OP.mult))
        st("V", r("rotc"), r("rotc"), lambda e: e.tensor_scalar(
            rotc[:, 10:12], rotc[:, 8:10], -1.0, None, OP.mult))


        st("S", r("c_col"), r("silub"), lambda e: e.activation(
            silub.ap(), c_col.ap(), AF.Silu))

        p3_tail = []
        p3_emitters = []

        def emit_modhead():
            st("G", [], r("arenaB"), lambda e: e.dma_start(
                arenaB[:, 0:6144].rearrange("p (c q) -> p c q", c=8),
                wmod_h.rearrange("(c p) q -> c p q", p=128).transpose([1, 0, 2])),
               dma=True)
            st("G", [], r("h1tm2"), lambda e: e.dma_start(
                wstageB.ap().rearrange("p (c d) -> p c d", c=4),
                win_h.rearrange("(c p) d -> c p d", p=128).transpose([1, 0, 2])),
               dma=True)

            def mm_matvec(e):
                ins = []
                silv = silub.ap().rearrange("p (b k) -> p k b", k=8)
                for k in range(8):
                    ins.append(e.matmul(
                        PS[0:8, 0:512], silv[:, k, :],
                        arenaB[:, 768 * k:768 * k + 512],
                        start=(k == 0), stop=(k == 7)))
                for k in range(8):
                    ins.append(e.matmul(
                        PS[0:8, 512:768], silv[:, k, :],
                        arenaB[:, 768 * k + 512:768 * (k + 1)],
                        start=(k == 0), stop=(k == 7)))
                return ins
            st("T", r("silub", "arenaB"), r("ps0", "ps1"), mm_matvec)
            st("S", r("ps0", "ps1"), r("rows8"), lambda e: e.copy(
                rows8.ap(), PS[0:8, 0:768]))
            st("G", r("rows8"), r("ccb"), lambda e: e.dma_start(
                ccin_h.rearrange("(b q) -> b q", q=768), rows8.ap()), dma=True)
            st("G", [], r("bselc"), lambda e: e.dma_start(
                bselc.ap(), bsel_h.rearrange("(b q) -> b q", q=1)),
               dma=True)
            for c4 in range(4):
                def mm_t(e, c4=c4):
                    ins = []
                    for u in range(8):
                        ins.append(e.matmul(
                            PT[:, 128 * u:128 * (u + 1)],
                            wstageB[:, 1024 * c4 + 128 * u:
                                    1024 * c4 + 128 * (u + 1)],
                            idb.ap(), start=(u == 0), stop=(u == 7),
                            is_transpose=True))
                    return ins
                st("T", r("h1tm2", "idb"), r("pt"), mm_t)
                st("V", r("pt"), r("winT"), lambda e, c4=c4: e.tensor_copy(
                    winT.ap().rearrange("p (u f) -> p u f", u=8)
                    [:, :, 128 * c4:128 * (c4 + 1)],
                    PT.ap().rearrange("p (u f) -> p u f", u=8)))

            st("G", r("ccb"), r("ccb"), lambda e: e.collective_compute(
                "AllGather", OP.bypass, replica_groups=[list(range(8))],
                ins=[ccin_h.opt()], outs=[ccout_h.opt()]))

        def emit_p123():
            for k_ in range(8):
                sbase = 768 * (k_ % 2)
                sR = "selbuf" if k_ % 2 == 0 else "selbuf2"
                st("G", r("ccb"), r(sR), lambda e, k_=k_, sbase=sbase:
                   e.dma_start(
                       selbuf[:, sbase:sbase + 768],
                       ccout_h[6144 * k_:6144 * (k_ + 1)]
                       .rearrange("(b q) -> b q", b=8)), dma=True)

                def mm_sel(e, k_=k_, sbase=sbase):
                    ins = []
                    for m in range(6):
                        q = 6 * k_ + m
                        ins.append(e.matmul(
                            PO[:, q:q + 1],
                            selbuf[:, sbase + 128 * m:sbase + 128 * (m + 1)],
                            bselc.ap(),
                            start=(q == 0), stop=(q == 47)))
                    return ins
                st("T", r(sR, "bselc"), r("po0"), mm_sel)
            st("V", r("po0"), r("modc"), lambda e: e.tensor_copy(
                modc.ap(), PO[:, 0:48]))
            st("V", r("modc"), r("dcol"), lambda e: e.tensor_scalar(
                dcol[:, 0:8], modc[:, 8:16], 1.0, None, OP.add))
            st("V", r("modc"), r("dcol"), lambda e: e.tensor_scalar(
                dcol[:, 8:16], modc[:, 32:40], 1.0, None, OP.add))
            st("V", r("modc"), r("sab"), lambda e: e.tensor_copy(
                sab.ap(), modc[:, 0:8]))
            st("V", r("modc"), r("sbb"), lambda e: e.tensor_copy(
                sbb.ap(), modc[:, 24:32]))

            def clamp_cols(src0, dst0, sgn_dst):
                st("V", r("modc"), r("rstat"), lambda e: e.tensor_scalar(
                    rstat[:, 0:8], modc[:, src0:src0 + 8], 0.0, 2.0,
                    OP.is_ge, OP.mult))
                st("V", r("rstat"), r("rstat"), lambda e: e.tensor_scalar(
                    rstat[:, 0:8], rstat[:, 0:8], -1.0, None, OP.add))
                st("S", r("modc"), r("rstat"), lambda e: e.activation(
                    rstat[:, 8:16], modc[:, src0:src0 + 8], AF.Abs))
                st("V", r("rstat"), r("rstat"), lambda e: e.tensor_scalar(
                    rstat[:, 8:16], rstat[:, 8:16], 0.25, None, OP.max))
                st("V", r("rstat"), r("dcol"), lambda e: e.tensor_tensor(
                    dcol[:, dst0:dst0 + 8], rstat[:, 0:8], rstat[:, 8:16],
                    OP.mult))
                if sgn_dst is not None:
                    st("V", r("rstat"), r("dcol"), lambda e: e.tensor_copy(
                        dcol[:, sgn_dst:sgn_dst + 8], rstat[:, 0:8]))
            clamp_cols(16, 16, None)
            clamp_cols(40, 24, None)

            def bcast(cols_off, dst_name, dst_ap_fn):
                def mm_t(e):
                    ins = []
                    for cch in range(8):
                        ins.append(e.transpose(
                            PO[0:1, 128 * cch:128 * (cch + 1)],
                            dcol[:, cols_off + cch:cols_off + cch + 1],
                            idf.ap()))
                    return ins
                st("T", r("dcol", "idf"), r("po0", "po1"), mm_t)
                st("V", r("po0", "po1"), r("grow"), lambda e: e.tensor_copy(
                    grow.ap(), PO[0:1, 0:1024]))

                def mm_b(e):
                    ins = []
                    for hh in range(2):
                        ins.append(e.matmul(
                            PS[:, 512 * hh:512 * (hh + 1)], onesf.ap(),
                            grow[0:1, 512 * hh:512 * (hh + 1)],
                            start=True, stop=True))
                    return ins
                st("T", r("grow", "onesf"), r("ps0", "ps1"), mm_b)
                st("V", r("ps0", "ps1"), r(dst_name), lambda e: e.tensor_copy(
                    dst_ap_fn(), PS[:, 0:1024]))
            bcast(16, "scrF", lambda: scrF[:, 0:1024])
            bcast(24, "gbbc", lambda: gbbc.ap())
            def mm_s1(e):
                ins = []
                for u in range(8):
                    for g in range(4):
                        ins.append(e.matmul(
                            s1psum[:, g:g + 1],
                            winT[:, 512 * u + 128 * g:512 * u + 128 * (g + 1)],
                            sab[:, u:u + 1],
                            start=(u == 0 and g == 0), stop=(u == 7 and g == 3)))
                return ins
            s1psum = PO
            st("T", r("winT", "sab"), r("po0"), mm_s1)
            st("V", r("po0"), r("s1col"), lambda e: e.tensor_copy(
                s1col.ap(), PO[:, 0:4]))

            def sc_win(e):
                ins = []
                for u in range(8):
                    ins.append(e.tensor_scalar(
                        winT[:, 512 * u:512 * (u + 1)],
                        winT[:, 512 * u:512 * (u + 1)],
                        dcol[:, u:u + 1], None, OP.mult))
                return ins
            st("V", r("dcol"), r("winT"), sc_win)

            st("G", [], r("arenaB"), lambda e: e.dma_start(
                arenaB[:, 0:4096].rearrange("p (c f) -> p c f", c=8),
                wout_h.rearrange("(c p) f -> c p f", p=128).transpose([1, 0, 2])),
               dma=True)
            for g in range(4):
                def mm_t(e, g=g):
                    ins = []
                    for c8 in range(8):
                        ins.append(e.matmul(
                            PT[:, 128 * c8:128 * (c8 + 1)],
                            arenaB[:, 512 * c8 + 128 * g:512 * c8 + 128 * (g + 1)],
                            idb.ap(), start=(c8 == 0), stop=(c8 == 7),
                            is_transpose=True))
                    return ins
                st("T", r("arenaB", "idb"), r("pt"), mm_t)
                st("V", r("pt"), r("woT"), lambda e, g=g: e.tensor_copy(
                    woT[:, 1024 * g:1024 * (g + 1)], PT.ap()))

            def sc_wo(e):
                ins = []
                for g in range(4):
                    ins.append(e.tensor_tensor(
                        woT[:, 1024 * g:1024 * (g + 1)],
                        woT[:, 1024 * g:1024 * (g + 1)],
                        scrF[:, 0:1024], OP.mult))
                return ins
            st("V", r("scrF"), r("woT"), sc_wo)
            st("Y", [], r("scrF", "scrFa", "scrFb"), lambda e: e.nop())


        def emit_p3head(lo=0, hi=16):
            pairA = (arenaF[:, 0:1024], ("arenaF",),
                     arenaF[:, 1024:2048], ("arenaF2",))
            pairC = (scr2[:, 0:1024], ("scr2a",),
                     scr2[:, 1024:2048], ("scr2b",))
            pairD = (scrF[:, 0:1024], ("scrFa",),
                     scrF[:, 1024:2048], ("scrFb",))
            for (mi, (mat_h, noise_h, dst, dstR, shift)) in enumerate((
                    (th_h, thn_h, cwt, "cwt", PI / 2.0),
                    (ph_h, phn_h, swt, "swt", 0.0))):
                for q in range(8):
                    qq = 8 * mi + q
                    if not (lo <= qq < hi):
                        continue
                    if qq < 6:
                        slot, aR, nslot, nR = pairA if qq % 2 == 0 else pairC
                    else:
                        slot, aR, nslot, nR = pairC if qq % 2 == 0 else pairD
                    st("Y", [], r(*aR), lambda e, mat_h=mat_h, q=q, slot=slot:
                       e.dma_start(slot, mat_h[128 * q:128 * (q + 1), :]),
                       dma=True)
                    st("Y", [], r(*nR), lambda e, noise_h=noise_h, q=q,
                       nslot=nslot:
                       e.dma_start(nslot, noise_h[128 * q:128 * (q + 1), :]),
                       dma=True)
                    st("V", r(*aR, *nR), r(*aR),
                       lambda e, shift=shift, slot=slot, nslot=nslot:
                       e.scalar_tensor_tensor(slot, nslot, shift, slot,
                                              OP.add, OP.add))
                    st("V", r(*aR), r(*aR), lambda e, slot=slot:
                       e.tensor_scalar(slot, slot, -PI, PI,
                                       OP.max, OP.min))
                    st("S", r(*aR), r(dstR), lambda e, dst=dst, q=q, slot=slot:
                       e.activation(dst[:, 1024 * q:1024 * (q + 1)],
                                    slot, AF.Sin))
                im = 0 if dst is cwt else 1

                def p3_tail_fn(dst=dst, dstR=dstR, im=im):
                    def mm_s1c(e, dst=dst):
                        ins = []
                        for u in range(8):
                            for hh in range(2):
                                ins.append(e.matmul(
                                    PO[0:1, 512 * hh:512 * (hh + 1)],
                                    sbb[:, u:u + 1],
                                    dst[:, 1024 * u + 512 * hh:
                                        1024 * u + 512 * (hh + 1)],
                                    start=(u == 0), stop=(u == 7)))
                        return ins
                    st("T", r(dstR, "sbb"), r("po0", "po1"), mm_s1c)
                    st("V", r("po0", "po1"), r("s1crowb"),
                       lambda e, im=im: e.tensor_copy(
                           s1crowb[0:1, 1024 * im:1024 * (im + 1)],
                           PO[0:1, 0:1024]))

                    def sc_cb(e, dst=dst):
                        ins = []
                        for u in range(8):
                            ins.append(e.tensor_scalar(
                                dst[:, 1024 * u:1024 * (u + 1)],
                                dst[:, 1024 * u:1024 * (u + 1)],
                                dcol[:, 8 + u:9 + u], None, OP.mult))
                        return ins
                    st("V", r("dcol"), r(dstR), sc_cb)
                if hi == 16:
                    p3_tail.append(p3_tail_fn)

        xslot = ["xt0", "xt1", "xt2"]
        XPOOL = ["arenaF", "arenaF2", "poolB", "iotac", "wtmp", "wtmp2"]

        def xap(i, lo, hi):
            sl = i % 3
            if sl < 2:
                base = 4096 * sl
                return xt[:, base + lo:base + hi]
            if hi <= 2048:
                return arenaF[:, lo:hi]
            assert lo >= 2048
            return poolB[:, lo - 2048:hi - 2048]

        def xres(i):
            sl = i % 3
            if sl < 2:
                return [R[xslot[sl]]]
            return [R["xt2"]] + [R[n] for n in XPOOL]

        def hfof(i):
            return (hf, "hf") if i % 2 == 0 else (hf2, "hf2")

        def h1of(i):
            return (h1tm, "h1tm") if i % 2 == 0 else (h1tm2, "h1tm2")

        def ptof(k):
            return (PT, "pt")

        ptctr = [0]

        def stage_load(i):
            sl = i % 3
            if sl < 2:
                st("Y", [], xres(i), lambda e, i=i, sl=sl: e.dma_start(
                    xt[:, 4096 * sl:4096 * sl + 4096]
                    .rearrange("p (s d) -> p s d", s=4),
                    x_h[512 * i:512 * (i + 1), :]
                    .rearrange("(s p) d -> s p d", p=128).transpose([1, 0, 2])),
                   dma=True)
            else:
                st("Y", [], xres(i), lambda e, i=i: e.dma_start(
                    arenaF.ap().rearrange("p (s d) -> p s d", s=2),
                    x_h[512 * i:512 * i + 256, :]
                    .rearrange("(s p) d -> s p d", p=128).transpose([1, 0, 2])),
                   dma=True)
                st("Y", [], xres(i), lambda e, i=i: e.dma_start(
                    poolB.ap().rearrange("p (s d) -> p s d", s=2),
                    x_h[512 * i + 256:512 * (i + 1), :]
                    .rearrange("(s p) d -> s p d", p=128).transpose([1, 0, 2])),
                   dma=True)

        def stage_ln(i, which, part="all"):
            htm, htmR = h1of(i)
            hfb, hfR = hfof(i)
            c0 = 4 * i
            if part == "apply":
                emit_ln_apply(i, htm, htmR, hfb, hfR, c0)
                return

            def stats_g(e, i=i, c0=c0, htm=htm):
                return [e.tensor_scalar(
                    htm[:, 1024 * s_:1024 * (s_ + 1)],
                    xap(i, 1024 * s_, 1024 * (s_ + 1)),
                    1.0, None, OP.mult, OP.add,
                    accum_out=sums[:, c0 + s_:c0 + s_ + 1])
                    for s_ in range(NSUB)]
            st("V", xres(i), r("sums", htmR), stats_g)

            def stats_s(e, i=i, c0=c0):
                return [e.activation(
                    arenaB[:, 1024 * s_:1024 * (s_ + 1)],
                    xap(i, 1024 * s_, 1024 * (s_ + 1)),
                    AF.Square, accum_out=sumsq[:, c0 + s_:c0 + s_ + 1])
                    for s_ in range(NSUB)]
            st("S", xres(i), r("sumsq", "arenaB"), stats_s)

            c1 = c0 + 4
            st("V", r("sumsq"), r("rstat"), lambda e, c0=c0, c1=c1:
               e.tensor_scalar(rstat[:, c0:c1], sumsq[:, c0:c1],
                               1.0 / D, EPS, OP.mult, OP.add))
            st("V", r("sums"), r("nmr"), lambda e, c0=c0, c1=c1:
               e.tensor_scalar(nmr[:, c0:c1], sums[:, c0:c1],
                               -1.0 / D, None, OP.mult))
            st("V", r("nmr"), r("rstat"), lambda e, c0=c0, c1=c1:
               e.tensor_tensor(rstat[:, 16 + c0:16 + c1], nmr[:, c0:c1],
                               nmr[:, c0:c1], OP.mult))
            st("V", r("rstat"), r("rstat"), lambda e, c0=c0, c1=c1:
               e.tensor_tensor(rstat[:, c0:c1], rstat[:, c0:c1],
                               rstat[:, 16 + c0:16 + c1], OP.subtract))
            st("S", r("rstat"), r("rstat"), lambda e, c0=c0: e.activation(
                rstat[:, 32 + c0:32 + c0 + 4], rstat[:, c0:c0 + 4], AF.Sqrt))
            st("V", r("rstat"), r("rsig"), lambda e, c0=c0: e.reciprocal(
                rsig[:, c0:c0 + 4], rstat[:, 32 + c0:32 + c0 + 4]))
            st("V", r("rsig", "nmr"), r("nmr"), lambda e, c0=c0:
               e.tensor_tensor(nmr[:, c0:c0 + 4], nmr[:, c0:c0 + 4],
                               rsig[:, c0:c0 + 4], OP.mult))
            if part == "stats":
                return
            emit_ln_apply(i, htm, htmR, hfb, hfR, c0)

        def emit_ln_apply(i, htm, htmR, hfb, hfR, c0):
            def h_apply(e, i=i, c0=c0, htm=htm):
                return [e.activation(
                    htm[:, 1024 * s_:1024 * (s_ + 1)],
                    xap(i, 1024 * s_, 1024 * (s_ + 1)),
                    AF.Identity, bias=nmr[:, c0 + s_:c0 + s_ + 1],
                    scale=rsig[:, c0 + s_:c0 + s_ + 1]) for s_ in range(NSUB)]
            st("S", xres(i) + r("rsig", "nmr"), r(htmR), h_apply)

            for s_ in range(NSUB):
                def mm_ht(e, s_=s_, htm=htm):
                    ins = []
                    for cch in range(8):
                        ins.append(e.matmul(
                            PT[:, 128 * cch:128 * (cch + 1)],
                            htm[:, 1024 * s_ + 128 * cch:
                                1024 * s_ + 128 * (cch + 1)],
                            idb.ap(), start=(cch == 0), stop=(cch == 7),
                            is_transpose=True))
                    return ins
                st("T", r(htmR, "idb"), r("pt"), mm_ht)
                st("S", r("pt"), r(hfR), lambda e, s_=s_, hfb=hfb:
                   e.copy(
                       hfb.ap().rearrange("p (u t) -> p u t", u=8)
                       [:, :, 128 * s_:128 * (s_ + 1)],
                       PT.ap().rearrange("p (u t) -> p u t", u=8)))

        def stage_spectral(i, g):
            hfb, hfR = hfof(i)

            def mm_sp(e, g=g, hfb=hfb):
                ins = []
                for (bk, fch) in ((2 * g, g), (2 * g + 1, 2 + g)):
                    for u in range(8):
                        ins.append(e.matmul(
                            PS[:, 512 * bk:512 * (bk + 1)],
                            winT[:, 512 * u + 128 * fch:
                                 512 * u + 128 * (fch + 1)],
                            hfb[:, 512 * u:512 * (u + 1)],
                            start=(u == 0), stop=(u == 7)))
                return ins
            st("T", r("winT", hfR), r("ps%d" % (2 * g), "ps%d" % (2 * g + 1)),
               mm_sp)

        def rot_to_tbl2(g):
            gb_ = 512 * g
            st("V", r("ctbl", "rotc"), r("ctbl2"), lambda e:
               e.tensor_scalar(ctbl2[:, gb_:gb_ + 512], ctbl[:, gb_:gb_ + 512],
                               rotc[:, 0 + g:1 + g], None, OP.mult))
            st("V", r("stbl", "rotc", "ctbl2"), r("ctbl2"),
               lambda e: e.scalar_tensor_tensor(
                   ctbl2[:, gb_:gb_ + 512], stbl[:, gb_:gb_ + 512],
                   rotc[:, 4 + g:5 + g], ctbl2[:, gb_:gb_ + 512],
                   OP.mult, OP.add))
            st("V", r("ctbl", "rotc"), r("stbl2"), lambda e:
               e.tensor_scalar(stbl2[:, gb_:gb_ + 512], ctbl[:, gb_:gb_ + 512],
                               rotc[:, 2 + g:3 + g], None, OP.mult))
            st("V", r("stbl", "rotc", "stbl2"), r("stbl2"),
               lambda e: e.scalar_tensor_tensor(
                   stbl2[:, gb_:gb_ + 512], stbl[:, gb_:gb_ + 512],
                   rotc[:, 0 + g:1 + g], stbl2[:, gb_:gb_ + 512],
                   OP.mult, OP.add))

        def rot_carry(g):
            st("V", r("carry", "rotc"), r("carry2"), lambda e, g=g:
               e.tensor_scalar(carry2[:, g:g + 1], carry[:, g:g + 1],
                               rotc[:, 6 + g:7 + g], None, OP.mult))
            st("V", r("carry", "rotc"), r("carry2"), lambda e, g=g:
               e.tensor_scalar(carry2[:, 2 + g:3 + g], carry[:, g:g + 1],
                               rotc[:, 8 + g:9 + g], None, OP.mult))
            st("V", r("carry", "rotc", "carry2"), r("carry"), lambda e, g=g:
               e.scalar_tensor_tensor(
                   carry[:, g:g + 1], carry[:, 2 + g:3 + g],
                   rotc[:, 10 + g:11 + g], carry2[:, g:g + 1],
                   OP.mult, OP.add))
            st("V", r("carry", "rotc", "carry2"), r("carry"), lambda e, g=g:
               e.scalar_tensor_tensor(
                   carry[:, 2 + g:3 + g], carry[:, 2 + g:3 + g],
                   rotc[:, 6 + g:7 + g], carry2[:, 2 + g:3 + g],
                   OP.mult, OP.add))

        def emit_tblinit(g):
            gb_ = 512 * g
            if True:
                st("V", r("iotac", "omg_col"), r("scrF"),
                   lambda e, g=g, gb_=gb_: e.tensor_scalar(
                       scrF[:, 1536:2048], iotac,
                       omg_col[:, g:g + 1], None, OP.mult))
                st("V", r("scrF"), r("wtmp2"), lambda e, gb_=gb_:
                   e.tensor_scalar(wtmp2, scrF[:, 1536:2048],
                                   1.0 / TWO_PI, None, OP.mult))
                st("V", r("wtmp2"), r("kbuf"), lambda e: e.tensor_copy(
                    kbuf[:, 0:512], wtmp2))
                st("V", r("kbuf", "scrF"), r("wtmp"), lambda e, gb_=gb_:
                   e.scalar_tensor_tensor(
                       wtmp, kbuf[:, 0:512], -TWO_PI,
                       scrF[:, 1536:2048], OP.mult, OP.add))
                st("V", r("wtmp"), r("wtmp2"), lambda e: e.tensor_scalar(
                    wtmp2, wtmp, PI, -TWO_PI, OP.is_gt, OP.mult))
                st("V", r("wtmp", "wtmp2"), r("wtmp"), lambda e:
                   e.tensor_tensor(wtmp, wtmp, wtmp2, OP.add))
                st("V", r("wtmp"), r("wtmp2"), lambda e: e.tensor_scalar(
                    wtmp2, wtmp, -PI, TWO_PI, OP.is_lt, OP.mult))
                st("V", r("wtmp", "wtmp2"), r("scrF"), lambda e, gb_=gb_:
                   e.tensor_tensor(scrF[:, 1536:2048], wtmp,
                                   wtmp2, OP.add))
                st("V", r("scrF"), r("scrF"), lambda e, gb_=gb_:
                   e.tensor_scalar(scrF[:, 1536:2048],
                                   scrF[:, 1536:2048],
                                   -PI, PI, OP.max, OP.min))
                st("S", r("scrF"), r("stbl"), lambda e, gb_=gb_:
                   e.activation(stbl[:, gb_:gb_ + 512],
                                scrF[:, 1536:2048], AF.Sin))
                st("V", r("scrF"), r("wtmp2"), lambda e, gb_=gb_:
                   e.tensor_scalar(wtmp2, scrF[:, 1536:2048],
                                   PI / 2.0, None, OP.add))
                st("V", r("wtmp2"), r("wtmp"), lambda e: e.tensor_scalar(
                    wtmp, wtmp2, PI, -TWO_PI, OP.is_gt, OP.mult))
                st("V", r("wtmp2", "wtmp"), r("wtmp"), lambda e:
                   e.tensor_tensor(wtmp, wtmp2, wtmp, OP.add))
                st("V", r("wtmp"), r("wtmp"), lambda e: e.tensor_scalar(
                    wtmp, wtmp, -PI, PI, OP.max, OP.min))
                st("S", r("wtmp"), r("ctbl"), lambda e, gb_=gb_:
                   e.activation(ctbl[:, gb_:gb_ + 512], wtmp, AF.Sin))
                rot_to_tbl2(g)

        def stage_phase(i, g):
            gb_ = 512 * g
            pr, pi_ = "ps%d" % (2 * g), "ps%d" % (2 * g + 1)
            PRa = PS[:, 512 * 2 * g:512 * (2 * g + 1)]
            PIa = PS[:, 512 * (2 * g + 1):512 * (2 * g + 2)]

            def ev_p(e, PRa=PRa, PIa=PIa, g=g):
                a = e.activation(pbuf[:, 0:512], PRa, AF.Identity,
                                 bias=s1col[:, g:g + 1])
                b = e.activation(pbuf[:, 512:1024], PIa, AF.Identity,
                                 bias=s1col[:, 2 + g:3 + g])
                return [a, b]
            st("S", r(pr, pi_, "s1col"), r("pbuf"), ev_p)
            st("V", r("pbuf", "ctbl"), r("vb16"), lambda e, gb_=gb_:
               e.tensor_tensor(vb16[:, 0:512], ctbl[:, gb_:gb_ + 512],
                               pbuf[:, 0:512], OP.mult))
            st("V", r("pbuf", "stbl"), r("vb16"), lambda e, gb_=gb_:
               e.tensor_tensor(vb16[:, 512:1024], stbl[:, gb_:gb_ + 512],
                               pbuf[:, 512:1024], OP.mult))
            st("V", r("vb16"), r("vbuf"), lambda e: e.tensor_tensor(
                vbuf[:, 0:512], vb16[:, 0:512], vb16[:, 512:1024], OP.add))
            st("V", r("pbuf", "ctbl"), r("vb16"), lambda e, gb_=gb_:
               e.tensor_tensor(vb16[:, 0:512], ctbl[:, gb_:gb_ + 512],
                               pbuf[:, 512:1024], OP.mult))
            st("V", r("pbuf", "stbl"), r("vb16"), lambda e, gb_=gb_:
               e.tensor_tensor(vb16[:, 512:1024], stbl[:, gb_:gb_ + 512],
                               pbuf[:, 0:512], OP.mult))
            st("V", r("vb16"), r("vbuf"), lambda e: e.tensor_tensor(
                vbuf[:, 512:1024], vb16[:, 0:512], vb16[:, 512:1024],
                OP.subtract))

            init_r = 0.0 if i == 0 else carry[:, g:g + 1]
            init_i = 0.0 if i == 0 else carry[:, 2 + g:3 + g]
            st("V", r("vbuf", "rho_col", "carry"), r("zbuf"),
               lambda e, g=g, init_r=init_r: e.tensor_tensor_scan(
                   zbuf[:, 0:512], rho_col[:, g:g + 1].broadcast_to([128, 512]),
                   vbuf[:, 0:512], init_r, OP.mult, OP.add))
            st("V", r("vbuf", "rho_col", "carry"), r("zbuf"),
               lambda e, g=g, init_i=init_i: e.tensor_tensor_scan(
                   zbuf[:, 512:1024],
                   rho_col[:, g:g + 1].broadcast_to([128, 512]),
                   vbuf[:, 512:1024], init_i, OP.mult, OP.add))

            def carry_copy(e, g=g):
                a = e.tensor_copy(carry[:, g:g + 1], zbuf[:, 511:512])
                b = e.tensor_copy(carry[:, 2 + g:3 + g], zbuf[:, 1023:1024])
                return [a, b]
            st("V", r("zbuf"), r("carry"), carry_copy)
            if i + 1 < NSUP:
                rot_carry(g)

            fyb, fyR = (fy, "fy") if i % 2 == 0 else (fy2, "fy2")
            st("V", r("ctbl2", "zbuf"), r("vb16"), lambda e, gb_=gb_:
               e.tensor_tensor(vb16[:, 0:512], ctbl2[:, gb_:gb_ + 512],
                               zbuf[:, 0:512], OP.mult))
            st("V", r("stbl2", "zbuf"), r("vb16"), lambda e, gb_=gb_:
               e.tensor_tensor(vb16[:, 512:1024], stbl2[:, gb_:gb_ + 512],
                               zbuf[:, 512:1024], OP.mult))
            st("V", r("vb16"), r(fyR), lambda e, g=g, fyb=fyb: e.tensor_tensor(
                fyb[:, 512 * g:512 * (g + 1)], vb16[:, 0:512],
                vb16[:, 512:1024], OP.subtract))
            st("V", r("stbl2", "zbuf"), r("vb16"), lambda e, gb_=gb_:
               e.tensor_tensor(vb16[:, 0:512], stbl2[:, gb_:gb_ + 512],
                               zbuf[:, 0:512], OP.mult))
            st("V", r("ctbl2", "zbuf"), r("vb16"), lambda e, gb_=gb_:
               e.tensor_tensor(vb16[:, 512:1024], ctbl2[:, gb_:gb_ + 512],
                               zbuf[:, 512:1024], OP.mult))
            st("V", r("vb16"), r(fyR), lambda e, g=g, fyb=fyb: e.tensor_tensor(
                fyb[:, 512 * (2 + g):512 * (3 + g)], vb16[:, 0:512],
                vb16[:, 512:1024], OP.add))

        def stage_outproj(i):
            fyb, fyR = (fy, "fy") if i % 2 == 0 else (fy2, "fy2")
            banks = [(PO[:, 0:512], "po0"), (PO[:, 512:1024], "po1"),
                     (PF.ap(), "pf")]
            for k, (s_, hh) in enumerate([(s_, hh) for s_ in range(NSUB)
                                          for hh in range(2)]):
                bk, bkR = banks[k % 3]

                def mm_op(e, s_=s_, hh=hh, fyb=fyb, bk=bk):
                    ins = []
                    for g in range(4):
                        ins.append(e.matmul(
                            bk,
                            fyb[:, 512 * g + 128 * s_:
                                512 * g + 128 * (s_ + 1)],
                            woT[:, 1024 * g + 512 * hh:
                                1024 * g + 512 * (hh + 1)],
                            start=(g == 0), stop=(g == 3)))
                    return ins
                st("T", r(fyR, "woT"), r(bkR), mm_op)
                st("V", r(bkR) + xres(i), xres(i),
                   lambda e, i=i, s_=s_, hh=hh, bk=bk: e.tensor_tensor(
                       xap(i, 1024 * s_ + 512 * hh,
                           1024 * s_ + 512 * (hh + 1)),
                       xap(i, 1024 * s_ + 512 * hh,
                           1024 * s_ + 512 * (hh + 1)),
                       bk, OP.add))

        def stage_ffn(i):
            hfb, hfR = hfof(i)
            banks = [(PO[:, 0:512], "po0"), (PO[:, 512:1024], "po1"),
                     (PF.ap(), "pf")]
            for k, (s_, hh) in enumerate([(s_, hh) for s_ in range(NSUB)
                                          for hh in range(2)]):
                bA, bAR = banks[(2 * k) % 3]
                bB, bBR = banks[(2 * k + 1) % 3]
                if k % 2 == 0:
                    e1, e2, e3, e4 = (scrF[:, 0:512], scrF[:, 512:1024],
                                      scrF[:, 1024:1536], scrF[:, 1536:2048])
                    eR = "scrF"
                else:
                    e1, e2, e3, e4 = (scr2[:, 0:512], scr2[:, 512:1024],
                                      scr2[:, 1024:1536], scr2[:, 1536:2048])
                    eR = "scr2"
                    eR2 = "scr2"

                def mm_f(e, s_=s_, hh=hh, hfb=hfb, bA=bA, bB=bB):
                    ins = []
                    for u in range(8):
                        for (bk, wmat) in ((bA, cwt), (bB, swt)):
                            ins.append(e.matmul(
                                bk,
                                hfb[:, 512 * u + 128 * s_:
                                    512 * u + 128 * (s_ + 1)],
                                wmat[:, 1024 * u + 512 * hh:
                                     1024 * u + 512 * (hh + 1)],
                                start=(u == 0), stop=False))
                    for (bk, im) in ((bA, 0), (bB, 1)):
                        ins.append(e.matmul(
                            bk, onesb.ap(),
                            s1crowb[0:1, 1024 * im + 512 * hh:
                                    1024 * im + 512 * (hh + 1)],
                            start=False, stop=True))
                    return ins
                st("T", r(hfR, "cwt", "swt", "onesb", "s1crowb"),
                   r(bAR, bBR), mm_f)

                if k % 2 == 0:
                    def sq_f(e, bA=bA, bB=bB, e1=e1, e2=e2):
                        a = e.activation(e1, bA, AF.Square)
                        b = e.activation(e2, bB, AF.Square)
                        return [a, b]
                    st("S", r(bAR, bBR), r(eR, "scrFa", "scrFb"), sq_f)
                    st("G", r(eR), r(eR), lambda e, e1=e1, e2=e2, e3=e3:
                       e.tensor_tensor(e3, e1, e2, OP.add))
                    st("S", r(eR), r(eR), lambda e, e3=e3, e4=e4:
                       e.activation(e4, e3, AF.Sqrt))
                    st("V", r(eR, "gbbc"), r(eR),
                       lambda e, e3=e3, e4=e4, hh=hh: e.tensor_tensor(
                           e3, e4, gbbc[:, 512 * hh:512 * (hh + 1)],
                           OP.mult))
                    st("V", r(eR) + xres(i), xres(i),
                       lambda e, i=i, s_=s_, hh=hh, e3=e3: e.tensor_tensor(
                           xap(i, 1024 * s_ + 512 * hh,
                               1024 * s_ + 512 * (hh + 1)),
                           xap(i, 1024 * s_ + 512 * hh,
                               1024 * s_ + 512 * (hh + 1)),
                           e3, OP.add))
                else:
                    def sq_f(e, bA=bA, bB=bB, e1=e1, e2=e2):
                        a = e.activation(e1, bA, AF.Square)
                        b = e.activation(e2, bB, AF.Square)
                        return [a, b]
                    st("S", r(bAR, bBR), r(eR, "scr2a", "scr2b"), sq_f)
                    st("G", r(eR, eR2), r(eR2), lambda e, e1=e1, e2=e2, e3=e3:
                       e.tensor_tensor(e3, e1, e2, OP.add))
                    st("S", r(eR2), r(eR2), lambda e, e3=e3, e4=e4:
                       e.activation(e4, e3, AF.Sqrt))
                    st("V", r(eR2, "gbbc"), r(eR2),
                       lambda e, e3=e3, e4=e4, hh=hh: e.tensor_tensor(
                           e3, e4, gbbc[:, 512 * hh:512 * (hh + 1)],
                           OP.mult))
                    st("V", r(eR2) + xres(i), xres(i),
                       lambda e, i=i, s_=s_, hh=hh, e3=e3: e.tensor_tensor(
                           xap(i, 1024 * s_ + 512 * hh,
                               1024 * s_ + 512 * (hh + 1)),
                           xap(i, 1024 * s_ + 512 * hh,
                               1024 * s_ + 512 * (hh + 1)),
                           e3, OP.add))

        def stage_store(i):
            sl = i % 3
            if sl < 2:
                st("Y", xres(i), xres(i),
                   lambda e, i=i, sl=sl: e.dma_start(
                       out_h[512 * i:512 * (i + 1), :]
                       .rearrange("(s p) d -> s p d", p=128)
                       .transpose([1, 0, 2]),
                       xt[:, 4096 * sl:4096 * sl + 4096]
                       .rearrange("p (s d) -> p s d", s=4)),
                   dma=True)
            else:
                st("Y", xres(i), xres(i),
                   lambda e, i=i: e.dma_start(
                       out_h[512 * i:512 * i + 256, :]
                       .rearrange("(s p) d -> s p d", p=128)
                       .transpose([1, 0, 2]),
                       arenaF.ap().rearrange("p (s d) -> p s d", s=2)),
                   dma=True)
                st("Y", xres(i), xres(i),
                   lambda e, i=i: e.dma_start(
                       out_h[512 * i + 256:512 * (i + 1), :]
                       .rearrange("(s p) d -> s p d", p=128)
                       .transpose([1, 0, 2]),
                       poolB.ap().rearrange("p (s d) -> p s d", s=2)),
                   dma=True)

        stage_load(0)
        stage_load(1)
        emit_modhead()
        stage_ln(0, 1)
        emit_tblinit(0)
        emit_tblinit(1)
        stage_ln(1, 1)
        emit_p3head(0, 6)
        stage_load(2)
        emit_p123()
        stage_spectral(0, 0)
        stage_spectral(0, 1)
        stage_phase(0, 0)
        stage_phase(0, 1)
        stage_ln(2, 1, "stats")
        emit_p3head(6, 10)
        for i in range(NSUP):
            stage_outproj(i)
            stage_ln(i, 2, "stats")
            if i + 1 < NSUP:
                stage_spectral(i + 1, 0)
                stage_spectral(i + 1, 1)
            if i == 0:
                emit_p3head(10, 13)
            stage_ln(i, 2, "apply")
            if i + 1 < NSUP:
                stage_phase(i + 1, 0)
                stage_phase(i + 1, 1)
            if i == 0:
                emit_p3head(13, 16)
                for fn in p3_tail:
                    fn()
            stage_ffn(i)
            if i + 2 < NSUP:
                stage_ln(i + 2, 1, "apply")
            stage_store(i)
            if i + 3 < NSUP:
                stage_load(i + 3)
                stage_ln(i + 3, 1, "stats")

        st("Y", r(xslot[0], xslot[1], *XPOOL), [], lambda e: e.nop())

        semctx = contextlib.ExitStack()
        with semctx:
            sems = {}
            for n in sch.sem_names():
                sems[n] = semctx.enter_context(nc.semaphore("sem" + n))
            with nc.Block() as block:
                sch.emit(block, sems)

    return nc


_NC_CACHE = None


def kernel(**inputs):
    global _NC_CACHE
    x = np.ascontiguousarray(inputs["L0_attn_x"], dtype=np.float32)
    w_in = np.ascontiguousarray(inputs["L0_attn_W_in"], dtype=np.float32)
    w_out = np.ascontiguousarray(inputs["L0_attn_W_out"], dtype=np.float32)
    ld = np.ascontiguousarray(inputs["L0_attn_log_decay"], dtype=np.float32)
    fr = np.ascontiguousarray(inputs["L0_attn_frequencies"], dtype=np.float32)
    th = np.ascontiguousarray(inputs["L0_ffn_theta"], dtype=np.float32)
    ph = np.ascontiguousarray(inputs["L0_ffn_phi"], dtype=np.float32)
    thn = np.ascontiguousarray(inputs["L0_ffn_theta_noise"], dtype=np.float32)
    phn = np.ascontiguousarray(inputs["L0_ffn_phi_noise"], dtype=np.float32)
    c = np.ascontiguousarray(inputs["L0_c"], dtype=np.float32)
    wmod = np.ascontiguousarray(inputs["L0_wmod"], dtype=np.float32)

    if _NC_CACHE is None:
        _NC_CACHE = build_nc()
    nc = _NC_CACHE

    in_maps = []
    for b in range(8):
        in_maps.append({
            "x": x[b], "w_in": w_in, "w_out": w_out, "ld": ld, "fr": fr,
            "theta": th, "phi": ph, "thn": thn, "phn": phn,
            "c": c,
            "bsel": np.eye(8, dtype=np.float32)[b],
            "wmod": np.ascontiguousarray(wmod[:, 768 * b:768 * (b + 1)]),
        })
    res = run_bass_kernel_spmd(nc, in_maps, core_ids=list(range(8)))
    out = np.stack([r["out"] for r in res.results], axis=0)
    return out


if __name__ == "__main__":
    nc = build_nc()
    print("instructions:", nc.n_instructions()
          if hasattr(nc, "n_instructions") else "built")
